# version 02
# baseline (speedup 1.0000x reference)
import math
import os
import contextlib
import numpy as np
import concourse.bass as bass
import concourse.mybir as mybir
from concourse.bass_utils import run_bass_kernel_spmd

F32 = mybir.dt.float32
BF16 = mybir.dt.bfloat16
I32 = mybir.dt.int32
AF = mybir.ActivationFunctionType
ALU = mybir.AluOpType
AX = mybir.AxisListType

B, T, D = 2, 4096, 4096
DFF = 11008
DH = DFF // 2
NDH = DH // 128
NID = D // 128
R = 16
HD = 128
NH = 32
NKV = 8
KVW = NKV * HD
QBLK = 1024
EPS = 1e-5
ISQ = 1.0 / math.sqrt(HD)
CHK = 512
NTT = 4

NF4 = np.array([
    -1.0, -0.6961928009986877, -0.5250730514526367, -0.39491748809814453,
    -0.28444138169288635, -0.18477343022823334, -0.09105003625154495, 0.0,
    0.07958029955625534, 0.16093020141124725, 0.2461123019456863,
    0.33791524171829224, 0.44070982933044434, 0.5626170039176941,
    0.7229568362236023, 1.0], dtype=np.float64)


def _nf4_coeffs():
    E, O, Sq = [], [], []
    for k in range(8):
        hi, lo = NF4[8 + k], NF4[7 - k]
        u = (k + 0.5) / 7.5
        E.append((hi + lo) / 2.0)
        O.append((hi - lo) / (2.0 * u))
        Sq.append(u * u)
    cE = np.polynomial.polynomial.polyfit(np.array(Sq), np.array(E), 7)
    cO = np.polynomial.polynomial.polyfit(np.array(Sq), np.array(O), 7)
    return [float(x) for x in cE], [float(x) for x in cO]

CE, CO = _nf4_coeffs()


def _llama3_cos_sin():
    base = 500000.0
    inv_freq = 1.0 / base ** (np.arange(0, HD, 2, dtype=np.float32) / HD)
    factor, low_f, high_f, old_ctx = 32.0, 1.0, 4.0, 8192
    low_wl, high_wl = old_ctx / low_f, old_ctx / high_f
    wavelen = 2 * math.pi / inv_freq
    inv_llama = np.where(wavelen > low_wl, inv_freq / factor, inv_freq)
    smooth = (old_ctx / wavelen - low_f) / (high_f - low_f)
    smoothed = (1 - smooth) * inv_llama / factor + smooth * inv_llama
    is_med = ~(wavelen < high_wl) & ~(wavelen > low_wl)
    inv = np.where(is_med, smoothed, inv_llama).astype(np.float32)
    pos = np.arange(T, dtype=np.float32)
    freqs = pos[:, None] * inv[None, :]
    emb = np.concatenate([freqs, freqs], axis=-1)
    return np.cos(emb).astype(np.float32), np.sin(emb).astype(np.float32)


class Led:
    __slots__ = ("h", "n", "base", "fence_born")

    def __init__(self, h):
        self.h = h
        self.n = 0
        self.base = 0
        self.fence_born = -10

    def inc(self, k):
        self.n += k
        return k

    def t(self, c):
        return self.base + c


class KB:
    POOL = 92

    def __init__(self):
        self.nc = bass.Bass("TRN2", target_bir_lowering=False, debug=False)
        self.es = contextlib.ExitStack()
        self.pool = []
        self.cursor = 0
        self.nfence = 0
        self.ndedic = 0

    def new_sem(self):
        if len(self.pool) < self.POOL:
            led = Led(self.es.enter_context(self.nc.semaphore(f"s{len(self.pool)}")))
            self.pool.append(led)
        else:
            led = self.pool[self.cursor % self.POOL]
            assert self.nfence >= led.fence_born + 2, \
                f"semaphore recycled too soon ({self.nfence} vs {led.fence_born})"
        self.cursor += 1
        led.base = led.n
        led.fence_born = self.nfence
        return led

    def new_dedicated(self):
        led = Led(self.es.enter_context(self.nc.semaphore(f"d{self.ndedic}")))
        self.ndedic += 1
        return led


def _fence(kb, block, V):
    f = kb.new_sem()
    kb.nfence += 1
    for eng in ("sync", "tensor", "scalar", "vector", "gpsimd"):
        @getattr(block, eng)
        def _(e):
            e.drain()
            e.sem_inc(f.h, 1)
            e.wait_ge(f.h, f.t(5))
    f.n = f.base + 5


def _getreg(eng, V, name):
    regs = V.setdefault("_regs", {})
    if name not in regs:
        regs[name] = eng.alloc_register(name)
    return regs[name]


def _dsep(vector, V, n=2):
    for _ in range(n):
        vector.reduce_sum(out=V["s_stat"][:, 7:8], in_=V["s_lnb"][:, 0:1024],
                          axis=AX.X)


def _barrier(kb, block, V, slot):
    bar = kb.new_dedicated()

    @block.gpsimd
    def _(g):
        member = _getreg(g, V, "g_member")
        g.load(member, V["s_meta"][0:1, 0:1])
        with g.If_eq(member, 0):
            g.dma_start(out=V["flags_sh"][0:1, slot:slot + 1],
                        in_=V["s_meta"][0:1, 1:2]).then_inc(bar.h, bar.inc(16))
        with g.Else():
            g.dma_start(out=V["flags_sh"][1:2, slot:slot + 1],
                        in_=V["s_meta"][0:1, 1:2]).then_inc(bar.h, 16)
        g.wait_ge(bar.h, bar.t(16))
        nonce = _getreg(g, V, "g_nonce")
        g.load(nonce, V["s_meta"][0:1, 1:2])
        cnt = _getreg(g, V, "g_cnt")
        g.reg_mov(cnt, bar.t(16))
        val = _getreg(g, V, "g_val")
        go = _getreg(g, V, "g_go")
        g.reg_mov(go, 1)
        with g.While(go):
            with g.If_eq(member, 0):
                g.dma_start(out=V["s_flag"][0:1, slot:slot + 1],
                            in_=V["flags_sh"][1:2, slot:slot + 1]).then_inc(bar.h, 16)
            with g.Else():
                g.dma_start(out=V["s_flag"][0:1, slot:slot + 1],
                            in_=V["flags_sh"][0:1, slot:slot + 1]).then_inc(bar.h, 16)
            g.reg_add(cnt, cnt, 16)
            g.wait_ge(bar.h, cnt)
            g.load(val, V["s_flag"][0:1, slot:slot + 1])
            g.reg_alu(go, val, nonce, ALU.not_equal)

    _fence(kb, block, V)


def build_program(stages=9, dbg=False):
    kb = KB()
    nc = kb.nc
    es = kb.es

    def din(name, shape, dt=F32):
        return nc.dram_tensor(name, shape, dt, kind="ExternalInput").ap()

    def dint(name, shape, dt, shared=False):
        if shared:
            return nc.dram_tensor(name, shape, dt, addr_space="Shared").ap()
        return nc.dram_tensor(name, shape, dt).ap()

    x_in = din("x_b", [T, D])
    codes_in = {"g": din("g_codesT_h", [D, DH], I32),
                "u": din("u_codesT_h", [D, DH], I32),
                "d": din("d_codesT_h", [DH, D], I32)}
    am_in = {"g": din("g_absmaxT_h", [64, DH]),
             "u": din("u_absmaxT_h", [64, DH]),
             "d": din("d_absmaxT_h", [86, D])}
    la_in = {"g": din("g_lora_a", [D, R]),
             "u": din("u_lora_a", [D, R]),
             "d": din("d_lora_a_h", [DH, R])}
    lb_in = {"g": din("g_lora_b_h", [R, DH]),
             "u": din("u_lora_b_h", [R, DH]),
             "d": din("d_lora_b", [R, D])}
    wq_in = din("wqT", [D, D])
    wk_in = din("wkT_h", [D, 512])
    wv_in = din("wvT_h", [D, 512])
    wo_in = din("woT", [D, D])
    cosk_in = din("cosk4", [T, 512])
    sink_in = din("sink4", [T, 512])
    cosq_in = din("cosq4", [QBLK, 512])
    sinq_in = din("sinq4", [QBLK, 512])
    mask_in = din("mask01", [QBLK, T], BF16)
    identb_in = din("identb", [128, 128], BF16)
    identf_in = din("identf", [128, 128], F32)
    meta_in = din("meta", [1, 8], I32)

    out_x3 = nc.dram_tensor("out_x3", [QBLK, D], F32, kind="ExternalOutput").ap()
    if dbg:
        dbg_w = nc.dram_tensor("dbg_w", [256, DH], BF16, kind="ExternalOutput").ap()
        dbg_x1 = nc.dram_tensor("dbg_x1", [T, D], F32, kind="ExternalOutput").ap()
        dbg_pa = [nc.dram_tensor(f"dbg_pa{m}", [T, D], F32, kind="ExternalOutput").ap()
                  for m in range(2)]
        dbg_ab1 = nc.dram_tensor("dbg_ab1", [128, NDH * CHK], BF16, kind="ExternalOutput").ap()
        dbg_ab2 = nc.dram_tensor("dbg_ab2", [128, NID * CHK], BF16, kind="ExternalOutput").ap()
        dbg_P = nc.dram_tensor("dbg_P", [128, 16384], BF16, kind="ExternalOutput").ap()
        dbg_mn = nc.dram_tensor("dbg_mn", [128, 48], F32, kind="ExternalOutput").ap()
        dbg_k = nc.dram_tensor("dbg_k", [KVW, T], BF16, kind="ExternalOutput").ap()
        dbg_v = nc.dram_tensor("dbg_v", [T, KVW], BF16, kind="ExternalOutput").ap()
        dbg_q = nc.dram_tensor("dbg_q", [D, QBLK], BF16, kind="ExternalOutput").ap()
        dbg_x2 = nc.dram_tensor("dbg_x2", [QBLK, D], F32, kind="ExternalOutput").ap()

    w_cache = {"g": dint("wg_cache", [D, DH], BF16),
               "u": dint("wu_cache", [D, DH], BF16),
               "d": dint("wd_cache", [DH, D], BF16)}
    lb_cache = dint("lb_cache", [16, 2 * DH + D], BF16)
    wkv_cache = dint("wkv_cache", [D, 1024], BF16)
    wq_cache = dint("wq_cache", [D, D], BF16)
    wo_cache = dint("wo_cache", [D, D], BF16)
    q_fm = dint("q_fm", [D, QBLK], BF16)
    x1own_dram = dint("x1own", [QBLK, D], F32)
    x2_dram = dint("x2_dram", [QBLK, D], F32)

    pa_sh = [dint(f"pa_sh{m}", [T, D], F32, shared=True) for m in range(2)]
    x1_sh = dint("x1_sh", [T, D], F32, shared=True)
    k_sh = dint("k_sh", [KVW, T], BF16, shared=True)
    v_sh = dint("v_sh", [T, KVW], BF16, shared=True)
    ln3_sh = dint("ln3_sh", [4 * NID * 128, CHK], BF16, shared=True)
    pc_sh = [dint(f"pc_sh{m}", [2 * QBLK, D], F32, shared=True) for m in range(2)]
    flags_sh = dint("flags_sh", [2, 16], I32, shared=True)

    def sb(name, shape, dt):
        return es.enter_context(nc.sbuf_tensor(name, shape, dt))

    ab1 = sb("ab1", [128, NDH * CHK], BF16)
    ab2 = sb("ab2", [128, NID * CHK], BF16)
    s_codes = [sb(f"s_codes{i}", [128, 512], I32) for i in range(2)]
    s_du = sb("s_du", [128, 512], F32)
    s_ds = sb("s_ds", [128, 512], F32)
    s_dz = sb("s_dz", [128, 512], F32)
    s_pe = [sb(f"s_pe{i}", [128, 512], F32) for i in range(4)]
    s_po = [sb(f"s_po{i}", [128, 512], F32) for i in range(4)]
    s_te = sb("s_te", [128, 512], F32)
    s_to = sb("s_to", [128, 512], F32)
    s_wout = [sb(f"s_wout{i}", [128, 512], BF16) for i in range(2)]
    s_arow = [sb(f"s_arow{i}", [1, 512], F32) for i in range(2)]
    s_brow = [sb(f"s_brow{i}", [1, 512], F32) for i in range(2)]
    s_onesf = sb("s_onesf", [1, 64], F32)
    s_xt = sb("s_xt", [128, 2048], F32)
    s_p0 = sb("s_p0", [128, 2048], F32)
    s_p1 = sb("s_p1", [128, 2048], F32)
    s_ln = sb("s_ln", [128, 2048], F32)
    s_lnb = sb("s_lnb", [128, D], BF16)
    s_stat = sb("s_stat", [128, 8], F32)
    s_wg = [sb(f"s_wg{i}", [128, 512], BF16) for i in range(2)]
    s_wb = [sb(f"s_wb{i}", [128, 512], BF16) for i in range(2)]
    s_lbt = [sb(f"s_lbt{i}", [16, 512], BF16) for i in range(2)]
    s_y1 = sb("s_y1", [128, 2048], BF16)
    s_ev = [sb(f"s_ev{i}", [128, 512], F32) for i in range(2)]
    s_ptm = sb("s_ptm", [128, 2048], F32)
    s_la = sb("s_la", [128, (2 * NID + NDH) * R], BF16)
    s_t1 = {"g": sb("s_t1g", [16, 512], BF16),
            "u": sb("s_t1u", [16, 512], BF16),
            "d": sb("s_t1d", [16, 512], BF16)}
    s_rot = sb("s_rot", [128, 512], F32)
    s_ra = sb("s_ra", [128, 512], F32)
    s_kfm = sb("s_kfm", [128, T], BF16)
    s_qh = sb("s_qh", [128, CHK], BF16)
    s_ident = sb("s_ident", [128, 128], BF16)
    s_identf = sb("s_identf", [128, 128], F32)
    s_pt = sb("s_pt", [128, CHK], BF16)
    s_mask = [sb(f"s_mask{i}", [128, 2048], BF16) for i in range(2)]
    s_mneg = sb("s_mneg", [128, 40], F32)
    s_sig = sb("s_sig", [128, 8], F32)
    s_rcp = sb("s_rcp", [128, 8], F32)
    s_flag = sb("s_flag", [1, 8], I32)
    s_meta = sb("s_meta", [1, 8], I32)

    def ps(name, shape, dt):
        return es.enter_context(nc.psum_tensor(name, shape, dt))

    p_mm = [ps(f"p_mm{i}", [128, 512], F32) for i in range(4)]
    p_ma = [ps(f"p_ma{i}", [128, 512], F32) for i in range(2)]
    p_tr = ps("p_tr", [128, 512], F32)
    p_trb = ps("p_trb", [128, 1024], BF16)

    deq_tiles = []
    for mat in ("g", "u", "d"):
        nit = NID if mat != "d" else NDH
        width = DH if mat != "d" else D
        for it in range(nit):
            for oc in range((width + 511) // 512):
                deq_tiles.append((mat, it, oc, min(512, width - oc * 512)))

    V = dict(locals())
    V["dbg"] = dbg

    with es:
        with nc.Block() as block:
            _build_all(kb, block, V, stages, dbg)
    return nc



def _ln_to_fm(kb, block, V, src_fn, n_tt):
    ab2 = V["ab2"]
    lx = kb.new_sem()
    ctr = kb.new_sem()
    acc = kb.new_sem()
    tT = kb.new_sem()
    fm = kb.new_sem()

    @block.sync
    def _(sync):
        for tt in range(n_tt):
            for half in range(2):
                if half == 0 and tt > 0:
                    sync.wait_ge(ctr.h, ctr.t(5 * tt - 2))
                sync.dma_start(out=(V["s_xt"] if half == 0 else V["s_p0"])[:, :],
                               in_=src_fn(tt, half)).then_inc(lx.h, lx.inc(16))

    @block.scalar
    def _(scalar):
        for tt in range(n_tt):
            scalar.wait_ge(ctr.h, ctr.t(5 * tt + 2))
            scalar.activation(out=V["s_xt"][:, :], in_=V["s_ln"][:, :],
                              func=AF.Square, accum_out=V["s_stat"][:, 2:3])
            scalar.activation(out=V["s_p0"][:, :], in_=V["s_p1"][:, :],
                              func=AF.Square, accum_out=V["s_stat"][:, 3:4]) \
                .then_inc(acc.h, acc.inc(1))
            scalar.wait_ge(ctr.h, ctr.t(5 * tt + 3))
            scalar.activation(out=V["s_stat"][:, 5:6], in_=V["s_stat"][:, 4:5],
                              func=AF.Ln)
            scalar.activation(out=V["s_y1"][:, 0:512], in_=V["s_y1"][:, 0:512],
                              func=AF.Copy)
            scalar.activation(out=V["s_stat"][:, 6:7], in_=V["s_stat"][:, 5:6],
                              func=AF.Exp, scale=-0.5).then_inc(acc.h, acc.inc(1))

    @block.tensor
    def _(tensor):
        for tt in range(n_tt):
            tensor.wait_ge(ctr.h, ctr.t(5 * tt + 5))
            for ig in range(4):
                g = tt * 4 + ig
                if g >= 1:
                    tensor.wait_ge(fm.h, fm.t(g))
                for i8 in range(8):
                    i = ig * 8 + i8
                    ins = tensor.transpose(out=V["p_trb"][:, i8 * 128:(i8 + 1) * 128],
                                           in_=V["s_lnb"][:, i * 128:(i + 1) * 128],
                                           identity=V["s_ident"][:, :])
                ins.then_inc(tT.h, tT.inc(1))

    @block.vector
    def _(vector):
        for tt in range(n_tt):
            vector.wait_ge(lx.h, lx.t((2 * tt + 2) * 16))
            vector.reduce_sum(out=V["s_stat"][:, 0:1], in_=V["s_xt"][:, :], axis=AX.X)
            vector.reduce_sum(out=V["s_stat"][:, 1:2], in_=V["s_p0"][:, :], axis=AX.X)
            _dsep(vector, V)
            vector.tensor_add(out=V["s_stat"][:, 0:1], in0=V["s_stat"][:, 0:1],
                              in1=V["s_stat"][:, 1:2])
            _dsep(vector, V)
            vector.tensor_scalar(out=V["s_stat"][:, 1:2], in0=V["s_stat"][:, 0:1],
                                 scalar1=1.0 / D, scalar2=None, op0=ALU.mult)
            _dsep(vector, V)
            vector.tensor_scalar(out=V["s_ln"][:, :], in0=V["s_xt"][:, :],
                                 scalar1=V["s_stat"][:, 1:2], scalar2=None,
                                 op0=ALU.subtract).then_inc(ctr.h, ctr.inc(1))
            vector.tensor_scalar(out=V["s_p1"][:, :], in0=V["s_p0"][:, :],
                                 scalar1=V["s_stat"][:, 1:2], scalar2=None,
                                 op0=ALU.subtract).then_inc(ctr.h, ctr.inc(1))
            vector.wait_ge(acc.h, acc.t(2 * tt + 1))
            vector.tensor_add(out=V["s_stat"][:, 4:5], in0=V["s_stat"][:, 2:3],
                              in1=V["s_stat"][:, 3:4])
            _dsep(vector, V)
            vector.tensor_scalar(out=V["s_stat"][:, 4:5], in0=V["s_stat"][:, 4:5],
                                 scalar1=1.0 / D, scalar2=EPS,
                                 op0=ALU.mult, op1=ALU.add).then_inc(ctr.h, ctr.inc(1))
            vector.wait_ge(acc.h, acc.t(2 * tt + 2))
            if tt > 0:
                vector.wait_ge(tT.h, tT.t(tt * 4))
            vector.tensor_scalar(out=V["s_lnb"][:, 0:2048], in0=V["s_ln"][:, :],
                                 scalar1=V["s_stat"][:, 6:7], scalar2=None,
                                 op0=ALU.mult).then_inc(ctr.h, ctr.inc(1))
            vector.tensor_scalar(out=V["s_lnb"][:, 2048:4096], in0=V["s_p1"][:, :],
                                 scalar1=V["s_stat"][:, 6:7], scalar2=None,
                                 op0=ALU.mult).then_inc(ctr.h, ctr.inc(1))
            for ig in range(4):
                vector.wait_ge(tT.h, tT.t(tt * 4 + ig + 1))
                vector.tensor_copy(
                    out=ab2[:, :].rearrange("p (i t) -> p i t", t=CHK)
                        [:, ig * 8:(ig + 1) * 8, tt * 128:(tt + 1) * 128],
                    in_=V["p_trb"][:, :].rearrange("p (a b) -> p a b", b=128)) \
                    .then_inc(fm.h, fm.inc(1))

    return fm



def _ffn(kb, block, V, phase):
    ab1, ab2 = V["ab1"], V["ab2"]
    p_mm, p_tr = V["p_mm"], V["p_tr"]
    ntc = 8 if phase == "A" else 4
    psh = V["pa_sh"] if phase == "A" else V["pc_sh"]
    NOC = 11
    ow = [512] * 10 + [384]
    NDC = 8

    for tc in range(ntc):
        t0 = tc * CHK
        if phase == "A":
            fm = _ln_to_fm(kb, block, V,
                           lambda tt, half, t0=t0: V["x_in"][t0 + tt * 128:t0 + (tt + 1) * 128,
                                                             half * 2048:(half + 1) * 2048],
                           NTT)
            fm_target = 4 * NTT
        else:
            fm = kb.new_sem()

            @block.sync
            def _(sync, tc=tc):
                for g8 in range(8):
                    sync.dma_start(
                        out=ab2[:, g8 * 2048:(g8 + 1) * 2048]
                            .rearrange("p (i t) -> p i t", t=CHK),
                        in_=V["ln3_sh"][(tc * NID + g8 * 4) * 128:(tc * NID + (g8 + 1) * 4) * 128, :]
                            .rearrange("(i p) t -> p i t", p=128)) \
                        .then_inc(fm.h, fm.inc(16))
            fm_target = 8 * 16
        _fence(kb, block, V)

        lt = kb.new_sem()
        le = kb.new_sem()
        lw = [kb.new_sem(), kb.new_sem()]
        lbl = [kb.new_sem(), kb.new_sem()]
        mmi = kb.new_sem()
        grp = kb.new_sem()
        rel = kb.new_sem()
        x3s = kb.new_sem()

        @block.sync
        def _(sync):
            k = 0
            for oc in range(NOC):
                for mj, mat in enumerate(("g", "u")):
                    kb2 = 2 * oc + mj
                    if kb2 >= 2:
                        sync.wait_ge(grp.h, grp.t(kb2 - 1))
                    lbo = 0 if mat == "g" else DH
                    sync.dma_start(out=V["s_lbt"][kb2 % 2][0:16, 0:ow[oc]],
                                   in_=V["lb_cache"][0:16, lbo + oc * 512:lbo + oc * 512 + ow[oc]]) \
                        .then_inc(lbl[kb2 % 2].h, lbl[kb2 % 2].inc(16))
                    for i in range(NID):
                        if k >= 2:
                            sync.wait_ge(mmi.h, mmi.t(k - 1))
                        sync.dma_start(out=V["s_wg"][k % 2][:, 0:ow[oc]],
                                       in_=V["w_cache"][mat][i * 128:(i + 1) * 128,
                                                             oc * 512:oc * 512 + ow[oc]]) \
                            .then_inc(lw[k % 2].h, lw[k % 2].inc(16))
                        k += 1

        @block.tensor
        def _(tensor):
            tensor.wait_ge(fm.h, fm.t(fm_target))
            for j, mat in enumerate(("g", "u")):
                for i in range(NID):
                    ins = tensor.matmul(out=p_mm[j][0:16, :],
                                  lhsT=V["s_la"][:, (j * NID + i) * R:(j * NID + i + 1) * R],
                                  rhs=ab2[:, i * CHK:(i + 1) * CHK],
                                  start=(i == 0), stop=(i == NID - 1))
                ins.then_inc(lt.h, lt.inc(1))
            tensor.wait_ge(le.h, le.t(2))
            k = 0
            for oc in range(NOC):
                nsub = (ow[oc] + 127) // 128
                for mj, mat in enumerate(("g", "u")):
                    kb2 = 2 * oc + mj
                    if mj == 0 and oc >= 1:
                        tensor.wait_ge(x3s.h, x3s.t(oc))
                    for i in range(NID):
                        tensor.wait_ge(lw[k % 2].h, lw[k % 2].t((k // 2 + 1) * 16))
                        for sub in range(nsub):
                            ins = tensor.matmul(out=p_mm[sub][:, :],
                                          lhsT=V["s_wg"][k % 2][:, sub * 128:(sub + 1) * 128],
                                          rhs=ab2[:, i * CHK:(i + 1) * CHK],
                                          start=(i == 0), stop=False)
                        ins.then_inc(mmi.h, mmi.inc(1))
                        k += 1
                    tensor.wait_ge(lbl[kb2 % 2].h, lbl[kb2 % 2].t((kb2 // 2 + 1) * 16))
                    for sub in range(nsub):
                        ins = tensor.matmul(out=p_mm[sub][:, :],
                                      lhsT=V["s_lbt"][kb2 % 2][0:16, sub * 128:(sub + 1) * 128],
                                      rhs=V["s_t1"][mat][0:16, :],
                                      start=False, stop=True)
                    ins.then_inc(grp.h, grp.inc(1))
                    if mj == 0:
                        tensor.wait_ge(rel.h, rel.t(oc + 1))

        @block.scalar
        def _(scalar):
            for oc in range(NOC):
                nsub = (ow[oc] + 127) // 128
                scalar.wait_ge(grp.h, grp.t(2 * oc + 1))
                if oc >= 1:
                    scalar.wait_ge(x3s.h, x3s.t(oc))
                for sub in range(nsub):
                    ins = scalar.activation(out=V["s_y1"][:, sub * 512:sub * 512 + 512],
                                            in_=p_mm[sub][:, :], func=AF.Relu)
                ins.then_inc(rel.h, rel.inc(1))

        @block.vector
        def _(vector):
            for j, mat in enumerate(("g", "u")):
                vector.wait_ge(lt.h, lt.t(j + 1))
                vector.tensor_copy(out=V["s_t1"][mat][0:16, :],
                                   in_=p_mm[j][0:16, :]).then_inc(le.h, le.inc(1))
            for oc in range(NOC):
                nsub = (ow[oc] + 127) // 128
                vector.wait_ge(grp.h, grp.t(2 * oc + 2))
                vector.wait_ge(rel.h, rel.t(oc + 1))
                for sub in range(nsub):
                    ot = oc * 4 + sub
                    ins = vector.tensor_mul(out=ab1[:, ot * CHK:(ot + 1) * CHK],
                                            in0=V["s_y1"][:, sub * 512:sub * 512 + 512],
                                            in1=p_mm[sub][:, :])
                ins.then_inc(x3s.h, x3s.inc(1))

        ltd = kb.new_sem()
        led = kb.new_sem()
        lwd = [kb.new_sem(), kb.new_sem()]
        lbd = [kb.new_sem(), kb.new_sem()]
        mdi = kb.new_sem()
        mdd = kb.new_sem()
        evd = kb.new_sem()
        trd = kb.new_sem()
        ptd = kb.new_sem()
        std = kb.new_sem()

        @block.sync
        def _(sync):
            k = 0
            for dc in range(NDC):
                if dc < 2:
                    sync.wait_ge(grp.h, grp.t(2 * NOC))
                if dc >= 2:
                    sync.wait_ge(mdd.h, mdd.t(dc - 1))
                sync.dma_start(out=V["s_lbt"][dc % 2][0:16, :],
                               in_=V["lb_cache"][0:16, 2 * DH + dc * 512:2 * DH + (dc + 1) * 512]) \
                    .then_inc(lbd[dc % 2].h, lbd[dc % 2].inc(16))
                for i in range(NDH):
                    if k >= 2:
                        sync.wait_ge(mdi.h, mdi.t(k - 1))
                    sync.dma_start(out=V["s_wg"][k % 2][:, :],
                                   in_=V["w_cache"]["d"][i * 128:(i + 1) * 128,
                                                         dc * 512:(dc + 1) * 512]) \
                        .then_inc(lwd[k % 2].h, lwd[k % 2].inc(16))
                    k += 1

        @block.tensor
        def _(tensor):
            tensor.wait_ge(x3s.h, x3s.t(NOC))
            for i in range(NDH):
                ins = tensor.matmul(out=p_mm[0][0:16, :],
                              lhsT=V["s_la"][:, (2 * NID + i) * R:(2 * NID + i + 1) * R],
                              rhs=ab1[:, i * CHK:(i + 1) * CHK],
                              start=(i == 0), stop=(i == NDH - 1))
            ins.then_inc(ltd.h, ltd.inc(1))
            tensor.wait_ge(led.h, led.t(1))
            k = 0
            for dc in range(NDC):
                if dc >= 1:
                    tensor.wait_ge(evd.h, evd.t(4 * dc))
                for i in range(NDH):
                    tensor.wait_ge(lwd[k % 2].h, lwd[k % 2].t((k // 2 + 1) * 16))
                    for sub in range(4):
                        ins = tensor.matmul(out=p_mm[sub][:, :],
                                      lhsT=V["s_wg"][k % 2][:, sub * 128:(sub + 1) * 128],
                                      rhs=ab1[:, i * CHK:(i + 1) * CHK],
                                      start=(i == 0), stop=False)
                    ins.then_inc(mdi.h, mdi.inc(1))
                    k += 1
                tensor.wait_ge(lbd[dc % 2].h, lbd[dc % 2].t((dc // 2 + 1) * 16))
                for sub in range(4):
                    ins = tensor.matmul(out=p_mm[sub][:, :],
                                  lhsT=V["s_lbt"][dc % 2][0:16, sub * 128:(sub + 1) * 128],
                                  rhs=V["s_t1"]["d"][0:16, :],
                                  start=False, stop=True)
                ins.then_inc(mdd.h, mdd.inc(1))
                for sub in range(4):
                    dt = 4 * dc + sub
                    tensor.wait_ge(evd.h, evd.t(dt + 1))
                    if dt >= 1:
                        tensor.wait_ge(ptd.h, ptd.t(dt))
                    for j in range(4):
                        ins = tensor.transpose(out=p_tr[:, j * 128:(j + 1) * 128],
                                         in_=V["s_ev"][sub % 2][:, j * 128:(j + 1) * 128],
                                         identity=V["s_identf"][:, :])
                    ins.then_inc(trd.h, trd.inc(1))

        @block.vector
        def _(vector):
            vector.wait_ge(ltd.h, ltd.t(1))
            vector.tensor_copy(out=V["s_t1"]["d"][0:16, :],
                               in_=p_mm[0][0:16, :]).then_inc(led.h, led.inc(1))
            for dc in range(NDC):
                vector.wait_ge(mdd.h, mdd.t(dc + 1))
                for sub in range(4):
                    dt = 4 * dc + sub
                    if dt >= 2:
                        vector.wait_ge(trd.h, trd.t(dt - 1))
                    vector.tensor_copy(out=V["s_ev"][sub % 2][:, :],
                                       in_=p_mm[sub][:, :]).then_inc(evd.h, evd.inc(1))
                    vector.wait_ge(trd.h, trd.t(dt + 1))
                    if dc >= 1 and sub == 0:
                        vector.wait_ge(std.h, std.t(dc * 64))
                    vector.tensor_copy(
                        out=V["s_ptm"][:, :].rearrange("p (j s c) -> p j s c", s=4, c=128)
                            [:, :, sub, :],
                        in_=p_tr[:, :].rearrange("p (j c) -> p j c", c=128)) \
                        .then_inc(ptd.h, ptd.inc(1))

        @block.gpsimd
        def _(gps, tc=tc, t0=t0):
            member = _getreg(gps, V, "g_member")
            gps.load(member, V["s_meta"][0:1, 0:1])
            for dc in range(NDC):
                gps.wait_ge(ptd.h, ptd.t(4 * dc + 4))
                for j in range(4):
                    row = t0 + j * 128
                    src = V["s_ptm"][:, j * 512:(j + 1) * 512]
                    with gps.If_eq(member, 0):
                        gps.dma_start(out=psh[0][row:row + 128, dc * 512:(dc + 1) * 512],
                                      in_=src).then_inc(std.h, std.inc(16))
                    with gps.Else():
                        gps.dma_start(out=psh[1][row:row + 128, dc * 512:(dc + 1) * 512],
                                      in_=src).then_inc(std.h, 16)
            gps.wait_ge(std.h, std.n)
            if V.get("dbg") and phase == "A" and tc == 0:
                gps.dma_start(out=V["dbg_ab1"], in_=ab1[:, :]).then_inc(std.h, std.inc(16))
                gps.dma_start(out=V["dbg_ab2"], in_=ab2[:, :]).then_inc(std.h, std.inc(16))
                gps.wait_ge(std.h, std.n)

        _fence(kb, block, V)



def _qkv(kb, block, V):
    ab2 = V["ab2"]
    p_mm = V["p_mm"]

    for tc in range(8):
        t0 = tc * CHK
        fm = _ln_to_fm(kb, block, V,
                       lambda tt, half, t0=t0: V["x1_sh"][t0 + tt * 128:t0 + (tt + 1) * 128,
                                                          half * 2048:(half + 1) * 2048],
                       NTT)
        _fence(kb, block, V)

        for j in range(2):
            lw = [kb.new_sem(), kb.new_sem()]
            mmi = kb.new_sem()
            kro = kb.new_sem()
            kfe = kb.new_sem()
            ktT = kb.new_sem()
            stk = kb.new_sem()
            csl = kb.new_sem()

            @block.sync
            def _(sync, j=j, t0=t0):
                if j == 0:
                    sync.dma_start(out=V["s_xt"][:, :].rearrange("p (a c) -> p a c", c=512),
                                   in_=V["cosk_in"][t0:t0 + 512, :]
                                   .rearrange("(a p) c -> p a c", p=128)) \
                        .then_inc(csl.h, csl.inc(16))
                    sync.dma_start(out=V["s_p0"][:, :].rearrange("p (a c) -> p a c", c=512),
                                   in_=V["sink_in"][t0:t0 + 512, :]
                                   .rearrange("(a p) c -> p a c", p=128)) \
                        .then_inc(csl.h, csl.inc(16))
                k = 0
                for ttp in range(2):
                    for i in range(NID):
                        if k >= 2:
                            sync.wait_ge(mmi.h, mmi.t(k - 1))
                        sync.dma_start(out=V["s_wb"][k % 2][:, :],
                                       in_=V["wkv_cache"][i * 128:(i + 1) * 128,
                                                          j * 512:(j + 1) * 512]) \
                            .then_inc(lw[k % 2].h, lw[k % 2].inc(16))
                        k += 1

            @block.tensor
            def _(tensor, j=j):
                tensor.wait_ge(fm.h, fm.t(4 * NTT))
                k = 0
                for ttp in range(2):
                    if ttp == 1:
                        tensor.wait_ge(kro.h, kro.t(2))
                    for i in range(NID):
                        tensor.wait_ge(lw[k % 2].h, lw[k % 2].t((k // 2 + 1) * 16))
                        for h2 in range(2):
                            tt = ttp * 2 + h2
                            ins = tensor.matmul(out=p_mm[h2][:, :],
                                          lhsT=ab2[:, i * CHK + tt * 128:i * CHK + (tt + 1) * 128],
                                          rhs=V["s_wb"][k % 2][:, :],
                                          start=(i == 0), stop=(i == NID - 1))
                        ins.then_inc(mmi.h, mmi.inc(1))
                        k += 1
                    if j == 0:
                        for h2 in range(2):
                            tt = ttp * 2 + h2
                            tensor.wait_ge(kro.h, kro.t(tt + 1))
                            if tt >= 1:
                                tensor.wait_ge(kfe.h, kfe.t(tt))
                            for sub in range(4):
                                ins = tensor.transpose(out=V["p_trb"][:, sub * 128:(sub + 1) * 128],
                                                 in_=V["s_lnb"][:, sub * 128:(sub + 1) * 128],
                                                 identity=V["s_ident"][:, :])
                            ins.then_inc(ktT.h, ktT.inc(1))

            @block.vector
            def _(vector, j=j):
                if j == 0:
                    vector.wait_ge(csl.h, csl.t(32))
                for ttp in range(2):
                    vector.wait_ge(mmi.h, mmi.t((ttp + 1) * 32))
                    for h2 in range(2):
                        tt = ttp * 2 + h2
                        if j == 1:
                            if tt >= 1:
                                vector.wait_ge(stk.h, stk.t(tt * 16))
                            vector.tensor_copy(out=V["s_lnb"][:, 0:512],
                                               in_=p_mm[h2][:, :]) \
                                .then_inc(kro.h, kro.inc(1))
                        else:
                            pm3 = p_mm[h2][:, :].rearrange("p (a c) -> p a c", c=128)
                            rt3 = V["s_rot"][:, :].rearrange("p (a c) -> p a c", c=128)
                            vector.tensor_scalar(out=rt3[:, :, 0:64], in0=pm3[:, :, 64:128],
                                                 scalar1=-1.0, scalar2=None, op0=ALU.mult)
                            vector.tensor_copy(out=rt3[:, :, 64:128], in_=pm3[:, :, 0:64])
                            vector.tensor_mul(out=V["s_ra"][:, :], in0=p_mm[h2][:, :],
                                              in1=V["s_xt"][:, tt * 512:(tt + 1) * 512])
                            vector.tensor_mul(out=V["s_rot"][:, :], in0=V["s_rot"][:, :],
                                              in1=V["s_p0"][:, tt * 512:(tt + 1) * 512])
                            if tt >= 1:
                                vector.wait_ge(ktT.h, ktT.t(tt))
                            vector.tensor_add(out=V["s_lnb"][:, 0:512], in0=V["s_ra"][:, :],
                                              in1=V["s_rot"][:, :]).then_inc(kro.h, kro.inc(1))
                            vector.wait_ge(ktT.h, ktT.t(tt + 1))
                            vector.tensor_copy(
                                out=V["s_kfm"][:, 0:2048].rearrange("p (s t) -> p s t", t=512)
                                    [:, :, tt * 128:(tt + 1) * 128],
                                in_=V["p_trb"][:, 0:512].rearrange("p (s c) -> p s c", c=128)) \
                                .then_inc(kfe.h, kfe.inc(1))

            @block.gpsimd
            def _(gps, j=j, t0=t0):
                gmem = _getreg(gps, V, "g_member")
                gps.load(gmem, V["s_meta"][0:1, 0:1])
                if j == 0:
                    gps.wait_ge(kfe.h, kfe.t(4))
                    for sub in range(4):
                        with gps.If_eq(gmem, 0):
                            gps.dma_start(out=V["k_sh"][sub * 128:(sub + 1) * 128, t0:t0 + 512],
                                          in_=V["s_kfm"][:, sub * 512:(sub + 1) * 512]) \
                                .then_inc(stk.h, stk.inc(16))
                        with gps.Else():
                            gps.dma_start(out=V["k_sh"][512 + sub * 128:512 + (sub + 1) * 128,
                                                        t0:t0 + 512],
                                          in_=V["s_kfm"][:, sub * 512:(sub + 1) * 512]) \
                                .then_inc(stk.h, 16)
                else:
                    for tt in range(NTT):
                        gps.wait_ge(kro.h, kro.t(tt + 1))
                        r = t0 + tt * 128
                        with gps.If_eq(gmem, 0):
                            gps.dma_start(out=V["v_sh"][r:r + 128, 0:512],
                                          in_=V["s_lnb"][:, 0:512]).then_inc(stk.h, stk.inc(16))
                        with gps.Else():
                            gps.dma_start(out=V["v_sh"][r:r + 128, 512:1024],
                                          in_=V["s_lnb"][:, 0:512]).then_inc(stk.h, 16)
                gps.wait_ge(stk.h, stk.n)

            _fence(kb, block, V)

    for qc2 in range(2):
        q0 = qc2 * CHK
        fmq = _ln_to_fm(kb, block, V,
                        lambda tt, half, q0=q0: V["x1own_dram"][q0 + tt * 128:q0 + (tt + 1) * 128,
                                                                half * 2048:(half + 1) * 2048],
                        NTT)
        _fence(kb, block, V)

        lwq = [kb.new_sem(), kb.new_sem()]
        mmq = kb.new_sem()
        rpq = kb.new_sem()
        stq = kb.new_sem()
        csl = kb.new_sem()
        qtT = kb.new_sem()
        qfe = kb.new_sem()

        @block.sync
        def _(sync, q0=q0):
            sync.dma_start(out=V["s_xt"][:, :].rearrange("p (a c) -> p a c", c=512),
                           in_=V["cosq_in"][q0:q0 + 512, :]
                           .rearrange("(a p) c -> p a c", p=128)).then_inc(csl.h, csl.inc(16))
            sync.dma_start(out=V["s_p0"][:, :].rearrange("p (a c) -> p a c", c=512),
                           in_=V["sinq_in"][q0:q0 + 512, :]
                           .rearrange("(a p) c -> p a c", p=128)).then_inc(csl.h, csl.inc(16))
            k = 0
            for qc in range(8):
                for ttp in range(2):
                    for i in range(NID):
                        if k >= 2:
                            sync.wait_ge(mmq.h, mmq.t(k - 1))
                        sync.dma_start(out=V["s_wb"][k % 2][:, :],
                                       in_=V["wq_cache"][i * 128:(i + 1) * 128,
                                                         qc * 512:(qc + 1) * 512]) \
                            .then_inc(lwq[k % 2].h, lwq[k % 2].inc(16))
                        k += 1

        @block.tensor
        def _(tensor):
            tensor.wait_ge(fmq.h, fmq.t(4 * NTT))
            k = 0
            for qc in range(8):
                for ttp in range(2):
                    g2 = qc * 2 + ttp
                    if g2 >= 1:
                        tensor.wait_ge(rpq.h, rpq.t(g2 * 2))
                    for i in range(NID):
                        tensor.wait_ge(lwq[k % 2].h, lwq[k % 2].t((k // 2 + 1) * 16))
                        for h2 in range(2):
                            tt = ttp * 2 + h2
                            ins = tensor.matmul(out=p_mm[h2][:, :],
                                          lhsT=ab2[:, i * CHK + tt * 128:i * CHK + (tt + 1) * 128],
                                          rhs=V["s_wb"][k % 2][:, :],
                                          start=(i == 0), stop=(i == NID - 1))
                        ins.then_inc(mmq.h, mmq.inc(1))
                        k += 1
                    for h2 in range(2):
                        tt = ttp * 2 + h2
                        gtt = qc * 4 + tt
                        tensor.wait_ge(rpq.h, rpq.t(gtt + 1))
                        if gtt >= 1:
                            tensor.wait_ge(qfe.h, qfe.t(gtt))
                        for sub in range(4):
                            ins = tensor.transpose(out=V["p_trb"][:, sub * 128:(sub + 1) * 128],
                                             in_=V["s_lnb"][:, sub * 128:(sub + 1) * 128],
                                             identity=V["s_ident"][:, :])
                        ins.then_inc(qtT.h, qtT.inc(1))


        @block.vector
        def _(vector):
            vector.wait_ge(csl.h, csl.t(32))
            idx = 0
            for qc in range(8):
                for ttp in range(2):
                    vector.wait_ge(mmq.h, mmq.t((qc * 2 + ttp + 1) * 32))
                    for h2 in range(2):
                        tt = ttp * 2 + h2
                        pm3 = p_mm[h2][:, :].rearrange("p (a c) -> p a c", c=128)
                        rt3 = V["s_rot"][:, :].rearrange("p (a c) -> p a c", c=128)
                        vector.tensor_scalar(out=rt3[:, :, 0:64], in0=pm3[:, :, 64:128],
                                             scalar1=-1.0, scalar2=None, op0=ALU.mult)
                        vector.tensor_copy(out=rt3[:, :, 64:128], in_=pm3[:, :, 0:64])
                        vector.tensor_mul(out=V["s_ra"][:, :], in0=p_mm[h2][:, :],
                                          in1=V["s_xt"][:, tt * 512:(tt + 1) * 512])
                        vector.tensor_mul(out=V["s_rot"][:, :], in0=V["s_rot"][:, :],
                                          in1=V["s_p0"][:, tt * 512:(tt + 1) * 512])
                        if idx >= 1:
                            vector.wait_ge(qtT.h, qtT.t(idx))
                        vector.tensor_add(out=V["s_lnb"][:, 0:512], in0=V["s_ra"][:, :],
                                          in1=V["s_rot"][:, :]).then_inc(rpq.h, rpq.inc(1))
                        vector.wait_ge(qtT.h, qtT.t(idx + 1))
                        if idx >= 4 and idx % 4 == 0:
                            vector.wait_ge(stq.h, stq.t(16 * idx))
                        vector.tensor_copy(
                            out=V["s_kfm"][:, 0:2048].rearrange("p (s t) -> p s t", t=512)
                                [:, :, tt * 128:(tt + 1) * 128],
                            in_=V["p_trb"][:, 0:512].rearrange("p (s c) -> p s c", c=128)) \
                            .then_inc(qfe.h, qfe.inc(1))
                        idx += 1

        @block.gpsimd
        def _(gps, q0=q0):
            for qc in range(8):
                gps.wait_ge(qfe.h, qfe.t((qc + 1) * 4))
                for sub in range(4):
                    gps.dma_start(out=V["q_fm"][qc * 512 + sub * 128:qc * 512 + (sub + 1) * 128,
                                                q0:q0 + 512],
                                  in_=V["s_kfm"][:, sub * 512:(sub + 1) * 512]) \
                        .then_inc(stq.h, stq.inc(16))
            gps.wait_ge(stq.h, stq.n)

        _fence(kb, block, V)



def _attn(kb, block, V):
    ab1, ab2 = V["ab1"], V["ab2"]
    p_mm, p_trb = V["p_mm"], V["p_trb"]

    grp_state = {}
    for qh in range(2):
        q0 = qh * CHK
        for h in range(NH):
            kvh = h // 4
            if h % 4 == 0:
                grp_state["klf"] = kb.new_sem()
                grp_state["vlk"] = kb.new_sem()
            klf = grp_state["klf"]
            vlk = grp_state["vlk"]
            ql = kb.new_sem()
            mm1 = kb.new_sem()
            rxp = kb.new_sem()
            rxf = kb.new_sem()
            mm2 = kb.new_sem()
            exl = kb.new_sem()
            mkl = [kb.new_sem(), kb.new_sem()]
            msk = kb.new_sem()
            sgm = kb.new_sem()
            ptT = kb.new_sem()
            pte = kb.new_sem()
            avm = kb.new_sem()

            @block.sync
            def _(sync, h=h, kvh=kvh, q0=q0):
                if h % 4 == 0:
                    sync.dma_start(out=V["s_kfm"][:, :],
                                   in_=V["k_sh"][kvh * 128:(kvh + 1) * 128, :]) \
                        .then_inc(klf.h, klf.inc(16))
                    for ks in range(32):
                        sync.dma_start(out=ab1[:, 16384 + ks * 128:16384 + (ks + 1) * 128],
                                       in_=V["v_sh"][ks * 128:(ks + 1) * 128,
                                                     kvh * 128:(kvh + 1) * 128]) \
                            .then_inc(vlk.h, vlk.inc(16))
                sync.dma_start(out=V["s_qh"][:, :],
                               in_=V["q_fm"][h * 128:(h + 1) * 128, q0:q0 + 512]) \
                    .then_inc(ql.h, ql.inc(16))
                for k in range(8):
                    qt, half = k // 2, k % 2
                    if k >= 2:
                        sync.wait_ge(msk.h, msk.t(k - 1))
                    sync.dma_start(out=V["s_mask"][k % 2][:, :],
                                   in_=V["mask_in"][q0 + qt * 128:q0 + (qt + 1) * 128,
                                                    half * 2048:(half + 1) * 2048]) \
                        .then_inc(mkl[k % 2].h, mkl[k % 2].inc(16))

            @block.tensor
            def _(tensor, h=h):
                tensor.wait_ge(ql.h, ql.t(16))
                tensor.wait_ge(klf.h, klf.t(16))
                for qt in range(4):
                    for uh in range(8):
                        idx = qt * 8 + uh
                        if idx >= 2:
                            tensor.wait_ge(rxp.h, rxp.t(idx - 1))
                        tensor.matmul(out=p_mm[uh % 2][:, :],
                                      lhsT=V["s_qh"][:, qt * 128:(qt + 1) * 128],
                                      rhs=V["s_kfm"][:, uh * 512:(uh + 1) * 512],
                                      start=True, stop=True) \
                            .then_inc(mm1.h, mm1.inc(1))
                tensor.wait_ge(rxf.h, rxf.t(4))
                for qt in range(4):
                    for uh in range(8):
                        idx = qt * 8 + uh
                        if idx >= 2:
                            tensor.wait_ge(exl.h, exl.t(idx - 1))
                        tensor.matmul(out=p_mm[uh % 2][:, :],
                                      lhsT=V["s_qh"][:, qt * 128:(qt + 1) * 128],
                                      rhs=V["s_kfm"][:, uh * 512:(uh + 1) * 512],
                                      start=True, stop=True) \
                            .then_inc(mm2.h, mm2.inc(1))
                tensor.wait_ge(sgm.h, sgm.t(5))
                if h % 4 == 0:
                    tensor.wait_ge(vlk.h, vlk.t(512))
                for ks in range(32):
                    if ks >= 1:
                        tensor.wait_ge(pte.h, pte.t(ks))
                    for qt in range(4):
                        ins = tensor.transpose(out=p_trb[:, qt * 128:(qt + 1) * 128],
                                         in_=ab1[:, qt * 4096 + ks * 128:qt * 4096 + ks * 128 + 128],
                                         identity=V["s_ident"][:, :])
                    ins.then_inc(ptT.h, ptT.inc(1))
                    tensor.wait_ge(pte.h, pte.t(ks + 1))
                    tensor.matmul(out=p_mm[2][:, :],
                                  lhsT=ab1[:, 16384 + ks * 128:16384 + (ks + 1) * 128],
                                  rhs=V["s_pt"][:, :],
                                  start=(ks == 0), stop=(ks == 31)) \
                        .then_inc(avm.h, avm.inc(1))

            @block.scalar
            def _(scalar, h=h):
                for qt in range(4):
                    for uh in range(8):
                        idx = qt * 8 + uh
                        scalar.wait_ge(mm2.h, mm2.t(idx + 1))
                        scalar.activation(
                            out=ab1[:, qt * 4096 + uh * 512:qt * 4096 + (uh + 1) * 512],
                            in_=p_mm[uh % 2][:, :], func=AF.Exp,
                            bias=V["s_mneg"][:, qt:qt + 1]).then_inc(exl.h, exl.inc(1))

            @block.vector
            def _(vector, h=h):
                for qt in range(4):
                    for uh in range(8):
                        c = 8 + qt * 8 + uh
                        vector.wait_ge(mm1.h, mm1.t(qt * 8 + uh + 1))
                        vector.reduce_max(out=V["s_mneg"][:, c:c + 1],
                                          in_=p_mm[uh % 2][:, :], axis=AX.X) \
                            .then_inc(rxp.h, rxp.inc(1))
                _dsep(vector, V)
                for qt in range(4):
                    vector.reduce_max(out=V["s_mneg"][:, qt:qt + 1],
                                      in_=V["s_mneg"][:, 8 + qt * 8:16 + qt * 8],
                                      axis=AX.X, negate=True) \
                        .then_inc(rxf.h, rxf.inc(1))
                for qt in range(4):
                    for half in range(2):
                        k = qt * 2 + half
                        vector.wait_ge(mkl[k % 2].h, mkl[k % 2].t((k // 2 + 1) * 16))
                        for u4 in range(4):
                            uh = half * 4 + u4
                            vector.wait_ge(exl.h, exl.t(qt * 8 + uh + 1))
                            ins = vector.tensor_mul(
                                out=ab1[:, qt * 4096 + uh * 512:qt * 4096 + (uh + 1) * 512],
                                in0=ab1[:, qt * 4096 + uh * 512:qt * 4096 + (uh + 1) * 512],
                                in1=V["s_mask"][k % 2][:, u4 * 512:(u4 + 1) * 512])
                        ins.then_inc(msk.h, msk.inc(1))
                    vector.reduce_sum(out=V["s_sig"][:, qt:qt + 1],
                                      in_=ab1[:, qt * 4096:(qt + 1) * 4096], axis=AX.X) \
                        .then_inc(sgm.h, sgm.inc(1))
                _dsep(vector, V)
                vector.reciprocal(out=V["s_rcp"][:, 0:4], in_=V["s_sig"][:, 0:4])
                _dsep(vector, V)
                for qt in range(4):
                    ins = vector.tensor_scalar(out=ab1[:, qt * 4096:(qt + 1) * 4096],
                                         in0=ab1[:, qt * 4096:(qt + 1) * 4096],
                                         scalar1=V["s_rcp"][:, qt:qt + 1], scalar2=None,
                                         op0=ALU.mult)
                ins.then_inc(sgm.h, sgm.inc(1))
                for ks in range(32):
                    vector.wait_ge(ptT.h, ptT.t(ks + 1))
                    if ks >= 1:
                        vector.wait_ge(avm.h, avm.t(ks))
                    vector.tensor_copy(out=V["s_pt"][:, :], in_=p_trb[:, 0:512]) \
                        .then_inc(pte.h, pte.inc(1))
                vector.wait_ge(avm.h, avm.t(32))
                vector.tensor_copy(out=ab2[:, h * 512:(h + 1) * 512], in_=p_mm[2][:, :])

            if V.get("dbg") and qh == 0 and h == 0:
                dmp = kb.new_sem()

                @block.gpsimd
                def _(gps):
                    gps.wait_ge(sgm.h, sgm.t(5))
                    for c4 in range(4):
                        gps.dma_start(out=V["dbg_P"][:, c4 * 4096:(c4 + 1) * 4096],
                                      in_=ab1[:, c4 * 4096:(c4 + 1) * 4096]) \
                            .then_inc(dmp.h, dmp.inc(16))
                    gps.dma_start(out=V["dbg_mn"][:, 0:40], in_=V["s_mneg"][:, :]) \
                        .then_inc(dmp.h, dmp.inc(16))
                    gps.dma_start(out=V["dbg_mn"][:, 40:44], in_=V["s_sig"][:, 0:4]) \
                        .then_inc(dmp.h, dmp.inc(16))
                    gps.dma_start(out=V["dbg_mn"][:, 44:48], in_=V["s_rcp"][:, 0:4]) \
                        .then_inc(dmp.h, dmp.inc(16))
                    gps.wait_ge(dmp.h, dmp.n)

            _fence(kb, block, V)

        lwo = [kb.new_sem(), kb.new_sem()]
        wti = kb.new_sem()
        x1l = [kb.new_sem(), kb.new_sem()]
        ewo = kb.new_sem()
        swo = kb.new_sem()

        @block.sync
        def _(sync, q0=q0):
            k = 0
            for dc in range(8):
                if dc >= 2:
                    sync.wait_ge(ewo.h, ewo.t((dc - 1) * 4))
                xbuf = V["s_xt"] if dc % 2 == 0 else V["s_p0"]
                for qt in range(4):
                    sync.dma_start(out=xbuf[:, qt * 512:(qt + 1) * 512],
                                   in_=V["x1own_dram"][q0 + qt * 128:q0 + (qt + 1) * 128,
                                                       dc * 512:(dc + 1) * 512]) \
                        .then_inc(x1l[dc % 2].h, x1l[dc % 2].inc(16))
                for ot in range(NID):
                    if k >= 2:
                        sync.wait_ge(wti.h, wti.t(k - 1))
                    sync.dma_start(out=V["s_wb"][k % 2][:, :],
                                   in_=V["wo_cache"][ot * 128:(ot + 1) * 128,
                                                     dc * 512:(dc + 1) * 512]) \
                        .then_inc(lwo[k % 2].h, lwo[k % 2].inc(16))
                    k += 1

        @block.tensor
        def _(tensor):
            k = 0
            for dc in range(8):
                if dc >= 1:
                    tensor.wait_ge(ewo.h, ewo.t(dc * 4))
                for ot in range(NID):
                    tensor.wait_ge(lwo[k % 2].h, lwo[k % 2].t((k // 2 + 1) * 16))
                    for qt in range(4):
                        ins = tensor.matmul(out=p_mm[qt][:, :],
                                      lhsT=ab2[:, ot * 512 + qt * 128:ot * 512 + (qt + 1) * 128],
                                      rhs=V["s_wb"][k % 2][:, :],
                                      start=(ot == 0), stop=(ot == NID - 1))
                    ins.then_inc(wti.h, wti.inc(1))
                    k += 1

        @block.vector
        def _(vector):
            for dc in range(8):
                vector.wait_ge(wti.h, wti.t((dc + 1) * 32))
                vector.wait_ge(x1l[dc % 2].h, x1l[dc % 2].t((dc // 2 + 1) * 64))
                xbuf = V["s_xt"] if dc % 2 == 0 else V["s_p0"]
                for qt in range(4):
                    idx = dc * 4 + qt
                    if idx >= 2:
                        vector.wait_ge(swo.h, swo.t((idx - 1) * 16))
                    vector.scalar_tensor_tensor(out=V["s_ev"][qt % 2][:, :],
                                                in0=p_mm[qt][:, :], scalar=1.0,
                                                in1=xbuf[:, qt * 512:(qt + 1) * 512],
                                                op0=ALU.mult, op1=ALU.add) \
                        .then_inc(ewo.h, ewo.inc(1))

        @block.gpsimd
        def _(gps, q0=q0):
            for dc in range(8):
                for qt in range(4):
                    gps.wait_ge(ewo.h, ewo.t(dc * 4 + qt + 1))
                    gps.dma_start(out=V["x2_dram"][q0 + qt * 128:q0 + (qt + 1) * 128,
                                                   dc * 512:(dc + 1) * 512],
                                  in_=V["s_ev"][qt % 2][:, :]).then_inc(swo.h, swo.inc(16))
            gps.wait_ge(swo.h, swo.n)

        _fence(kb, block, V)


def _ln3(kb, block, V):
    for qc2 in range(2):
        q0 = qc2 * CHK
        fm = _ln_to_fm(kb, block, V,
                       lambda tt, half, q0=q0: V["x2_dram"][q0 + tt * 128:q0 + (tt + 1) * 128,
                                                            half * 2048:(half + 1) * 2048],
                       NTT)
        st3 = kb.new_sem()

        @block.gpsimd
        def _(gps, qc2=qc2):
            gmem = _getreg(gps, V, "g_member")
            gps.load(gmem, V["s_meta"][0:1, 0:1])
            gps.wait_ge(fm.h, fm.t(4 * NTT))
            for g8 in range(8):
                r0 = g8 * 4 * 128
                src = V["ab2"][:, g8 * 2048:(g8 + 1) * 2048] \
                    .rearrange("p (i t) -> p i t", t=CHK)
                with gps.If_eq(gmem, 0):
                    gps.dma_start(
                        out=V["ln3_sh"][qc2 * NID * 128 + r0:qc2 * NID * 128 + r0 + 512, :]
                        .rearrange("(i p) t -> p i t", p=128),
                        in_=src).then_inc(st3.h, st3.inc(16))
                with gps.Else():
                    gps.dma_start(
                        out=V["ln3_sh"][(2 + qc2) * NID * 128 + r0:(2 + qc2) * NID * 128 + r0 + 512, :]
                        .rearrange("(i p) t -> p i t", p=128),
                        in_=src).then_inc(st3.h, 16)
            gps.wait_ge(st3.h, st3.n)

        _fence(kb, block, V)


def _finish(kb, block, V):
    fl = kb.new_sem()
    fv = kb.new_sem()
    fs = kb.new_sem()

    @block.sync
    def _(sync):
        smem = _getreg(sync, V, "s_member")
        sync.load(smem, V["s_meta"][0:1, 0:1])
        for k in range(16):
            tt, half = k // 2, k % 2
            if k > 0:
                sync.wait_ge(fv.h, fv.t(k))
            hs = slice(half * 2048, (half + 1) * 2048)
            sync.dma_start(out=V["s_xt"][:, :],
                           in_=V["x2_dram"][tt * 128:(tt + 1) * 128, hs]) \
                .then_inc(fl.h, fl.inc(16))
            with sync.If_eq(smem, 0):
                r = tt * 128
                sync.dma_start(out=V["s_p0"][:, :],
                               in_=V["pc_sh"][0][r:r + 128, hs]).then_inc(fl.h, fl.inc(16))
                sync.dma_start(out=V["s_p1"][:, :],
                               in_=V["pc_sh"][1][r:r + 128, hs]).then_inc(fl.h, fl.inc(16))
            with sync.Else():
                r = 1024 + tt * 128
                sync.dma_start(out=V["s_p0"][:, :],
                               in_=V["pc_sh"][0][r:r + 128, hs]).then_inc(fl.h, 16)
                sync.dma_start(out=V["s_p1"][:, :],
                               in_=V["pc_sh"][1][r:r + 128, hs]).then_inc(fl.h, 16)

    @block.vector
    def _(vector):
        for k in range(16):
            vector.wait_ge(fl.h, fl.t((k + 1) * 48))
            if k >= 1:
                vector.wait_ge(fs.h, fs.t(k * 16))
            vector.tensor_add(out=V["s_p0"][:, :], in0=V["s_p0"][:, :], in1=V["s_p1"][:, :])
            vector.scalar_tensor_tensor(out=V["s_ln"][:, :], in0=V["s_p0"][:, :],
                                        scalar=0.5, in1=V["s_xt"][:, :],
                                        op0=ALU.mult, op1=ALU.add) \
                .then_inc(fv.h, fv.inc(1))

    @block.gpsimd
    def _(gps):
        for k in range(16):
            tt, half = k // 2, k % 2
            gps.wait_ge(fv.h, fv.t(k + 1))
            gps.dma_start(out=V["out_x3"][tt * 128:(tt + 1) * 128,
                                          half * 2048:(half + 1) * 2048],
                          in_=V["s_ln"][:, :]).then_inc(fs.h, fs.inc(16))
        gps.wait_ge(fs.h, fs.n)


def _dbg_out(kb, block, V, upto):
    ze = kb.new_sem()
    st = kb.new_sem()

    @block.vector
    def _(vector):
        vector.memset(V["s_ln"][:, :], 0.0).then_inc(ze.h, ze.inc(1))

    @block.gpsimd
    def _(gps):
        gps.wait_ge(ze.h, ze.t(1))
        if V.get("dbg"):
            if upto >= 1:
                gps.dma_start(out=V["dbg_w"][:, :],
                              in_=V["w_cache"]["g"][0:256, :]).then_inc(st.h, st.inc(16))
            if upto >= 2:
                for q in range(4):
                    gps.dma_start(out=V["dbg_x1"][q * 1024:(q + 1) * 1024, :],
                                  in_=V["x1_sh"][q * 1024:(q + 1) * 1024, :]) \
                        .then_inc(st.h, st.inc(16))
                    for m in range(2):
                        gps.dma_start(out=V["dbg_pa"][m][q * 1024:(q + 1) * 1024, :],
                                      in_=V["pa_sh"][m][q * 1024:(q + 1) * 1024, :]) \
                            .then_inc(st.h, st.inc(16))
            if upto >= 3:
                gps.dma_start(out=V["dbg_k"][:, :], in_=V["k_sh"][:, :]).then_inc(st.h, st.inc(16))
                gps.dma_start(out=V["dbg_v"][:, :], in_=V["v_sh"][:, :]).then_inc(st.h, st.inc(16))
                gps.dma_start(out=V["dbg_q"][:, :], in_=V["q_fm"][:, :]) \
                    .then_inc(st.h, st.inc(16))
            if upto >= 4:
                gps.dma_start(out=V["dbg_x2"][:, :], in_=V["x2_dram"][:, :]).then_inc(st.h, st.inc(16))
        for k in range(16):
            tt, half = k // 2, k % 2
            gps.dma_start(out=V["out_x3"][tt * 128:(tt + 1) * 128,
                                          half * 2048:(half + 1) * 2048],
                          in_=V["s_ln"][:, :]).then_inc(st.h, st.inc(16))
        gps.wait_ge(st.h, st.n)



def _build_all(kb, block, V, stages=9, dbg=False):
    p_ma = V["p_ma"]

    m0 = kb.new_sem()

    @block.sync
    def _(sync):
        sync.dma_start(out=V["s_meta"][:], in_=V["meta_in"]).then_inc(m0.h, m0.inc(16))
        sync.dma_start(out=V["s_ident"][:, :], in_=V["identb_in"]).then_inc(m0.h, m0.inc(16))
        sync.dma_start(out=V["s_identf"][:, :], in_=V["identf_in"]).then_inc(m0.h, m0.inc(16))

    @block.vector
    def _(vector):
        vector.memset(V["s_onesf"][:], 1.0)
        vector.memset(V["s_lnb"][:, :], 0.0)
        vector.wait_ge(m0.h, m0.t(48))

    _fence(kb, block, V)

    lca = kb.new_sem()
    lcv = kb.new_sem()
    la_specs = [("g", NID, 0), ("u", NID, NID * R), ("d", NDH, 2 * NID * R)]

    @block.sync
    def _(sync):
        for k, (mat, nit, dst) in enumerate(la_specs):
            if k >= 1:
                sync.wait_ge(lcv.h, lcv.t(k))
            for i in range(nit):
                sync.dma_start(out=V["s_xt"][:, i * R:(i + 1) * R],
                               in_=V["la_in"][mat][i * 128:(i + 1) * 128, :]) \
                    .then_inc(lca.h, lca.inc(16))

    @block.vector
    def _(vector):
        tot = 0
        for k, (mat, nit, dst) in enumerate(la_specs):
            tot += nit
            vector.wait_ge(lca.h, lca.t(tot * 16))
            vector.tensor_copy(out=V["s_la"][:, dst:dst + nit * R],
                               in_=V["s_xt"][:, 0:nit * R]).then_inc(lcv.h, lcv.inc(1))

    lb_specs = []
    for mat, width, lbo in (("g", DH, 0), ("u", DH, DH), ("d", D, 2 * DH)):
        for c0 in range(0, width, 2048):
            lb_specs.append((mat, c0, lbo, min(2048, width - c0)))
    lbL = kb.new_sem()
    lbC = kb.new_sem()
    lbS = kb.new_sem()

    @block.sync
    def _(sync):
        sync.wait_ge(lcv.h, lcv.t(3))
        for k, (mat, c0, lbo, w) in enumerate(lb_specs):
            if k >= 1:
                sync.wait_ge(lbC.h, lbC.t(k))
            sync.dma_start(out=V["s_xt"][0:16, 0:w],
                           in_=V["lb_in"][mat][0:16, c0:c0 + w]) \
                .then_inc(lbL.h, lbL.inc(16))

    @block.vector
    def _(vector):
        for k, (mat, c0, lbo, w) in enumerate(lb_specs):
            vector.wait_ge(lbL.h, lbL.t((k + 1) * 16))
            if k >= 1:
                vector.wait_ge(lbS.h, lbS.t(k * 16))
            vector.tensor_copy(out=V["s_lnb"][0:16, 0:w],
                               in_=V["s_xt"][0:16, 0:w]).then_inc(lbC.h, lbC.inc(1))

    @block.scalar
    def _(scalar):
        for k, (mat, c0, lbo, w) in enumerate(lb_specs):
            scalar.wait_ge(lbC.h, lbC.t(k + 1))
            scalar.dma_start(out=V["lb_cache"][0:16, lbo + c0:lbo + c0 + w],
                             in_=V["s_lnb"][0:16, 0:w]).then_inc(lbS.h, lbS.inc(16))
        scalar.wait_ge(lbS.h, lbS.n)

    _fence(kb, block, V)

    wc_specs = []
    for it in range(NID):
        wc_specs.append((V["wk_in"], it * 128, 0, V["wkv_cache"], it * 128, 0))
        wc_specs.append((V["wv_in"], it * 128, 0, V["wkv_cache"], it * 128, 512))
    for it in range(NID):
        for oc in range(8):
            wc_specs.append((V["wq_in"], it * 128, oc * 512, V["wq_cache"], it * 128, oc * 512))
            wc_specs.append((V["wo_in"], it * 128, oc * 512, V["wo_cache"], it * 128, oc * 512))
    wca = [kb.new_sem(), kb.new_sem()]
    wcv = kb.new_sem()
    wcs = kb.new_sem()

    @block.sync
    def _(sync):
        for k, (src, r0, c0, dst, dr, dc) in enumerate(wc_specs):
            if k >= 2:
                sync.wait_ge(wcv.h, wcv.t(k - 1))
            sync.dma_start(out=(V["s_xt"] if k % 2 == 0 else V["s_p0"])[:, 0:512],
                           in_=src[r0:r0 + 128, c0:c0 + 512]) \
                .then_inc(wca[k % 2].h, wca[k % 2].inc(16))

    @block.vector
    def _(vector):
        for k in range(len(wc_specs)):
            vector.wait_ge(wca[k % 2].h, wca[k % 2].t((k // 2 + 1) * 16))
            if k >= 2:
                vector.wait_ge(wcs.h, wcs.t((k - 1) * 16))
            vector.tensor_copy(out=V["s_wb"][k % 2][:, :],
                               in_=(V["s_xt"] if k % 2 == 0 else V["s_p0"])[:, 0:512]) \
                .then_inc(wcv.h, wcv.inc(1))

    @block.scalar
    def _(scalar):
        for k, (src, r0, c0, dst, dr, dc) in enumerate(wc_specs):
            scalar.wait_ge(wcv.h, wcv.t(k + 1))
            scalar.dma_start(out=dst[dr:dr + 128, dc:dc + 512],
                             in_=V["s_wb"][k % 2][:, :]).then_inc(wcs.h, wcs.inc(16))
        scalar.wait_ge(wcs.h, wcs.n)

    _fence(kb, block, V)

    deq = V["deq_tiles"]
    cdl = [kb.new_sem(), kb.new_sem()]
    arl = [kb.new_sem(), kb.new_sem()]
    mrow = kb.new_sem()
    dqa = kb.new_sem()
    dqv = kb.new_sem()
    wst = kb.new_sem()

    @block.sync
    def _(sync):
        for k, (mat, it, oc, w) in enumerate(deq):
            if k >= 2:
                sync.wait_ge(dqv.h, dqv.t(k - 1))
            sync.dma_start(out=V["s_codes"][k % 2][:, 0:w],
                           in_=V["codes_in"][mat][it * 128:(it + 1) * 128,
                                                  oc * 512:oc * 512 + w]) \
                .then_inc(cdl[k % 2].h, cdl[k % 2].inc(16))
            sync.dma_start(out=V["s_arow"][k % 2][0:1, 0:w],
                           in_=V["am_in"][mat][2 * it:2 * it + 1, oc * 512:oc * 512 + w]) \
                .then_inc(arl[k % 2].h, arl[k % 2].inc(16))
            sync.dma_start(out=V["s_brow"][k % 2][0:1, 0:w],
                           in_=V["am_in"][mat][2 * it + 1:2 * it + 2, oc * 512:oc * 512 + w]) \
                .then_inc(arl[k % 2].h, arl[k % 2].inc(16))

    @block.tensor
    def _(tensor):
        for k, (mat, it, oc, w) in enumerate(deq):
            tensor.wait_ge(arl[k % 2].h, arl[k % 2].t((k // 2 + 1) * 32))
            if k >= 2:
                tensor.wait_ge(dqv.h, dqv.t(k - 1))
            tensor.matmul(out=p_ma[k % 2][0:64, 0:w], lhsT=V["s_onesf"][0:1, 0:64],
                          rhs=V["s_arow"][k % 2][0:1, 0:w], start=True, stop=True)
            tensor.matmul(out=p_ma[k % 2][64:128, 0:w], lhsT=V["s_onesf"][0:1, 0:64],
                          rhs=V["s_brow"][k % 2][0:1, 0:w], start=True, stop=True,
                          tile_position=(0, 64)).then_inc(mrow.h, mrow.inc(1))

    @block.scalar
    def _(scalar):
        for k, (mat, it, oc, w) in enumerate(deq):
            scalar.wait_ge(cdl[k % 2].h, cdl[k % 2].t((k // 2 + 1) * 16))
            if k >= 1:
                scalar.wait_ge(dqv.h, dqv.t(k))
            cw = slice(0, w)
            scalar.activation(out=V["s_du"][:, cw], in_=V["s_codes"][k % 2][:, cw],
                              func=AF.Copy, scale=1.0 / 7.5, bias=-1.0)
            scalar.activation(out=V["s_ds"][:, cw], in_=V["s_du"][:, cw], func=AF.Square)
            scalar.activation(out=V["s_dz"][:, cw], in_=V["s_ds"][:, cw], func=AF.Square)
            for j in range(4):
                ins = scalar.activation(out=V["s_pe"][j][:, cw], in_=V["s_ds"][:, cw],
                                        func=AF.Copy, scale=CE[2 * j + 1], bias=CE[2 * j])
            ins.then_inc(dqa.h, dqa.inc(1))

    @block.vector
    def _(vector):
        for k, (mat, it, oc, w) in enumerate(deq):
            vector.wait_ge(dqa.h, dqa.t(k + 1))
            vector.wait_ge(mrow.h, mrow.t(k + 1))
            if k >= 2:
                vector.wait_ge(wst.h, wst.t((k - 1) * 16))
            cw = slice(0, w)
            for j in range(4):
                vector.tensor_scalar(out=V["s_po"][j][:, cw], in0=V["s_ds"][:, cw],
                                     scalar1=CO[2 * j + 1], scalar2=CO[2 * j],
                                     op0=ALU.mult, op1=ALU.add)
            for tgt, parts in ((V["s_te"], V["s_pe"]), (V["s_to"], V["s_po"])):
                vector.scalar_tensor_tensor(out=tgt[:, cw], in0=parts[3][:, cw],
                                            scalar=0.0, in1=V["s_dz"][:, cw],
                                            op0=ALU.add, op1=ALU.mult)
                vector.tensor_add(out=tgt[:, cw], in0=tgt[:, cw], in1=parts[2][:, cw])
                vector.scalar_tensor_tensor(out=tgt[:, cw], in0=tgt[:, cw],
                                            scalar=0.0, in1=V["s_dz"][:, cw],
                                            op0=ALU.add, op1=ALU.mult)
                vector.tensor_add(out=tgt[:, cw], in0=tgt[:, cw], in1=parts[1][:, cw])
                vector.scalar_tensor_tensor(out=tgt[:, cw], in0=tgt[:, cw],
                                            scalar=0.0, in1=V["s_dz"][:, cw],
                                            op0=ALU.add, op1=ALU.mult)
                vector.tensor_add(out=tgt[:, cw], in0=tgt[:, cw], in1=parts[0][:, cw])
            vector.tensor_mul(out=V["s_to"][:, cw], in0=V["s_to"][:, cw],
                              in1=V["s_du"][:, cw])
            vector.tensor_add(out=V["s_te"][:, cw], in0=V["s_te"][:, cw],
                              in1=V["s_to"][:, cw])
            vector.tensor_mul(out=V["s_wout"][k % 2][:, cw], in0=V["s_te"][:, cw],
                              in1=p_ma[k % 2][:, cw]).then_inc(dqv.h, dqv.inc(1))

    @block.gpsimd
    def _(gps):
        for k, (mat, it, oc, w) in enumerate(deq):
            gps.wait_ge(dqv.h, dqv.t(k + 1))
            gps.dma_start(out=V["w_cache"][mat][it * 128:(it + 1) * 128,
                                                oc * 512:oc * 512 + w],
                          in_=V["s_wout"][k % 2][:, 0:w]).then_inc(wst.h, wst.inc(16))
        gps.wait_ge(wst.h, wst.n)

    _fence(kb, block, V)
    if stages <= 1:
        _dbg_out(kb, block, V, 1)
        return

    _ffn(kb, block, V, "A")
    _barrier(kb, block, V, 0)

    xa = kb.new_sem()
    xl = kb.new_sem()
    xs = kb.new_sem()

    @block.sync
    def _(sync):
        smem = _getreg(sync, V, "s_member")
        sync.load(smem, V["s_meta"][0:1, 0:1])
        for k in range(32):
            tb, tt, half = k // 16, (k // 2) % 8, k % 2
            if k > 0:
                sync.wait_ge(xa.h, xa.t(k))
            hs = slice(half * 2048, (half + 1) * 2048)
            with sync.If_eq(smem, 0):
                r = tb * 2048 + tt * 128
                sync.dma_start(out=V["s_xt"][:, :], in_=V["x_in"][r:r + 128, hs]) \
                    .then_inc(xl.h, xl.inc(16))
                sync.dma_start(out=V["s_p0"][:, :], in_=V["pa_sh"][0][r:r + 128, hs]) \
                    .then_inc(xl.h, xl.inc(16))
                sync.dma_start(out=V["s_p1"][:, :], in_=V["pa_sh"][1][r:r + 128, hs]) \
                    .then_inc(xl.h, xl.inc(16))
            with sync.Else():
                r = 1024 + tb * 2048 + tt * 128
                sync.dma_start(out=V["s_xt"][:, :], in_=V["x_in"][r:r + 128, hs]) \
                    .then_inc(xl.h, 16)
                sync.dma_start(out=V["s_p0"][:, :], in_=V["pa_sh"][0][r:r + 128, hs]) \
                    .then_inc(xl.h, 16)
                sync.dma_start(out=V["s_p1"][:, :], in_=V["pa_sh"][1][r:r + 128, hs]) \
                    .then_inc(xl.h, 16)

    @block.vector
    def _(vector):
        for k in range(32):
            vector.wait_ge(xl.h, xl.t((k + 1) * 48))
            if k >= 1:
                vector.wait_ge(xs.h, xs.t(k * 32))
            vector.tensor_add(out=V["s_p0"][:, :], in0=V["s_p0"][:, :], in1=V["s_p1"][:, :])
            vector.scalar_tensor_tensor(out=V["s_ln"][:, :], in0=V["s_p0"][:, :],
                                        scalar=0.5, in1=V["s_xt"][:, :],
                                        op0=ALU.mult, op1=ALU.add) \
                .then_inc(xa.h, xa.inc(1))

    @block.gpsimd
    def _(gps):
        gmem = _getreg(gps, V, "g_member")
        gps.load(gmem, V["s_meta"][0:1, 0:1])
        for k in range(32):
            tb, tt, half = k // 16, (k // 2) % 8, k % 2
            hs = slice(half * 2048, (half + 1) * 2048)
            gps.wait_ge(xa.h, xa.t(k + 1))
            with gps.If_eq(gmem, 0):
                r = tb * 2048 + tt * 128
                gps.dma_start(out=V["x1_sh"][r:r + 128, hs], in_=V["s_ln"][:, :]) \
                    .then_inc(xs.h, xs.inc(16))
                if tb == 0:
                    gps.dma_start(out=V["x1own_dram"][tt * 128:(tt + 1) * 128, hs],
                                  in_=V["s_ln"][:, :]).then_inc(xs.h, xs.inc(16))
                else:
                    gps.dma_start(out=V["x1_sh"][r:r + 128, hs], in_=V["s_ln"][:, :]) \
                        .then_inc(xs.h, xs.inc(16))
            with gps.Else():
                r = 1024 + tb * 2048 + tt * 128
                gps.dma_start(out=V["x1_sh"][r:r + 128, hs], in_=V["s_ln"][:, :]) \
                    .then_inc(xs.h, 16)
                if tb == 0:
                    gps.dma_start(out=V["x1own_dram"][tt * 128:(tt + 1) * 128, hs],
                                  in_=V["s_ln"][:, :]).then_inc(xs.h, 16)
                else:
                    gps.dma_start(out=V["x1_sh"][r:r + 128, hs], in_=V["s_ln"][:, :]) \
                        .then_inc(xs.h, 16)
        gps.wait_ge(xs.h, xs.n)

    _fence(kb, block, V)
    _barrier(kb, block, V, 1)
    if stages <= 2:
        _dbg_out(kb, block, V, 2)
        return

    _qkv(kb, block, V)
    _barrier(kb, block, V, 2)
    if stages <= 3:
        _dbg_out(kb, block, V, 3)
        return
    _attn(kb, block, V)
    if stages <= 4:
        _dbg_out(kb, block, V, 4)
        return

    _ln3(kb, block, V)
    _barrier(kb, block, V, 3)
    _ffn(kb, block, V, "C")
    _barrier(kb, block, V, 4)
    _finish(kb, block, V)



_PROG = None
_PROG_KEY = None
LAST_RESULTS = None


def kernel(L0_attn_x, L0_attn_Wq, L0_attn_Wk, L0_attn_Wv, L0_attn_Wo,
           L0_ffn_g_codes, L0_ffn_g_absmax, L0_ffn_g_lora_a, L0_ffn_g_lora_b,
           L0_ffn_u_codes, L0_ffn_u_absmax, L0_ffn_u_lora_a, L0_ffn_u_lora_b,
           L0_ffn_d_codes, L0_ffn_d_absmax, L0_ffn_d_lora_a, L0_ffn_d_lora_b):
    global _PROG, _PROG_KEY, LAST_RESULTS
    import ml_dtypes
    bf16 = ml_dtypes.bfloat16
    rng = np.random.default_rng()
    nonce = int(rng.integers(1, 2**30))

    x = np.ascontiguousarray(np.asarray(L0_attn_x, dtype=np.float32))
    wqT = np.ascontiguousarray(np.asarray(L0_attn_Wq, np.float32).T)
    wkT = np.ascontiguousarray(np.asarray(L0_attn_Wk, np.float32).T)
    wvT = np.ascontiguousarray(np.asarray(L0_attn_Wv, np.float32).T)
    woT = np.ascontiguousarray(np.asarray(L0_attn_Wo, np.float32).T)
    gcT = np.ascontiguousarray(np.asarray(L0_ffn_g_codes, np.int32).T)
    ucT = np.ascontiguousarray(np.asarray(L0_ffn_u_codes, np.int32).T)
    dcT = np.ascontiguousarray(np.asarray(L0_ffn_d_codes, np.int32).T)
    gaT = np.ascontiguousarray(np.asarray(L0_ffn_g_absmax, np.float32).T)
    uaT = np.ascontiguousarray(np.asarray(L0_ffn_u_absmax, np.float32).T)
    daT = np.ascontiguousarray(np.asarray(L0_ffn_d_absmax, np.float32).T)
    gla = np.ascontiguousarray(np.asarray(L0_ffn_g_lora_a, np.float32))
    glb = np.ascontiguousarray(np.asarray(L0_ffn_g_lora_b, np.float32))
    ula = np.ascontiguousarray(np.asarray(L0_ffn_u_lora_a, np.float32))
    ulb = np.ascontiguousarray(np.asarray(L0_ffn_u_lora_b, np.float32))
    dla = np.ascontiguousarray(np.asarray(L0_ffn_d_lora_a, np.float32))
    dlb = np.ascontiguousarray(np.asarray(L0_ffn_d_lora_b, np.float32))

    cos, sin = _llama3_cos_sin()
    ident_b = np.eye(128, dtype=bf16)
    ident_f = np.eye(128, dtype=np.float32)

    stages = int(os.environ.get("ATH_STAGES", "9"))
    dbgf = os.environ.get("ATH_DBG", "0") == "1"
    if _PROG is None or _PROG_KEY != (stages, dbgf):
        _PROG = build_program(stages, dbgf)
        _PROG_KEY = (stages, dbgf)
    nc = _PROG

    in_maps = []
    for core in range(8):
        pair = core // 2
        member = core % 2
        batch = pair // 2
        pip = pair % 2
        rot = pip * 2048
        blk = 2 * pip + member

        perm = np.roll(np.arange(T), -rot)
        xb = np.ascontiguousarray(x[batch][perm])
        cosk4 = np.ascontiguousarray(np.tile(cos[perm], (1, 4)))
        sink4 = np.ascontiguousarray(np.tile(sin[perm], (1, 4)))
        qpos = np.arange(blk * QBLK, (blk + 1) * QBLK)
        cosq4 = np.ascontiguousarray(np.tile(cos[qpos] * ISQ, (1, 4)).astype(np.float32))
        sinq4 = np.ascontiguousarray(np.tile(sin[qpos] * ISQ, (1, 4)).astype(np.float32))
        mask01 = np.ascontiguousarray((qpos[:, None] >= perm[None, :]).astype(bf16))

        mh = slice(member * DH, (member + 1) * DH)
        kvs = slice(member * 512, (member + 1) * 512)

        in_maps.append({
            "x_b": xb,
            "g_codesT_h": np.ascontiguousarray(gcT[:, mh]),
            "u_codesT_h": np.ascontiguousarray(ucT[:, mh]),
            "d_codesT_h": np.ascontiguousarray(dcT[mh, :]),
            "g_absmaxT_h": np.ascontiguousarray(gaT[:, mh]),
            "u_absmaxT_h": np.ascontiguousarray(uaT[:, mh]),
            "d_absmaxT_h": np.ascontiguousarray(daT[member * 86:(member + 1) * 86, :]),
            "g_lora_a": gla, "u_lora_a": ula,
            "d_lora_a_h": np.ascontiguousarray(dla[mh, :]),
            "g_lora_b_h": np.ascontiguousarray(glb[:, mh]),
            "u_lora_b_h": np.ascontiguousarray(ulb[:, mh]),
            "d_lora_b": dlb,
            "wqT": wqT, "woT": woT,
            "wkT_h": np.ascontiguousarray(wkT[:, kvs]),
            "wvT_h": np.ascontiguousarray(wvT[:, kvs]),
            "cosk4": cosk4, "sink4": sink4, "cosq4": cosq4, "sinq4": sinq4,
            "mask01": mask01,
            "identb": ident_b, "identf": ident_f,
            "meta": np.array([[member, nonce, 0, 0, 0, 0, 0, 0]], np.int32),
        })

    res = run_bass_kernel_spmd(nc, in_maps, core_ids=list(range(8)))
    LAST_RESULTS = res
    out = np.empty((B, T, D), dtype=np.float32)
    for core in range(8):
        pair = core // 2
        member = core % 2
        batch = pair // 2
        pip = pair % 2
        blk = 2 * pip + member
        out[batch, blk * QBLK:(blk + 1) * QBLK, :] = res.results[core]["out_x3"]
    return out


# version 03
# speedup vs baseline: 1.2994x; 1.2994x over previous
import math
import os
import contextlib
import numpy as np
import concourse.bass as bass
import concourse.mybir as mybir
from concourse.bass_utils import run_bass_kernel_spmd

F32 = mybir.dt.float32
BF16 = mybir.dt.bfloat16
I32 = mybir.dt.int32
AF = mybir.ActivationFunctionType
ALU = mybir.AluOpType
AX = mybir.AxisListType

B, T, D = 2, 4096, 4096
DFF = 11008
DH = DFF // 2
NDH = DH // 128
NID = D // 128
R = 16
HD = 128
NH = 32
NKV = 8
KVW = NKV * HD
QBLK = 1024
EPS = 1e-5
ISQ = 1.0 / math.sqrt(HD)
CHK = 512
NTT = 4

NF4 = np.array([
    -1.0, -0.6961928009986877, -0.5250730514526367, -0.39491748809814453,
    -0.28444138169288635, -0.18477343022823334, -0.09105003625154495, 0.0,
    0.07958029955625534, 0.16093020141124725, 0.2461123019456863,
    0.33791524171829224, 0.44070982933044434, 0.5626170039176941,
    0.7229568362236023, 1.0], dtype=np.float64)


def _nf4_coeffs():
    E, O, Sq = [], [], []
    for k in range(8):
        hi, lo = NF4[8 + k], NF4[7 - k]
        u = (k + 0.5) / 7.5
        E.append((hi + lo) / 2.0)
        O.append((hi - lo) / (2.0 * u))
        Sq.append(u * u)
    cE = np.polynomial.polynomial.polyfit(np.array(Sq), np.array(E), 7)
    cO = np.polynomial.polynomial.polyfit(np.array(Sq), np.array(O), 7)
    return [float(x) for x in cE], [float(x) for x in cO]

CE, CO = _nf4_coeffs()


def _llama3_cos_sin():
    base = 500000.0
    inv_freq = 1.0 / base ** (np.arange(0, HD, 2, dtype=np.float32) / HD)
    factor, low_f, high_f, old_ctx = 32.0, 1.0, 4.0, 8192
    low_wl, high_wl = old_ctx / low_f, old_ctx / high_f
    wavelen = 2 * math.pi / inv_freq
    inv_llama = np.where(wavelen > low_wl, inv_freq / factor, inv_freq)
    smooth = (old_ctx / wavelen - low_f) / (high_f - low_f)
    smoothed = (1 - smooth) * inv_llama / factor + smooth * inv_llama
    is_med = ~(wavelen < high_wl) & ~(wavelen > low_wl)
    inv = np.where(is_med, smoothed, inv_llama).astype(np.float32)
    pos = np.arange(T, dtype=np.float32)
    freqs = pos[:, None] * inv[None, :]
    emb = np.concatenate([freqs, freqs], axis=-1)
    return np.cos(emb).astype(np.float32), np.sin(emb).astype(np.float32)


class Led:
    __slots__ = ("h", "n", "base", "fence_born")

    def __init__(self, h):
        self.h = h
        self.n = 0
        self.base = 0
        self.fence_born = -10

    def inc(self, k):
        self.n += k
        return k

    def t(self, c):
        return self.base + c


class KB:
    POOL = 92

    def __init__(self):
        self.nc = bass.Bass("TRN2", target_bir_lowering=False, debug=False)
        self.es = contextlib.ExitStack()
        self.pool = []
        self.cursor = 0
        self.nfence = 0
        self.ndedic = 0

    def new_sem(self):
        if len(self.pool) < self.POOL:
            led = Led(self.es.enter_context(self.nc.semaphore(f"s{len(self.pool)}")))
            self.pool.append(led)
        else:
            led = self.pool[self.cursor % self.POOL]
            assert self.nfence >= led.fence_born + 2, \
                f"semaphore recycled too soon ({self.nfence} vs {led.fence_born})"
        self.cursor += 1
        led.base = led.n
        led.fence_born = self.nfence
        return led

    def new_dedicated(self):
        led = Led(self.es.enter_context(self.nc.semaphore(f"d{self.ndedic}")))
        self.ndedic += 1
        return led


def _fence(kb, block, V):
    f = kb.new_sem()
    kb.nfence += 1
    for eng in ("sync", "tensor", "scalar", "vector", "gpsimd"):
        @getattr(block, eng)
        def _(e):
            e.drain()
            e.sem_inc(f.h, 1)
            e.wait_ge(f.h, f.t(5))
    f.n = f.base + 5


def _getreg(eng, V, name):
    regs = V.setdefault("_regs", {})
    if name not in regs:
        regs[name] = eng.alloc_register(name)
    return regs[name]


def _dsep(vector, V, n=2):
    for _ in range(n):
        vector.reduce_sum(out=V["s_stat"][:, 7:8], in_=V["s_lnb"][:, 0:1024],
                          axis=AX.X)


def _barrier(kb, block, V, slot):
    bar = kb.new_dedicated()

    @block.gpsimd
    def _(g):
        member = _getreg(g, V, "g_member")
        g.load(member, V["s_meta"][0:1, 0:1])
        with g.If_eq(member, 0):
            g.dma_start(out=V["flags_sh"][0:1, slot:slot + 1],
                        in_=V["s_meta"][0:1, 1:2]).then_inc(bar.h, bar.inc(16))
        with g.Else():
            g.dma_start(out=V["flags_sh"][1:2, slot:slot + 1],
                        in_=V["s_meta"][0:1, 1:2]).then_inc(bar.h, 16)
        g.wait_ge(bar.h, bar.t(16))
        nonce = _getreg(g, V, "g_nonce")
        g.load(nonce, V["s_meta"][0:1, 1:2])
        cnt = _getreg(g, V, "g_cnt")
        g.reg_mov(cnt, bar.t(16))
        val = _getreg(g, V, "g_val")
        go = _getreg(g, V, "g_go")
        g.reg_mov(go, 1)
        with g.While(go):
            with g.If_eq(member, 0):
                g.dma_start(out=V["s_flag"][0:1, slot:slot + 1],
                            in_=V["flags_sh"][1:2, slot:slot + 1]).then_inc(bar.h, 16)
            with g.Else():
                g.dma_start(out=V["s_flag"][0:1, slot:slot + 1],
                            in_=V["flags_sh"][0:1, slot:slot + 1]).then_inc(bar.h, 16)
            g.reg_add(cnt, cnt, 16)
            g.wait_ge(bar.h, cnt)
            g.load(val, V["s_flag"][0:1, slot:slot + 1])
            g.reg_alu(go, val, nonce, ALU.not_equal)

    _fence(kb, block, V)


def build_program(stages=9, dbg=False):
    kb = KB()
    nc = kb.nc
    es = kb.es

    def din(name, shape, dt=F32):
        return nc.dram_tensor(name, shape, dt, kind="ExternalInput").ap()

    def dint(name, shape, dt, shared=False):
        if shared:
            return nc.dram_tensor(name, shape, dt, addr_space="Shared").ap()
        return nc.dram_tensor(name, shape, dt).ap()

    x_in = din("x_b", [T, D])
    codes_in = {"g": din("g_codesT_h", [D, DH], I32),
                "u": din("u_codesT_h", [D, DH], I32),
                "d": din("d_codesT_h", [DH, D], I32)}
    am_in = {"g": din("g_absmaxT_h", [64, DH]),
             "u": din("u_absmaxT_h", [64, DH]),
             "d": din("d_absmaxT_h", [86, D])}
    la_in = {"g": din("g_lora_a", [D, R]),
             "u": din("u_lora_a", [D, R]),
             "d": din("d_lora_a_h", [DH, R])}
    lb_in = {"g": din("g_lora_b_h", [R, DH]),
             "u": din("u_lora_b_h", [R, DH]),
             "d": din("d_lora_b", [R, D])}
    wq_in = din("wqT", [D, D])
    wk_in = din("wkT_h", [D, 512])
    wv_in = din("wvT_h", [D, 512])
    wo_in = din("woT", [D, D])
    cosk_in = din("cosk4", [T, 512])
    sink_in = din("sink4", [T, 512])
    cosq_in = din("cosq4", [QBLK, 512])
    sinq_in = din("sinq4", [QBLK, 512])
    mask_in = din("mask01", [QBLK, T], BF16)
    identb_in = din("identb", [128, 128], BF16)
    identf_in = din("identf", [128, 128], F32)
    meta_in = din("meta", [1, 8], I32)

    out_x3 = nc.dram_tensor("out_x3", [QBLK, D], F32, kind="ExternalOutput").ap()
    if dbg:
        dbg_w = nc.dram_tensor("dbg_w", [256, DH], BF16, kind="ExternalOutput").ap()
        dbg_x1 = nc.dram_tensor("dbg_x1", [T, D], F32, kind="ExternalOutput").ap()
        dbg_pa = [nc.dram_tensor(f"dbg_pa{m}", [T, D], F32, kind="ExternalOutput").ap()
                  for m in range(2)]
        dbg_ab1 = nc.dram_tensor("dbg_ab1", [128, NDH * CHK], BF16, kind="ExternalOutput").ap()
        dbg_ab2 = nc.dram_tensor("dbg_ab2", [128, NID * CHK], BF16, kind="ExternalOutput").ap()
        dbg_P = nc.dram_tensor("dbg_P", [128, 16384], BF16, kind="ExternalOutput").ap()
        dbg_mn = nc.dram_tensor("dbg_mn", [128, 48], F32, kind="ExternalOutput").ap()
        dbg_k = nc.dram_tensor("dbg_k", [KVW, T], BF16, kind="ExternalOutput").ap()
        dbg_v = nc.dram_tensor("dbg_v", [T, KVW], BF16, kind="ExternalOutput").ap()
        dbg_q = nc.dram_tensor("dbg_q", [D, QBLK], BF16, kind="ExternalOutput").ap()
        dbg_x2 = nc.dram_tensor("dbg_x2", [QBLK, D], F32, kind="ExternalOutput").ap()

    w_cache = {"g": dint("wg_cache", [D, DH], BF16),
               "u": dint("wu_cache", [D, DH], BF16),
               "d": dint("wd_cache", [DH, D], BF16)}
    lb_cache = dint("lb_cache", [16, 2 * DH + D], BF16)
    wkv_cache = dint("wkv_cache", [D, 1024], BF16)
    wq_cache = dint("wq_cache", [D, D], BF16)
    wo_cache = dint("wo_cache", [D, D], BF16)
    q_fm = dint("q_fm", [D, QBLK], BF16)
    x1own_dram = dint("x1own", [QBLK, D], F32)
    x2_dram = dint("x2_dram", [QBLK, D], F32)

    pa_sh = [dint(f"pa_sh{m}", [T, D], F32, shared=True) for m in range(2)]
    x1_sh = dint("x1_sh", [T, D], F32, shared=True)
    k_sh = dint("k_sh", [KVW, T], BF16, shared=True)
    v_sh = dint("v_sh", [T, KVW], BF16, shared=True)
    ln3_sh = dint("ln3_sh", [4 * NID * 128, CHK], BF16, shared=True)
    pc_sh = [dint(f"pc_sh{m}", [2 * QBLK, D], F32, shared=True) for m in range(2)]
    flags_sh = dint("flags_sh", [2, 16], I32, shared=True)

    def sb(name, shape, dt):
        return es.enter_context(nc.sbuf_tensor(name, shape, dt))

    ab1 = sb("ab1", [128, NDH * CHK], BF16)
    ab2 = sb("ab2", [128, NID * CHK], BF16)
    s_codes = [sb(f"s_codes{i}", [128, 512], I32) for i in range(2)]
    s_du = sb("s_du", [128, 512], F32)
    s_ds = sb("s_ds", [128, 512], F32)
    s_dz = sb("s_dz", [128, 512], F32)
    s_pe = [sb(f"s_pe{i}", [128, 512], F32) for i in range(4)]
    s_po = [sb(f"s_po{i}", [128, 512], F32) for i in range(4)]
    s_te = sb("s_te", [128, 512], F32)
    s_to = sb("s_to", [128, 512], F32)
    s_wout = [sb(f"s_wout{i}", [128, 512], BF16) for i in range(2)]
    s_arow = [sb(f"s_arow{i}", [1, 512], F32) for i in range(2)]
    s_brow = [sb(f"s_brow{i}", [1, 512], F32) for i in range(2)]
    s_onesf = sb("s_onesf", [1, 64], F32)
    s_xt = sb("s_xt", [128, 2048], F32)
    s_p0 = sb("s_p0", [128, 2048], F32)
    s_p1 = sb("s_p1", [128, 2048], F32)
    s_ln = sb("s_ln", [128, 2048], F32)
    s_lnb = sb("s_lnb", [128, D], BF16)
    s_stat = sb("s_stat", [128, 8], F32)
    s_wg = [sb(f"s_wg{i}", [128, 512], BF16) for i in range(2)]
    s_wb = [sb(f"s_wb{i}", [128, 512], BF16) for i in range(2)]
    s_lbt = [sb(f"s_lbt{i}", [16, 512], BF16) for i in range(2)]
    s_y1 = sb("s_y1", [128, 2048], BF16)
    s_ev = [sb(f"s_ev{i}", [128, 512], F32) for i in range(2)]
    s_ptm = sb("s_ptm", [128, 2048], F32)
    s_la = sb("s_la", [128, (2 * NID + NDH) * R], BF16)
    s_t1 = {"g": sb("s_t1g", [16, 512], BF16),
            "u": sb("s_t1u", [16, 512], BF16),
            "d": sb("s_t1d", [16, 512], BF16)}
    s_rot = sb("s_rot", [128, 512], F32)
    s_ra = sb("s_ra", [128, 512], F32)
    s_kfm = sb("s_kfm", [128, T], BF16)
    s_qh = sb("s_qh", [128, CHK], BF16)
    s_ident = sb("s_ident", [128, 128], BF16)
    s_identf = sb("s_identf", [128, 128], F32)
    s_pt = sb("s_pt", [128, CHK], BF16)
    s_mask = [sb(f"s_mask{i}", [128, 2048], BF16) for i in range(2)]
    s_mneg = sb("s_mneg", [128, 40], F32)
    s_sig = sb("s_sig", [128, 8], F32)
    s_rcp = sb("s_rcp", [128, 8], F32)
    s_flag = sb("s_flag", [1, 8], I32)
    s_meta = sb("s_meta", [1, 8], I32)

    def ps(name, shape, dt):
        return es.enter_context(nc.psum_tensor(name, shape, dt))

    p_mm = [ps(f"p_mm{i}", [128, 512], F32) for i in range(4)]
    p_ma = [ps(f"p_ma{i}", [128, 512], F32) for i in range(2)]
    p_tr = ps("p_tr", [128, 512], F32)
    p_trb = ps("p_trb", [128, 1024], BF16)

    deq_tiles = []
    for mat in ("g", "u", "d"):
        nit = NID if mat != "d" else NDH
        width = DH if mat != "d" else D
        for it in range(nit):
            for oc in range((width + 511) // 512):
                deq_tiles.append((mat, it, oc, min(512, width - oc * 512)))

    V = dict(locals())
    V["dbg"] = dbg

    with es:
        with nc.Block() as block:
            _build_all(kb, block, V, stages, dbg)
    return nc



def _ln_to_fm(kb, block, V, src_fn, n_tt):
    ab2 = V["ab2"]
    lx = kb.new_sem()
    ctr = kb.new_sem()
    acc = kb.new_sem()
    tT = kb.new_sem()
    fm = kb.new_sem()

    @block.sync
    def _(sync):
        for tt in range(n_tt):
            for half in range(2):
                if half == 0 and tt > 0:
                    sync.wait_ge(ctr.h, ctr.t(5 * tt - 2))
                sync.dma_start(out=(V["s_xt"] if half == 0 else V["s_p0"])[:, :],
                               in_=src_fn(tt, half)).then_inc(lx.h, lx.inc(16))

    @block.scalar
    def _(scalar):
        for tt in range(n_tt):
            scalar.wait_ge(ctr.h, ctr.t(5 * tt + 2))
            scalar.activation(out=V["s_xt"][:, :], in_=V["s_ln"][:, :],
                              func=AF.Square, accum_out=V["s_stat"][:, 2:3])
            scalar.activation(out=V["s_p0"][:, :], in_=V["s_p1"][:, :],
                              func=AF.Square, accum_out=V["s_stat"][:, 3:4]) \
                .then_inc(acc.h, acc.inc(1))
            scalar.wait_ge(ctr.h, ctr.t(5 * tt + 3))
            scalar.activation(out=V["s_stat"][:, 5:6], in_=V["s_stat"][:, 4:5],
                              func=AF.Ln)
            scalar.activation(out=V["s_y1"][:, 0:512], in_=V["s_y1"][:, 0:512],
                              func=AF.Copy)
            scalar.activation(out=V["s_stat"][:, 6:7], in_=V["s_stat"][:, 5:6],
                              func=AF.Exp, scale=-0.5).then_inc(acc.h, acc.inc(1))

    @block.tensor
    def _(tensor):
        for tt in range(n_tt):
            tensor.wait_ge(ctr.h, ctr.t(5 * tt + 5))
            for ig in range(4):
                g = tt * 4 + ig
                if g >= 1:
                    tensor.wait_ge(fm.h, fm.t(g))
                for i8 in range(8):
                    i = ig * 8 + i8
                    ins = tensor.transpose(out=V["p_trb"][:, i8 * 128:(i8 + 1) * 128],
                                           in_=V["s_lnb"][:, i * 128:(i + 1) * 128],
                                           identity=V["s_ident"][:, :])
                ins.then_inc(tT.h, tT.inc(1))

    @block.vector
    def _(vector):
        for tt in range(n_tt):
            vector.wait_ge(lx.h, lx.t((2 * tt + 2) * 16))
            vector.reduce_sum(out=V["s_stat"][:, 0:1], in_=V["s_xt"][:, :], axis=AX.X)
            vector.reduce_sum(out=V["s_stat"][:, 1:2], in_=V["s_p0"][:, :], axis=AX.X)
            _dsep(vector, V)
            vector.tensor_add(out=V["s_stat"][:, 0:1], in0=V["s_stat"][:, 0:1],
                              in1=V["s_stat"][:, 1:2])
            _dsep(vector, V)
            vector.tensor_scalar(out=V["s_stat"][:, 1:2], in0=V["s_stat"][:, 0:1],
                                 scalar1=1.0 / D, scalar2=None, op0=ALU.mult)
            _dsep(vector, V)
            vector.tensor_scalar(out=V["s_ln"][:, :], in0=V["s_xt"][:, :],
                                 scalar1=V["s_stat"][:, 1:2], scalar2=None,
                                 op0=ALU.subtract).then_inc(ctr.h, ctr.inc(1))
            vector.tensor_scalar(out=V["s_p1"][:, :], in0=V["s_p0"][:, :],
                                 scalar1=V["s_stat"][:, 1:2], scalar2=None,
                                 op0=ALU.subtract).then_inc(ctr.h, ctr.inc(1))
            vector.wait_ge(acc.h, acc.t(2 * tt + 1))
            vector.tensor_add(out=V["s_stat"][:, 4:5], in0=V["s_stat"][:, 2:3],
                              in1=V["s_stat"][:, 3:4])
            _dsep(vector, V)
            vector.tensor_scalar(out=V["s_stat"][:, 4:5], in0=V["s_stat"][:, 4:5],
                                 scalar1=1.0 / D, scalar2=EPS,
                                 op0=ALU.mult, op1=ALU.add).then_inc(ctr.h, ctr.inc(1))
            vector.wait_ge(acc.h, acc.t(2 * tt + 2))
            if tt > 0:
                vector.wait_ge(tT.h, tT.t(tt * 4))
            vector.tensor_scalar(out=V["s_lnb"][:, 0:2048], in0=V["s_ln"][:, :],
                                 scalar1=V["s_stat"][:, 6:7], scalar2=None,
                                 op0=ALU.mult).then_inc(ctr.h, ctr.inc(1))
            vector.tensor_scalar(out=V["s_lnb"][:, 2048:4096], in0=V["s_p1"][:, :],
                                 scalar1=V["s_stat"][:, 6:7], scalar2=None,
                                 op0=ALU.mult).then_inc(ctr.h, ctr.inc(1))
            for ig in range(4):
                vector.wait_ge(tT.h, tT.t(tt * 4 + ig + 1))
                vector.tensor_copy(
                    out=ab2[:, :].rearrange("p (i t) -> p i t", t=CHK)
                        [:, ig * 8:(ig + 1) * 8, tt * 128:(tt + 1) * 128],
                    in_=V["p_trb"][:, :].rearrange("p (a b) -> p a b", b=128)) \
                    .then_inc(fm.h, fm.inc(1))

    return fm



def _ffn(kb, block, V, phase):
    ab1, ab2 = V["ab1"], V["ab2"]
    p_mm, p_tr = V["p_mm"], V["p_tr"]
    ntc = 8 if phase == "A" else 4
    psh = V["pa_sh"] if phase == "A" else V["pc_sh"]
    NOC = 11
    ow = [512] * 10 + [384]
    NDC = 8

    for tc in range(ntc):
        t0 = tc * CHK
        if phase == "A":
            fm = _ln_to_fm(kb, block, V,
                           lambda tt, half, t0=t0: V["x_in"][t0 + tt * 128:t0 + (tt + 1) * 128,
                                                             half * 2048:(half + 1) * 2048],
                           NTT)
            fm_target = 4 * NTT
        else:
            fm = kb.new_sem()

            @block.sync
            def _(sync, tc=tc):
                for g8 in range(8):
                    sync.dma_start(
                        out=ab2[:, g8 * 2048:(g8 + 1) * 2048]
                            .rearrange("p (i t) -> p i t", t=CHK),
                        in_=V["ln3_sh"][(tc * NID + g8 * 4) * 128:(tc * NID + (g8 + 1) * 4) * 128, :]
                            .rearrange("(i p) t -> p i t", p=128)) \
                        .then_inc(fm.h, fm.inc(16))
            fm_target = 8 * 16
        _fence(kb, block, V)

        lt = kb.new_sem()
        le = kb.new_sem()
        lw = [kb.new_sem() for _ in range(4)]
        lbl = [kb.new_sem(), kb.new_sem()]
        WB4 = [V["s_wg"][0], V["s_wg"][1], V["s_wb"][0], V["s_wb"][1]]
        mmi = kb.new_sem()
        grp = kb.new_sem()
        rel = kb.new_sem()
        x3s = kb.new_sem()

        @block.sync
        def _(sync):
            k = 0
            for oc in range(NOC):
                for mj, mat in enumerate(("g", "u")):
                    kb2 = 2 * oc + mj
                    if kb2 >= 2:
                        sync.wait_ge(grp.h, grp.t(kb2 - 1))
                    lbo = 0 if mat == "g" else DH
                    sync.dma_start(out=V["s_lbt"][kb2 % 2][0:16, 0:ow[oc]],
                                   in_=V["lb_cache"][0:16, lbo + oc * 512:lbo + oc * 512 + ow[oc]]) \
                        .then_inc(lbl[kb2 % 2].h, lbl[kb2 % 2].inc(16))
                    for i in range(NID):
                        if k % 2 == 0:
                            if k >= 4:
                                sync.wait_ge(mmi.h, mmi.t(k - 3))
                            sync.dma_start(out=WB4[k % 4][:, 0:ow[oc]],
                                           in_=V["w_cache"][mat][i * 128:(i + 1) * 128,
                                                                 oc * 512:oc * 512 + ow[oc]]) \
                                .then_inc(lw[k % 4].h, lw[k % 4].inc(16))
                        k += 1

        @block.tensor
        def _(tensor):
            tensor.wait_ge(fm.h, fm.t(fm_target))
            for j, mat in enumerate(("g", "u")):
                for i in range(NID):
                    ins = tensor.matmul(out=p_mm[j][0:16, :],
                                  lhsT=V["s_la"][:, (j * NID + i) * R:(j * NID + i + 1) * R],
                                  rhs=ab2[:, i * CHK:(i + 1) * CHK],
                                  start=(i == 0), stop=(i == NID - 1))
                ins.then_inc(lt.h, lt.inc(1))
            tensor.wait_ge(le.h, le.t(2))
            k = 0
            for oc in range(NOC):
                nsub = (ow[oc] + 127) // 128
                for mj, mat in enumerate(("g", "u")):
                    kb2 = 2 * oc + mj
                    if mj == 0 and oc >= 1:
                        tensor.wait_ge(x3s.h, x3s.t(oc))
                    for i in range(NID):
                        tensor.wait_ge(lw[k % 4].h, lw[k % 4].t((k // 4 + 1) * 16))
                        for sub in range(nsub):
                            ins = tensor.matmul(out=p_mm[sub][:, :],
                                          lhsT=WB4[k % 4][:, sub * 128:(sub + 1) * 128],
                                          rhs=ab2[:, i * CHK:(i + 1) * CHK],
                                          start=(i == 0), stop=False)
                        ins.then_inc(mmi.h, mmi.inc(1))
                        k += 1
                    tensor.wait_ge(lbl[kb2 % 2].h, lbl[kb2 % 2].t((kb2 // 2 + 1) * 16))
                    for sub in range(nsub):
                        ins = tensor.matmul(out=p_mm[sub][:, :],
                                      lhsT=V["s_lbt"][kb2 % 2][0:16, sub * 128:(sub + 1) * 128],
                                      rhs=V["s_t1"][mat][0:16, :],
                                      start=False, stop=True)
                    ins.then_inc(grp.h, grp.inc(1))
                    if mj == 0:
                        tensor.wait_ge(rel.h, rel.t(oc + 1))

        @block.scalar
        def _(scalar):
            k = 0
            for oc in range(NOC):
                nsub = (ow[oc] + 127) // 128
                for mj, mat in enumerate(("g", "u")):
                    for i in range(NID):
                        if k % 2 == 1:
                            if k >= 4:
                                scalar.wait_ge(mmi.h, mmi.t(k - 3))
                            scalar.dma_start(out=WB4[k % 4][:, 0:ow[oc]],
                                             in_=V["w_cache"][mat][i * 128:(i + 1) * 128,
                                                                   oc * 512:oc * 512 + ow[oc]]) \
                                .then_inc(lw[k % 4].h, lw[k % 4].inc(16))
                        k += 1
                    if mj == 0:
                        scalar.wait_ge(grp.h, grp.t(2 * oc + 1))
                        if oc >= 1:
                            scalar.wait_ge(x3s.h, x3s.t(oc))
                        for sub in range(nsub):
                            ins = scalar.activation(out=V["s_y1"][:, sub * 512:sub * 512 + 512],
                                                    in_=p_mm[sub][:, :], func=AF.Relu)
                        ins.then_inc(rel.h, rel.inc(1))

        @block.vector
        def _(vector):
            for j, mat in enumerate(("g", "u")):
                vector.wait_ge(lt.h, lt.t(j + 1))
                vector.tensor_copy(out=V["s_t1"][mat][0:16, :],
                                   in_=p_mm[j][0:16, :]).then_inc(le.h, le.inc(1))
            for oc in range(NOC):
                nsub = (ow[oc] + 127) // 128
                vector.wait_ge(grp.h, grp.t(2 * oc + 2))
                vector.wait_ge(rel.h, rel.t(oc + 1))
                for sub in range(nsub):
                    ot = oc * 4 + sub
                    ins = vector.tensor_mul(out=ab1[:, ot * CHK:(ot + 1) * CHK],
                                            in0=V["s_y1"][:, sub * 512:sub * 512 + 512],
                                            in1=p_mm[sub][:, :])
                ins.then_inc(x3s.h, x3s.inc(1))

        ltd = kb.new_sem()
        led = kb.new_sem()
        lwd = [kb.new_sem() for _ in range(4)]
        lbd = [kb.new_sem(), kb.new_sem()]
        mdi = kb.new_sem()
        mdd = kb.new_sem()
        evd = kb.new_sem()
        trd = kb.new_sem()
        ptd = kb.new_sem()
        std = kb.new_sem()

        @block.sync
        def _(sync):
            k = 0
            for dc in range(NDC):
                if dc < 2:
                    sync.wait_ge(grp.h, grp.t(2 * NOC))
                if dc >= 2:
                    sync.wait_ge(mdd.h, mdd.t(dc - 1))
                sync.dma_start(out=V["s_lbt"][dc % 2][0:16, :],
                               in_=V["lb_cache"][0:16, 2 * DH + dc * 512:2 * DH + (dc + 1) * 512]) \
                    .then_inc(lbd[dc % 2].h, lbd[dc % 2].inc(16))
                for i in range(NDH):
                    if k % 2 == 0:
                        if k >= 4:
                            sync.wait_ge(mdi.h, mdi.t(k - 3))
                        sync.dma_start(out=WB4[k % 4][:, :],
                                       in_=V["w_cache"]["d"][i * 128:(i + 1) * 128,
                                                             dc * 512:(dc + 1) * 512]) \
                            .then_inc(lwd[k % 4].h, lwd[k % 4].inc(16))
                    k += 1

        @block.scalar
        def _(scalar):
            scalar.wait_ge(grp.h, grp.t(2 * NOC))
            k = 0
            for dc in range(NDC):
                for i in range(NDH):
                    if k % 2 == 1:
                        if k >= 4:
                            scalar.wait_ge(mdi.h, mdi.t(k - 3))
                        scalar.dma_start(out=WB4[k % 4][:, :],
                                         in_=V["w_cache"]["d"][i * 128:(i + 1) * 128,
                                                               dc * 512:(dc + 1) * 512]) \
                            .then_inc(lwd[k % 4].h, lwd[k % 4].inc(16))
                    k += 1

        @block.tensor
        def _(tensor):
            tensor.wait_ge(x3s.h, x3s.t(NOC))
            for i in range(NDH):
                ins = tensor.matmul(out=p_mm[0][0:16, :],
                              lhsT=V["s_la"][:, (2 * NID + i) * R:(2 * NID + i + 1) * R],
                              rhs=ab1[:, i * CHK:(i + 1) * CHK],
                              start=(i == 0), stop=(i == NDH - 1))
            ins.then_inc(ltd.h, ltd.inc(1))
            tensor.wait_ge(led.h, led.t(1))
            k = 0
            for dc in range(NDC):
                if dc >= 1:
                    tensor.wait_ge(evd.h, evd.t(4 * dc))
                for i in range(NDH):
                    tensor.wait_ge(lwd[k % 4].h, lwd[k % 4].t((k // 4 + 1) * 16))
                    for sub in range(4):
                        ins = tensor.matmul(out=p_mm[sub][:, :],
                                      lhsT=WB4[k % 4][:, sub * 128:(sub + 1) * 128],
                                      rhs=ab1[:, i * CHK:(i + 1) * CHK],
                                      start=(i == 0), stop=False)
                    ins.then_inc(mdi.h, mdi.inc(1))
                    k += 1
                tensor.wait_ge(lbd[dc % 2].h, lbd[dc % 2].t((dc // 2 + 1) * 16))
                for sub in range(4):
                    ins = tensor.matmul(out=p_mm[sub][:, :],
                                  lhsT=V["s_lbt"][dc % 2][0:16, sub * 128:(sub + 1) * 128],
                                  rhs=V["s_t1"]["d"][0:16, :],
                                  start=False, stop=True)
                ins.then_inc(mdd.h, mdd.inc(1))
                for sub in range(4):
                    dt = 4 * dc + sub
                    tensor.wait_ge(evd.h, evd.t(dt + 1))
                    if dt >= 1:
                        tensor.wait_ge(ptd.h, ptd.t(dt))
                    for j in range(4):
                        ins = tensor.transpose(out=p_tr[:, j * 128:(j + 1) * 128],
                                         in_=V["s_ev"][sub % 2][:, j * 128:(j + 1) * 128],
                                         identity=V["s_identf"][:, :])
                    ins.then_inc(trd.h, trd.inc(1))

        @block.vector
        def _(vector):
            vector.wait_ge(ltd.h, ltd.t(1))
            vector.tensor_copy(out=V["s_t1"]["d"][0:16, :],
                               in_=p_mm[0][0:16, :]).then_inc(led.h, led.inc(1))
            for dc in range(NDC):
                vector.wait_ge(mdd.h, mdd.t(dc + 1))
                for sub in range(4):
                    dt = 4 * dc + sub
                    if dt >= 2:
                        vector.wait_ge(trd.h, trd.t(dt - 1))
                    vector.tensor_copy(out=V["s_ev"][sub % 2][:, :],
                                       in_=p_mm[sub][:, :]).then_inc(evd.h, evd.inc(1))
                    vector.wait_ge(trd.h, trd.t(dt + 1))
                    if dc >= 1 and sub == 0:
                        vector.wait_ge(std.h, std.t(dc * 64))
                    vector.tensor_copy(
                        out=V["s_ptm"][:, :].rearrange("p (j s c) -> p j s c", s=4, c=128)
                            [:, :, sub, :],
                        in_=p_tr[:, :].rearrange("p (j c) -> p j c", c=128)) \
                        .then_inc(ptd.h, ptd.inc(1))

        @block.gpsimd
        def _(gps, tc=tc, t0=t0):
            member = _getreg(gps, V, "g_member")
            gps.load(member, V["s_meta"][0:1, 0:1])
            for dc in range(NDC):
                gps.wait_ge(ptd.h, ptd.t(4 * dc + 4))
                for j in range(4):
                    row = t0 + j * 128
                    src = V["s_ptm"][:, j * 512:(j + 1) * 512]
                    with gps.If_eq(member, 0):
                        gps.dma_start(out=psh[0][row:row + 128, dc * 512:(dc + 1) * 512],
                                      in_=src).then_inc(std.h, std.inc(16))
                    with gps.Else():
                        gps.dma_start(out=psh[1][row:row + 128, dc * 512:(dc + 1) * 512],
                                      in_=src).then_inc(std.h, 16)
            gps.wait_ge(std.h, std.n)
            if V.get("dbg") and phase == "A" and tc == 0:
                gps.dma_start(out=V["dbg_ab1"], in_=ab1[:, :]).then_inc(std.h, std.inc(16))
                gps.dma_start(out=V["dbg_ab2"], in_=ab2[:, :]).then_inc(std.h, std.inc(16))
                gps.wait_ge(std.h, std.n)

        _fence(kb, block, V)



def _qkv(kb, block, V):
    ab2 = V["ab2"]
    p_mm = V["p_mm"]

    for tc in range(8):
        t0 = tc * CHK
        fm = _ln_to_fm(kb, block, V,
                       lambda tt, half, t0=t0: V["x1_sh"][t0 + tt * 128:t0 + (tt + 1) * 128,
                                                          half * 2048:(half + 1) * 2048],
                       NTT)
        _fence(kb, block, V)

        WB4 = [V["s_wg"][0], V["s_wg"][1], V["s_wb"][0], V["s_wb"][1]]
        for j in range(2):
            lw = [kb.new_sem() for _ in range(4)]
            mmi = kb.new_sem()
            kro = kb.new_sem()
            kfe = kb.new_sem()
            ktT = kb.new_sem()
            stk = kb.new_sem()
            csl = kb.new_sem()

            @block.sync
            def _(sync, j=j, t0=t0):
                if j == 0:
                    sync.dma_start(out=V["s_xt"][:, :].rearrange("p (a c) -> p a c", c=512),
                                   in_=V["cosk_in"][t0:t0 + 512, :]
                                   .rearrange("(a p) c -> p a c", p=128)) \
                        .then_inc(csl.h, csl.inc(16))
                    sync.dma_start(out=V["s_p0"][:, :].rearrange("p (a c) -> p a c", c=512),
                                   in_=V["sink_in"][t0:t0 + 512, :]
                                   .rearrange("(a p) c -> p a c", p=128)) \
                        .then_inc(csl.h, csl.inc(16))
                k = 0
                for ttp in range(2):
                    for i in range(NID):
                        if k % 2 == 0:
                            if k >= 4:
                                sync.wait_ge(mmi.h, mmi.t(k - 3))
                            sync.dma_start(out=WB4[k % 4][:, :],
                                           in_=V["wkv_cache"][i * 128:(i + 1) * 128,
                                                              j * 512:(j + 1) * 512]) \
                                .then_inc(lw[k % 4].h, lw[k % 4].inc(16))
                        k += 1

            @block.scalar
            def _(scalar, j=j):
                k = 0
                for ttp in range(2):
                    for i in range(NID):
                        if k % 2 == 1:
                            if k >= 4:
                                scalar.wait_ge(mmi.h, mmi.t(k - 3))
                            scalar.dma_start(out=WB4[k % 4][:, :],
                                             in_=V["wkv_cache"][i * 128:(i + 1) * 128,
                                                                j * 512:(j + 1) * 512]) \
                                .then_inc(lw[k % 4].h, lw[k % 4].inc(16))
                        k += 1

            @block.tensor
            def _(tensor, j=j):
                tensor.wait_ge(fm.h, fm.t(4 * NTT))
                k = 0
                for ttp in range(2):
                    if ttp == 1:
                        tensor.wait_ge(kro.h, kro.t(2))
                    for i in range(NID):
                        tensor.wait_ge(lw[k % 4].h, lw[k % 4].t((k // 4 + 1) * 16))
                        for h2 in range(2):
                            tt = ttp * 2 + h2
                            ins = tensor.matmul(out=p_mm[h2][:, :],
                                          lhsT=ab2[:, i * CHK + tt * 128:i * CHK + (tt + 1) * 128],
                                          rhs=WB4[k % 4][:, :],
                                          start=(i == 0), stop=(i == NID - 1))
                        ins.then_inc(mmi.h, mmi.inc(1))
                        k += 1
                    if j == 0:
                        for h2 in range(2):
                            tt = ttp * 2 + h2
                            tensor.wait_ge(kro.h, kro.t(tt + 1))
                            if tt >= 1:
                                tensor.wait_ge(kfe.h, kfe.t(tt))
                            for sub in range(4):
                                ins = tensor.transpose(out=V["p_trb"][:, sub * 128:(sub + 1) * 128],
                                                 in_=V["s_lnb"][:, sub * 128:(sub + 1) * 128],
                                                 identity=V["s_ident"][:, :])
                            ins.then_inc(ktT.h, ktT.inc(1))

            @block.vector
            def _(vector, j=j):
                if j == 0:
                    vector.wait_ge(csl.h, csl.t(32))
                for ttp in range(2):
                    vector.wait_ge(mmi.h, mmi.t((ttp + 1) * 32))
                    for h2 in range(2):
                        tt = ttp * 2 + h2
                        if j == 1:
                            if tt >= 1:
                                vector.wait_ge(stk.h, stk.t(tt * 16))
                            vector.tensor_copy(out=V["s_lnb"][:, 0:512],
                                               in_=p_mm[h2][:, :]) \
                                .then_inc(kro.h, kro.inc(1))
                        else:
                            pm3 = p_mm[h2][:, :].rearrange("p (a c) -> p a c", c=128)
                            rt3 = V["s_rot"][:, :].rearrange("p (a c) -> p a c", c=128)
                            vector.tensor_scalar(out=rt3[:, :, 0:64], in0=pm3[:, :, 64:128],
                                                 scalar1=-1.0, scalar2=None, op0=ALU.mult)
                            vector.tensor_copy(out=rt3[:, :, 64:128], in_=pm3[:, :, 0:64])
                            vector.tensor_mul(out=V["s_ra"][:, :], in0=p_mm[h2][:, :],
                                              in1=V["s_xt"][:, tt * 512:(tt + 1) * 512])
                            vector.tensor_mul(out=V["s_rot"][:, :], in0=V["s_rot"][:, :],
                                              in1=V["s_p0"][:, tt * 512:(tt + 1) * 512])
                            if tt >= 1:
                                vector.wait_ge(ktT.h, ktT.t(tt))
                            vector.tensor_add(out=V["s_lnb"][:, 0:512], in0=V["s_ra"][:, :],
                                              in1=V["s_rot"][:, :]).then_inc(kro.h, kro.inc(1))
                            vector.wait_ge(ktT.h, ktT.t(tt + 1))
                            vector.tensor_copy(
                                out=V["s_kfm"][:, 0:2048].rearrange("p (s t) -> p s t", t=512)
                                    [:, :, tt * 128:(tt + 1) * 128],
                                in_=V["p_trb"][:, 0:512].rearrange("p (s c) -> p s c", c=128)) \
                                .then_inc(kfe.h, kfe.inc(1))

            @block.gpsimd
            def _(gps, j=j, t0=t0):
                gmem = _getreg(gps, V, "g_member")
                gps.load(gmem, V["s_meta"][0:1, 0:1])
                if j == 0:
                    gps.wait_ge(kfe.h, kfe.t(4))
                    for sub in range(4):
                        with gps.If_eq(gmem, 0):
                            gps.dma_start(out=V["k_sh"][sub * 128:(sub + 1) * 128, t0:t0 + 512],
                                          in_=V["s_kfm"][:, sub * 512:(sub + 1) * 512]) \
                                .then_inc(stk.h, stk.inc(16))
                        with gps.Else():
                            gps.dma_start(out=V["k_sh"][512 + sub * 128:512 + (sub + 1) * 128,
                                                        t0:t0 + 512],
                                          in_=V["s_kfm"][:, sub * 512:(sub + 1) * 512]) \
                                .then_inc(stk.h, 16)
                else:
                    for tt in range(NTT):
                        gps.wait_ge(kro.h, kro.t(tt + 1))
                        r = t0 + tt * 128
                        with gps.If_eq(gmem, 0):
                            gps.dma_start(out=V["v_sh"][r:r + 128, 0:512],
                                          in_=V["s_lnb"][:, 0:512]).then_inc(stk.h, stk.inc(16))
                        with gps.Else():
                            gps.dma_start(out=V["v_sh"][r:r + 128, 512:1024],
                                          in_=V["s_lnb"][:, 0:512]).then_inc(stk.h, 16)
                gps.wait_ge(stk.h, stk.n)

            _fence(kb, block, V)

    for qc2 in range(2):
        q0 = qc2 * CHK
        fmq = _ln_to_fm(kb, block, V,
                        lambda tt, half, q0=q0: V["x1own_dram"][q0 + tt * 128:q0 + (tt + 1) * 128,
                                                                half * 2048:(half + 1) * 2048],
                        NTT)
        _fence(kb, block, V)

        lwq = [kb.new_sem() for _ in range(4)]
        WB4 = [V["s_wg"][0], V["s_wg"][1], V["s_wb"][0], V["s_wb"][1]]
        mmq = kb.new_sem()
        rpq = kb.new_sem()
        stq = kb.new_sem()
        csl = kb.new_sem()
        qtT = kb.new_sem()
        qfe = kb.new_sem()

        @block.sync
        def _(sync, q0=q0):
            sync.dma_start(out=V["s_xt"][:, :].rearrange("p (a c) -> p a c", c=512),
                           in_=V["cosq_in"][q0:q0 + 512, :]
                           .rearrange("(a p) c -> p a c", p=128)).then_inc(csl.h, csl.inc(16))
            sync.dma_start(out=V["s_p0"][:, :].rearrange("p (a c) -> p a c", c=512),
                           in_=V["sinq_in"][q0:q0 + 512, :]
                           .rearrange("(a p) c -> p a c", p=128)).then_inc(csl.h, csl.inc(16))
            k = 0
            for qc in range(8):
                for ttp in range(2):
                    for i in range(NID):
                        if k % 2 == 0:
                            if k >= 4:
                                sync.wait_ge(mmq.h, mmq.t(k - 3))
                            sync.dma_start(out=WB4[k % 4][:, :],
                                           in_=V["wq_cache"][i * 128:(i + 1) * 128,
                                                             qc * 512:(qc + 1) * 512]) \
                                .then_inc(lwq[k % 4].h, lwq[k % 4].inc(16))
                        k += 1

        @block.scalar
        def _(scalar):
            k = 0
            for qc in range(8):
                for ttp in range(2):
                    for i in range(NID):
                        if k % 2 == 1:
                            if k >= 4:
                                scalar.wait_ge(mmq.h, mmq.t(k - 3))
                            scalar.dma_start(out=WB4[k % 4][:, :],
                                             in_=V["wq_cache"][i * 128:(i + 1) * 128,
                                                               qc * 512:(qc + 1) * 512]) \
                                .then_inc(lwq[k % 4].h, lwq[k % 4].inc(16))
                        k += 1

        @block.tensor
        def _(tensor):
            tensor.wait_ge(fmq.h, fmq.t(4 * NTT))
            k = 0
            for qc in range(8):
                for ttp in range(2):
                    g2 = qc * 2 + ttp
                    if g2 >= 1:
                        tensor.wait_ge(rpq.h, rpq.t(g2 * 2))
                    for i in range(NID):
                        tensor.wait_ge(lwq[k % 4].h, lwq[k % 4].t((k // 4 + 1) * 16))
                        for h2 in range(2):
                            tt = ttp * 2 + h2
                            ins = tensor.matmul(out=p_mm[h2][:, :],
                                          lhsT=ab2[:, i * CHK + tt * 128:i * CHK + (tt + 1) * 128],
                                          rhs=WB4[k % 4][:, :],
                                          start=(i == 0), stop=(i == NID - 1))
                        ins.then_inc(mmq.h, mmq.inc(1))
                        k += 1
                    for h2 in range(2):
                        tt = ttp * 2 + h2
                        gtt = qc * 4 + tt
                        tensor.wait_ge(rpq.h, rpq.t(gtt + 1))
                        if gtt >= 1:
                            tensor.wait_ge(qfe.h, qfe.t(gtt))
                        for sub in range(4):
                            ins = tensor.transpose(out=V["p_trb"][:, sub * 128:(sub + 1) * 128],
                                             in_=V["s_lnb"][:, sub * 128:(sub + 1) * 128],
                                             identity=V["s_ident"][:, :])
                        ins.then_inc(qtT.h, qtT.inc(1))


        @block.vector
        def _(vector):
            vector.wait_ge(csl.h, csl.t(32))
            idx = 0
            for qc in range(8):
                for ttp in range(2):
                    vector.wait_ge(mmq.h, mmq.t((qc * 2 + ttp + 1) * 32))
                    for h2 in range(2):
                        tt = ttp * 2 + h2
                        pm3 = p_mm[h2][:, :].rearrange("p (a c) -> p a c", c=128)
                        rt3 = V["s_rot"][:, :].rearrange("p (a c) -> p a c", c=128)
                        vector.tensor_scalar(out=rt3[:, :, 0:64], in0=pm3[:, :, 64:128],
                                             scalar1=-1.0, scalar2=None, op0=ALU.mult)
                        vector.tensor_copy(out=rt3[:, :, 64:128], in_=pm3[:, :, 0:64])
                        vector.tensor_mul(out=V["s_ra"][:, :], in0=p_mm[h2][:, :],
                                          in1=V["s_xt"][:, tt * 512:(tt + 1) * 512])
                        vector.tensor_mul(out=V["s_rot"][:, :], in0=V["s_rot"][:, :],
                                          in1=V["s_p0"][:, tt * 512:(tt + 1) * 512])
                        if idx >= 1:
                            vector.wait_ge(qtT.h, qtT.t(idx))
                        vector.tensor_add(out=V["s_lnb"][:, 0:512], in0=V["s_ra"][:, :],
                                          in1=V["s_rot"][:, :]).then_inc(rpq.h, rpq.inc(1))
                        vector.wait_ge(qtT.h, qtT.t(idx + 1))
                        if idx >= 4 and idx % 4 == 0:
                            vector.wait_ge(stq.h, stq.t(16 * idx))
                        vector.tensor_copy(
                            out=V["s_kfm"][:, 0:2048].rearrange("p (s t) -> p s t", t=512)
                                [:, :, tt * 128:(tt + 1) * 128],
                            in_=V["p_trb"][:, 0:512].rearrange("p (s c) -> p s c", c=128)) \
                            .then_inc(qfe.h, qfe.inc(1))
                        idx += 1

        @block.gpsimd
        def _(gps, q0=q0):
            for qc in range(8):
                gps.wait_ge(qfe.h, qfe.t((qc + 1) * 4))
                for sub in range(4):
                    gps.dma_start(out=V["q_fm"][qc * 512 + sub * 128:qc * 512 + (sub + 1) * 128,
                                                q0:q0 + 512],
                                  in_=V["s_kfm"][:, sub * 512:(sub + 1) * 512]) \
                        .then_inc(stq.h, stq.inc(16))
            gps.wait_ge(stq.h, stq.n)

        _fence(kb, block, V)



def _attn(kb, block, V):
    ab1, ab2 = V["ab1"], V["ab2"]
    p_mm, p_trb = V["p_mm"], V["p_trb"]

    grp_state = {}
    for qh in range(2):
        q0 = qh * CHK
        for h in range(NH):
            kvh = h // 4
            if h % 4 == 0:
                grp_state["klf"] = kb.new_sem()
                grp_state["vlk"] = kb.new_sem()
            klf = grp_state["klf"]
            vlk = grp_state["vlk"]
            ql = kb.new_sem()
            mm1 = kb.new_sem()
            rxp = kb.new_sem()
            rxf = kb.new_sem()
            mm2 = kb.new_sem()
            exl = kb.new_sem()
            mkl = [kb.new_sem(), kb.new_sem()]
            msk = kb.new_sem()
            sgm = kb.new_sem()
            ptT = kb.new_sem()
            pte = kb.new_sem()
            avm = kb.new_sem()

            @block.sync
            def _(sync, h=h, kvh=kvh, q0=q0):
                if h % 4 == 0:
                    sync.dma_start(out=V["s_kfm"][:, :],
                                   in_=V["k_sh"][kvh * 128:(kvh + 1) * 128, :]) \
                        .then_inc(klf.h, klf.inc(16))
                    for ks in range(32):
                        sync.dma_start(out=ab1[:, 16384 + ks * 128:16384 + (ks + 1) * 128],
                                       in_=V["v_sh"][ks * 128:(ks + 1) * 128,
                                                     kvh * 128:(kvh + 1) * 128]) \
                            .then_inc(vlk.h, vlk.inc(16))
                sync.dma_start(out=V["s_qh"][:, :],
                               in_=V["q_fm"][h * 128:(h + 1) * 128, q0:q0 + 512]) \
                    .then_inc(ql.h, ql.inc(16))
                for k in range(8):
                    qt, half = k // 2, k % 2
                    if k >= 2:
                        sync.wait_ge(msk.h, msk.t(k - 1))
                    sync.dma_start(out=V["s_mask"][k % 2][:, :],
                                   in_=V["mask_in"][q0 + qt * 128:q0 + (qt + 1) * 128,
                                                    half * 2048:(half + 1) * 2048]) \
                        .then_inc(mkl[k % 2].h, mkl[k % 2].inc(16))

            @block.tensor
            def _(tensor, h=h):
                tensor.wait_ge(ql.h, ql.t(16))
                tensor.wait_ge(klf.h, klf.t(16))
                for qt in range(4):
                    for uh in range(8):
                        idx = qt * 8 + uh
                        if idx >= 2:
                            tensor.wait_ge(rxp.h, rxp.t(idx - 1))
                        tensor.matmul(out=p_mm[uh % 2][:, :],
                                      lhsT=V["s_qh"][:, qt * 128:(qt + 1) * 128],
                                      rhs=V["s_kfm"][:, uh * 512:(uh + 1) * 512],
                                      start=True, stop=True) \
                            .then_inc(mm1.h, mm1.inc(1))
                tensor.wait_ge(rxf.h, rxf.t(4))
                for qt in range(4):
                    for uh in range(8):
                        idx = qt * 8 + uh
                        if idx >= 2:
                            tensor.wait_ge(exl.h, exl.t(idx - 1))
                        tensor.matmul(out=p_mm[uh % 2][:, :],
                                      lhsT=V["s_qh"][:, qt * 128:(qt + 1) * 128],
                                      rhs=V["s_kfm"][:, uh * 512:(uh + 1) * 512],
                                      start=True, stop=True) \
                            .then_inc(mm2.h, mm2.inc(1))
                tensor.wait_ge(sgm.h, sgm.t(5))
                if h % 4 == 0:
                    tensor.wait_ge(vlk.h, vlk.t(512))
                for ks in range(32):
                    if ks >= 1:
                        tensor.wait_ge(pte.h, pte.t(ks))
                    for qt in range(4):
                        ins = tensor.transpose(out=p_trb[:, qt * 128:(qt + 1) * 128],
                                         in_=ab1[:, qt * 4096 + ks * 128:qt * 4096 + ks * 128 + 128],
                                         identity=V["s_ident"][:, :])
                    ins.then_inc(ptT.h, ptT.inc(1))
                    tensor.wait_ge(pte.h, pte.t(ks + 1))
                    tensor.matmul(out=p_mm[2][:, :],
                                  lhsT=ab1[:, 16384 + ks * 128:16384 + (ks + 1) * 128],
                                  rhs=V["s_pt"][:, :],
                                  start=(ks == 0), stop=(ks == 31)) \
                        .then_inc(avm.h, avm.inc(1))

            @block.scalar
            def _(scalar, h=h):
                for qt in range(4):
                    for uh in range(8):
                        idx = qt * 8 + uh
                        scalar.wait_ge(mm2.h, mm2.t(idx + 1))
                        scalar.activation(
                            out=ab1[:, qt * 4096 + uh * 512:qt * 4096 + (uh + 1) * 512],
                            in_=p_mm[uh % 2][:, :], func=AF.Exp,
                            bias=V["s_mneg"][:, qt:qt + 1]).then_inc(exl.h, exl.inc(1))

            @block.vector
            def _(vector, h=h):
                for qt in range(4):
                    for uh in range(8):
                        c = 8 + qt * 8 + uh
                        vector.wait_ge(mm1.h, mm1.t(qt * 8 + uh + 1))
                        vector.reduce_max(out=V["s_mneg"][:, c:c + 1],
                                          in_=p_mm[uh % 2][:, :], axis=AX.X) \
                            .then_inc(rxp.h, rxp.inc(1))
                _dsep(vector, V)
                for qt in range(4):
                    vector.reduce_max(out=V["s_mneg"][:, qt:qt + 1],
                                      in_=V["s_mneg"][:, 8 + qt * 8:16 + qt * 8],
                                      axis=AX.X, negate=True) \
                        .then_inc(rxf.h, rxf.inc(1))
                for qt in range(4):
                    for half in range(2):
                        k = qt * 2 + half
                        vector.wait_ge(mkl[k % 2].h, mkl[k % 2].t((k // 2 + 1) * 16))
                        for u4 in range(4):
                            uh = half * 4 + u4
                            vector.wait_ge(exl.h, exl.t(qt * 8 + uh + 1))
                            ins = vector.tensor_mul(
                                out=ab1[:, qt * 4096 + uh * 512:qt * 4096 + (uh + 1) * 512],
                                in0=ab1[:, qt * 4096 + uh * 512:qt * 4096 + (uh + 1) * 512],
                                in1=V["s_mask"][k % 2][:, u4 * 512:(u4 + 1) * 512])
                        ins.then_inc(msk.h, msk.inc(1))
                    vector.reduce_sum(out=V["s_sig"][:, qt:qt + 1],
                                      in_=ab1[:, qt * 4096:(qt + 1) * 4096], axis=AX.X) \
                        .then_inc(sgm.h, sgm.inc(1))
                _dsep(vector, V)
                vector.reciprocal(out=V["s_rcp"][:, 0:4], in_=V["s_sig"][:, 0:4])
                _dsep(vector, V)
                for qt in range(4):
                    ins = vector.tensor_scalar(out=ab1[:, qt * 4096:(qt + 1) * 4096],
                                         in0=ab1[:, qt * 4096:(qt + 1) * 4096],
                                         scalar1=V["s_rcp"][:, qt:qt + 1], scalar2=None,
                                         op0=ALU.mult)
                ins.then_inc(sgm.h, sgm.inc(1))
                for ks in range(32):
                    vector.wait_ge(ptT.h, ptT.t(ks + 1))
                    if ks >= 1:
                        vector.wait_ge(avm.h, avm.t(ks))
                    vector.tensor_copy(out=V["s_pt"][:, :], in_=p_trb[:, 0:512]) \
                        .then_inc(pte.h, pte.inc(1))
                vector.wait_ge(avm.h, avm.t(32))
                vector.tensor_copy(out=ab2[:, h * 512:(h + 1) * 512], in_=p_mm[2][:, :])

            if V.get("dbg") and qh == 0 and h == 0:
                dmp = kb.new_sem()

                @block.gpsimd
                def _(gps):
                    gps.wait_ge(sgm.h, sgm.t(5))
                    for c4 in range(4):
                        gps.dma_start(out=V["dbg_P"][:, c4 * 4096:(c4 + 1) * 4096],
                                      in_=ab1[:, c4 * 4096:(c4 + 1) * 4096]) \
                            .then_inc(dmp.h, dmp.inc(16))
                    gps.dma_start(out=V["dbg_mn"][:, 0:40], in_=V["s_mneg"][:, :]) \
                        .then_inc(dmp.h, dmp.inc(16))
                    gps.dma_start(out=V["dbg_mn"][:, 40:44], in_=V["s_sig"][:, 0:4]) \
                        .then_inc(dmp.h, dmp.inc(16))
                    gps.dma_start(out=V["dbg_mn"][:, 44:48], in_=V["s_rcp"][:, 0:4]) \
                        .then_inc(dmp.h, dmp.inc(16))
                    gps.wait_ge(dmp.h, dmp.n)

            _fence(kb, block, V)

        lwo = [kb.new_sem() for _ in range(4)]
        WB4 = [V["s_wg"][0], V["s_wg"][1], V["s_wb"][0], V["s_wb"][1]]
        wti = kb.new_sem()
        x1l = [kb.new_sem(), kb.new_sem()]
        ewo = kb.new_sem()
        swo = kb.new_sem()

        @block.sync
        def _(sync, q0=q0):
            k = 0
            for dc in range(8):
                if dc >= 2:
                    sync.wait_ge(ewo.h, ewo.t((dc - 1) * 4))
                xbuf = V["s_xt"] if dc % 2 == 0 else V["s_p0"]
                for qt in range(4):
                    sync.dma_start(out=xbuf[:, qt * 512:(qt + 1) * 512],
                                   in_=V["x1own_dram"][q0 + qt * 128:q0 + (qt + 1) * 128,
                                                       dc * 512:(dc + 1) * 512]) \
                        .then_inc(x1l[dc % 2].h, x1l[dc % 2].inc(16))
                for ot in range(NID):
                    if k % 2 == 0:
                        if k >= 4:
                            sync.wait_ge(wti.h, wti.t(k - 3))
                        sync.dma_start(out=WB4[k % 4][:, :],
                                       in_=V["wo_cache"][ot * 128:(ot + 1) * 128,
                                                         dc * 512:(dc + 1) * 512]) \
                            .then_inc(lwo[k % 4].h, lwo[k % 4].inc(16))
                    k += 1

        @block.scalar
        def _(scalar, q0=q0):
            k = 0
            for dc in range(8):
                for ot in range(NID):
                    if k % 2 == 1:
                        if k >= 4:
                            scalar.wait_ge(wti.h, wti.t(k - 3))
                        scalar.dma_start(out=WB4[k % 4][:, :],
                                         in_=V["wo_cache"][ot * 128:(ot + 1) * 128,
                                                           dc * 512:(dc + 1) * 512]) \
                            .then_inc(lwo[k % 4].h, lwo[k % 4].inc(16))
                    k += 1

        @block.tensor
        def _(tensor):
            k = 0
            for dc in range(8):
                if dc >= 1:
                    tensor.wait_ge(ewo.h, ewo.t(dc * 4))
                for ot in range(NID):
                    tensor.wait_ge(lwo[k % 4].h, lwo[k % 4].t((k // 4 + 1) * 16))
                    for qt in range(4):
                        ins = tensor.matmul(out=p_mm[qt][:, :],
                                      lhsT=ab2[:, ot * 512 + qt * 128:ot * 512 + (qt + 1) * 128],
                                      rhs=WB4[k % 4][:, :],
                                      start=(ot == 0), stop=(ot == NID - 1))
                    ins.then_inc(wti.h, wti.inc(1))
                    k += 1

        @block.vector
        def _(vector):
            for dc in range(8):
                vector.wait_ge(wti.h, wti.t((dc + 1) * 32))
                vector.wait_ge(x1l[dc % 2].h, x1l[dc % 2].t((dc // 2 + 1) * 64))
                xbuf = V["s_xt"] if dc % 2 == 0 else V["s_p0"]
                for qt in range(4):
                    idx = dc * 4 + qt
                    if idx >= 2:
                        vector.wait_ge(swo.h, swo.t((idx - 1) * 16))
                    vector.scalar_tensor_tensor(out=V["s_ev"][qt % 2][:, :],
                                                in0=p_mm[qt][:, :], scalar=1.0,
                                                in1=xbuf[:, qt * 512:(qt + 1) * 512],
                                                op0=ALU.mult, op1=ALU.add) \
                        .then_inc(ewo.h, ewo.inc(1))

        @block.gpsimd
        def _(gps, q0=q0):
            for dc in range(8):
                for qt in range(4):
                    gps.wait_ge(ewo.h, ewo.t(dc * 4 + qt + 1))
                    gps.dma_start(out=V["x2_dram"][q0 + qt * 128:q0 + (qt + 1) * 128,
                                                   dc * 512:(dc + 1) * 512],
                                  in_=V["s_ev"][qt % 2][:, :]).then_inc(swo.h, swo.inc(16))
            gps.wait_ge(swo.h, swo.n)

        _fence(kb, block, V)


def _ln3(kb, block, V):
    for qc2 in range(2):
        q0 = qc2 * CHK
        fm = _ln_to_fm(kb, block, V,
                       lambda tt, half, q0=q0: V["x2_dram"][q0 + tt * 128:q0 + (tt + 1) * 128,
                                                            half * 2048:(half + 1) * 2048],
                       NTT)
        st3 = kb.new_sem()

        @block.gpsimd
        def _(gps, qc2=qc2):
            gmem = _getreg(gps, V, "g_member")
            gps.load(gmem, V["s_meta"][0:1, 0:1])
            gps.wait_ge(fm.h, fm.t(4 * NTT))
            for g8 in range(8):
                r0 = g8 * 4 * 128
                src = V["ab2"][:, g8 * 2048:(g8 + 1) * 2048] \
                    .rearrange("p (i t) -> p i t", t=CHK)
                with gps.If_eq(gmem, 0):
                    gps.dma_start(
                        out=V["ln3_sh"][qc2 * NID * 128 + r0:qc2 * NID * 128 + r0 + 512, :]
                        .rearrange("(i p) t -> p i t", p=128),
                        in_=src).then_inc(st3.h, st3.inc(16))
                with gps.Else():
                    gps.dma_start(
                        out=V["ln3_sh"][(2 + qc2) * NID * 128 + r0:(2 + qc2) * NID * 128 + r0 + 512, :]
                        .rearrange("(i p) t -> p i t", p=128),
                        in_=src).then_inc(st3.h, 16)
            gps.wait_ge(st3.h, st3.n)

        _fence(kb, block, V)


def _finish(kb, block, V):
    fl = kb.new_sem()
    fv = kb.new_sem()
    fs = kb.new_sem()

    @block.sync
    def _(sync):
        smem = _getreg(sync, V, "s_member")
        sync.load(smem, V["s_meta"][0:1, 0:1])
        for k in range(16):
            tt, half = k // 2, k % 2
            if k > 0:
                sync.wait_ge(fv.h, fv.t(k))
            hs = slice(half * 2048, (half + 1) * 2048)
            sync.dma_start(out=V["s_xt"][:, :],
                           in_=V["x2_dram"][tt * 128:(tt + 1) * 128, hs]) \
                .then_inc(fl.h, fl.inc(16))
            with sync.If_eq(smem, 0):
                r = tt * 128
                sync.dma_start(out=V["s_p0"][:, :],
                               in_=V["pc_sh"][0][r:r + 128, hs]).then_inc(fl.h, fl.inc(16))
                sync.dma_start(out=V["s_p1"][:, :],
                               in_=V["pc_sh"][1][r:r + 128, hs]).then_inc(fl.h, fl.inc(16))
            with sync.Else():
                r = 1024 + tt * 128
                sync.dma_start(out=V["s_p0"][:, :],
                               in_=V["pc_sh"][0][r:r + 128, hs]).then_inc(fl.h, 16)
                sync.dma_start(out=V["s_p1"][:, :],
                               in_=V["pc_sh"][1][r:r + 128, hs]).then_inc(fl.h, 16)

    @block.vector
    def _(vector):
        for k in range(16):
            vector.wait_ge(fl.h, fl.t((k + 1) * 48))
            if k >= 1:
                vector.wait_ge(fs.h, fs.t(k * 16))
            vector.tensor_add(out=V["s_p0"][:, :], in0=V["s_p0"][:, :], in1=V["s_p1"][:, :])
            vector.scalar_tensor_tensor(out=V["s_ln"][:, :], in0=V["s_p0"][:, :],
                                        scalar=0.5, in1=V["s_xt"][:, :],
                                        op0=ALU.mult, op1=ALU.add) \
                .then_inc(fv.h, fv.inc(1))

    @block.gpsimd
    def _(gps):
        for k in range(16):
            tt, half = k // 2, k % 2
            gps.wait_ge(fv.h, fv.t(k + 1))
            gps.dma_start(out=V["out_x3"][tt * 128:(tt + 1) * 128,
                                          half * 2048:(half + 1) * 2048],
                          in_=V["s_ln"][:, :]).then_inc(fs.h, fs.inc(16))
        gps.wait_ge(fs.h, fs.n)


def _dbg_out(kb, block, V, upto):
    ze = kb.new_sem()
    st = kb.new_sem()

    @block.vector
    def _(vector):
        vector.memset(V["s_ln"][:, :], 0.0).then_inc(ze.h, ze.inc(1))

    @block.gpsimd
    def _(gps):
        gps.wait_ge(ze.h, ze.t(1))
        if V.get("dbg"):
            if upto >= 1:
                gps.dma_start(out=V["dbg_w"][:, :],
                              in_=V["w_cache"]["g"][0:256, :]).then_inc(st.h, st.inc(16))
            if upto >= 2:
                for q in range(4):
                    gps.dma_start(out=V["dbg_x1"][q * 1024:(q + 1) * 1024, :],
                                  in_=V["x1_sh"][q * 1024:(q + 1) * 1024, :]) \
                        .then_inc(st.h, st.inc(16))
                    for m in range(2):
                        gps.dma_start(out=V["dbg_pa"][m][q * 1024:(q + 1) * 1024, :],
                                      in_=V["pa_sh"][m][q * 1024:(q + 1) * 1024, :]) \
                            .then_inc(st.h, st.inc(16))
            if upto >= 3:
                gps.dma_start(out=V["dbg_k"][:, :], in_=V["k_sh"][:, :]).then_inc(st.h, st.inc(16))
                gps.dma_start(out=V["dbg_v"][:, :], in_=V["v_sh"][:, :]).then_inc(st.h, st.inc(16))
                gps.dma_start(out=V["dbg_q"][:, :], in_=V["q_fm"][:, :]) \
                    .then_inc(st.h, st.inc(16))
            if upto >= 4:
                gps.dma_start(out=V["dbg_x2"][:, :], in_=V["x2_dram"][:, :]).then_inc(st.h, st.inc(16))
        for k in range(16):
            tt, half = k // 2, k % 2
            gps.dma_start(out=V["out_x3"][tt * 128:(tt + 1) * 128,
                                          half * 2048:(half + 1) * 2048],
                          in_=V["s_ln"][:, :]).then_inc(st.h, st.inc(16))
        gps.wait_ge(st.h, st.n)



def _build_all(kb, block, V, stages=9, dbg=False):
    p_ma = V["p_ma"]

    m0 = kb.new_sem()

    @block.sync
    def _(sync):
        sync.dma_start(out=V["s_meta"][:], in_=V["meta_in"]).then_inc(m0.h, m0.inc(16))
        sync.dma_start(out=V["s_ident"][:, :], in_=V["identb_in"]).then_inc(m0.h, m0.inc(16))
        sync.dma_start(out=V["s_identf"][:, :], in_=V["identf_in"]).then_inc(m0.h, m0.inc(16))

    @block.vector
    def _(vector):
        vector.memset(V["s_onesf"][:], 1.0)
        vector.memset(V["s_lnb"][:, :], 0.0)
        vector.wait_ge(m0.h, m0.t(48))

    _fence(kb, block, V)

    lca = kb.new_sem()
    lcv = kb.new_sem()
    la_specs = [("g", NID, 0), ("u", NID, NID * R), ("d", NDH, 2 * NID * R)]

    @block.sync
    def _(sync):
        for k, (mat, nit, dst) in enumerate(la_specs):
            if k >= 1:
                sync.wait_ge(lcv.h, lcv.t(k))
            for i in range(nit):
                sync.dma_start(out=V["s_xt"][:, i * R:(i + 1) * R],
                               in_=V["la_in"][mat][i * 128:(i + 1) * 128, :]) \
                    .then_inc(lca.h, lca.inc(16))

    @block.vector
    def _(vector):
        tot = 0
        for k, (mat, nit, dst) in enumerate(la_specs):
            tot += nit
            vector.wait_ge(lca.h, lca.t(tot * 16))
            vector.tensor_copy(out=V["s_la"][:, dst:dst + nit * R],
                               in_=V["s_xt"][:, 0:nit * R]).then_inc(lcv.h, lcv.inc(1))

    lb_specs = []
    for mat, width, lbo in (("g", DH, 0), ("u", DH, DH), ("d", D, 2 * DH)):
        for c0 in range(0, width, 2048):
            lb_specs.append((mat, c0, lbo, min(2048, width - c0)))
    lbL = kb.new_sem()
    lbC = kb.new_sem()
    lbS = kb.new_sem()

    @block.sync
    def _(sync):
        sync.wait_ge(lcv.h, lcv.t(3))
        for k, (mat, c0, lbo, w) in enumerate(lb_specs):
            if k >= 1:
                sync.wait_ge(lbC.h, lbC.t(k))
            sync.dma_start(out=V["s_xt"][0:16, 0:w],
                           in_=V["lb_in"][mat][0:16, c0:c0 + w]) \
                .then_inc(lbL.h, lbL.inc(16))

    @block.vector
    def _(vector):
        for k, (mat, c0, lbo, w) in enumerate(lb_specs):
            vector.wait_ge(lbL.h, lbL.t((k + 1) * 16))
            if k >= 1:
                vector.wait_ge(lbS.h, lbS.t(k * 16))
            vector.tensor_copy(out=V["s_lnb"][0:16, 0:w],
                               in_=V["s_xt"][0:16, 0:w]).then_inc(lbC.h, lbC.inc(1))

    @block.scalar
    def _(scalar):
        for k, (mat, c0, lbo, w) in enumerate(lb_specs):
            scalar.wait_ge(lbC.h, lbC.t(k + 1))
            scalar.dma_start(out=V["lb_cache"][0:16, lbo + c0:lbo + c0 + w],
                             in_=V["s_lnb"][0:16, 0:w]).then_inc(lbS.h, lbS.inc(16))
        scalar.wait_ge(lbS.h, lbS.n)

    _fence(kb, block, V)

    wc_specs = []
    for it in range(NID):
        wc_specs.append((V["wk_in"], it * 128, 0, V["wkv_cache"], it * 128, 0))
        wc_specs.append((V["wv_in"], it * 128, 0, V["wkv_cache"], it * 128, 512))
    for it in range(NID):
        for oc in range(8):
            wc_specs.append((V["wq_in"], it * 128, oc * 512, V["wq_cache"], it * 128, oc * 512))
            wc_specs.append((V["wo_in"], it * 128, oc * 512, V["wo_cache"], it * 128, oc * 512))
    wca = [kb.new_sem(), kb.new_sem()]
    wcv = kb.new_sem()
    wcs = kb.new_sem()

    @block.sync
    def _(sync):
        for k, (src, r0, c0, dst, dr, dc) in enumerate(wc_specs):
            if k >= 2:
                sync.wait_ge(wcv.h, wcv.t(k - 1))
            sync.dma_start(out=(V["s_xt"] if k % 2 == 0 else V["s_p0"])[:, 0:512],
                           in_=src[r0:r0 + 128, c0:c0 + 512]) \
                .then_inc(wca[k % 2].h, wca[k % 2].inc(16))

    @block.vector
    def _(vector):
        for k in range(len(wc_specs)):
            vector.wait_ge(wca[k % 2].h, wca[k % 2].t((k // 2 + 1) * 16))
            if k >= 2:
                vector.wait_ge(wcs.h, wcs.t((k - 1) * 16))
            vector.tensor_copy(out=V["s_wb"][k % 2][:, :],
                               in_=(V["s_xt"] if k % 2 == 0 else V["s_p0"])[:, 0:512]) \
                .then_inc(wcv.h, wcv.inc(1))

    @block.scalar
    def _(scalar):
        for k, (src, r0, c0, dst, dr, dc) in enumerate(wc_specs):
            scalar.wait_ge(wcv.h, wcv.t(k + 1))
            scalar.dma_start(out=dst[dr:dr + 128, dc:dc + 512],
                             in_=V["s_wb"][k % 2][:, :]).then_inc(wcs.h, wcs.inc(16))
        scalar.wait_ge(wcs.h, wcs.n)

    _fence(kb, block, V)

    deq = V["deq_tiles"]
    cdl = [kb.new_sem(), kb.new_sem()]
    arl = [kb.new_sem(), kb.new_sem()]
    mrow = kb.new_sem()
    dqa = kb.new_sem()
    dqv = kb.new_sem()
    wst = kb.new_sem()

    @block.sync
    def _(sync):
        for k, (mat, it, oc, w) in enumerate(deq):
            if k >= 2:
                sync.wait_ge(dqv.h, dqv.t(k - 1))
            sync.dma_start(out=V["s_codes"][k % 2][:, 0:w],
                           in_=V["codes_in"][mat][it * 128:(it + 1) * 128,
                                                  oc * 512:oc * 512 + w]) \
                .then_inc(cdl[k % 2].h, cdl[k % 2].inc(16))
            sync.dma_start(out=V["s_arow"][k % 2][0:1, 0:w],
                           in_=V["am_in"][mat][2 * it:2 * it + 1, oc * 512:oc * 512 + w]) \
                .then_inc(arl[k % 2].h, arl[k % 2].inc(16))
            sync.dma_start(out=V["s_brow"][k % 2][0:1, 0:w],
                           in_=V["am_in"][mat][2 * it + 1:2 * it + 2, oc * 512:oc * 512 + w]) \
                .then_inc(arl[k % 2].h, arl[k % 2].inc(16))

    @block.tensor
    def _(tensor):
        for k, (mat, it, oc, w) in enumerate(deq):
            tensor.wait_ge(arl[k % 2].h, arl[k % 2].t((k // 2 + 1) * 32))
            if k >= 2:
                tensor.wait_ge(dqv.h, dqv.t(k - 1))
            tensor.matmul(out=p_ma[k % 2][0:64, 0:w], lhsT=V["s_onesf"][0:1, 0:64],
                          rhs=V["s_arow"][k % 2][0:1, 0:w], start=True, stop=True)
            tensor.matmul(out=p_ma[k % 2][64:128, 0:w], lhsT=V["s_onesf"][0:1, 0:64],
                          rhs=V["s_brow"][k % 2][0:1, 0:w], start=True, stop=True,
                          tile_position=(0, 64)).then_inc(mrow.h, mrow.inc(1))

    @block.scalar
    def _(scalar):
        for k, (mat, it, oc, w) in enumerate(deq):
            scalar.wait_ge(cdl[k % 2].h, cdl[k % 2].t((k // 2 + 1) * 16))
            if k >= 1:
                scalar.wait_ge(dqv.h, dqv.t(k))
            cw = slice(0, w)
            scalar.activation(out=V["s_du"][:, cw], in_=V["s_codes"][k % 2][:, cw],
                              func=AF.Copy, scale=1.0 / 7.5, bias=-1.0)
            scalar.activation(out=V["s_ds"][:, cw], in_=V["s_du"][:, cw], func=AF.Square)
            scalar.activation(out=V["s_dz"][:, cw], in_=V["s_ds"][:, cw], func=AF.Square)
            for j in range(4):
                ins = scalar.activation(out=V["s_pe"][j][:, cw], in_=V["s_ds"][:, cw],
                                        func=AF.Copy, scale=CE[2 * j + 1], bias=CE[2 * j])
            ins.then_inc(dqa.h, dqa.inc(1))

    @block.vector
    def _(vector):
        for k, (mat, it, oc, w) in enumerate(deq):
            vector.wait_ge(dqa.h, dqa.t(k + 1))
            vector.wait_ge(mrow.h, mrow.t(k + 1))
            if k >= 2:
                vector.wait_ge(wst.h, wst.t((k - 1) * 16))
            cw = slice(0, w)
            for j in range(4):
                vector.tensor_scalar(out=V["s_po"][j][:, cw], in0=V["s_ds"][:, cw],
                                     scalar1=CO[2 * j + 1], scalar2=CO[2 * j],
                                     op0=ALU.mult, op1=ALU.add)
            for tgt, parts in ((V["s_te"], V["s_pe"]), (V["s_to"], V["s_po"])):
                vector.scalar_tensor_tensor(out=tgt[:, cw], in0=parts[3][:, cw],
                                            scalar=0.0, in1=V["s_dz"][:, cw],
                                            op0=ALU.add, op1=ALU.mult)
                vector.tensor_add(out=tgt[:, cw], in0=tgt[:, cw], in1=parts[2][:, cw])
                vector.scalar_tensor_tensor(out=tgt[:, cw], in0=tgt[:, cw],
                                            scalar=0.0, in1=V["s_dz"][:, cw],
                                            op0=ALU.add, op1=ALU.mult)
                vector.tensor_add(out=tgt[:, cw], in0=tgt[:, cw], in1=parts[1][:, cw])
                vector.scalar_tensor_tensor(out=tgt[:, cw], in0=tgt[:, cw],
                                            scalar=0.0, in1=V["s_dz"][:, cw],
                                            op0=ALU.add, op1=ALU.mult)
                vector.tensor_add(out=tgt[:, cw], in0=tgt[:, cw], in1=parts[0][:, cw])
            vector.tensor_mul(out=V["s_to"][:, cw], in0=V["s_to"][:, cw],
                              in1=V["s_du"][:, cw])
            vector.tensor_add(out=V["s_te"][:, cw], in0=V["s_te"][:, cw],
                              in1=V["s_to"][:, cw])
            vector.tensor_mul(out=V["s_wout"][k % 2][:, cw], in0=V["s_te"][:, cw],
                              in1=p_ma[k % 2][:, cw]).then_inc(dqv.h, dqv.inc(1))

    @block.gpsimd
    def _(gps):
        for k, (mat, it, oc, w) in enumerate(deq):
            gps.wait_ge(dqv.h, dqv.t(k + 1))
            gps.dma_start(out=V["w_cache"][mat][it * 128:(it + 1) * 128,
                                                oc * 512:oc * 512 + w],
                          in_=V["s_wout"][k % 2][:, 0:w]).then_inc(wst.h, wst.inc(16))
        gps.wait_ge(wst.h, wst.n)

    _fence(kb, block, V)
    if stages <= 1:
        _dbg_out(kb, block, V, 1)
        return

    _ffn(kb, block, V, "A")
    _barrier(kb, block, V, 0)

    xa = kb.new_sem()
    xl = kb.new_sem()
    xs = kb.new_sem()

    @block.sync
    def _(sync):
        smem = _getreg(sync, V, "s_member")
        sync.load(smem, V["s_meta"][0:1, 0:1])
        for k in range(32):
            tb, tt, half = k // 16, (k // 2) % 8, k % 2
            if k > 0:
                sync.wait_ge(xa.h, xa.t(k))
            hs = slice(half * 2048, (half + 1) * 2048)
            with sync.If_eq(smem, 0):
                r = tb * 2048 + tt * 128
                sync.dma_start(out=V["s_xt"][:, :], in_=V["x_in"][r:r + 128, hs]) \
                    .then_inc(xl.h, xl.inc(16))
                sync.dma_start(out=V["s_p0"][:, :], in_=V["pa_sh"][0][r:r + 128, hs]) \
                    .then_inc(xl.h, xl.inc(16))
                sync.dma_start(out=V["s_p1"][:, :], in_=V["pa_sh"][1][r:r + 128, hs]) \
                    .then_inc(xl.h, xl.inc(16))
            with sync.Else():
                r = 1024 + tb * 2048 + tt * 128
                sync.dma_start(out=V["s_xt"][:, :], in_=V["x_in"][r:r + 128, hs]) \
                    .then_inc(xl.h, 16)
                sync.dma_start(out=V["s_p0"][:, :], in_=V["pa_sh"][0][r:r + 128, hs]) \
                    .then_inc(xl.h, 16)
                sync.dma_start(out=V["s_p1"][:, :], in_=V["pa_sh"][1][r:r + 128, hs]) \
                    .then_inc(xl.h, 16)

    @block.vector
    def _(vector):
        for k in range(32):
            vector.wait_ge(xl.h, xl.t((k + 1) * 48))
            if k >= 1:
                vector.wait_ge(xs.h, xs.t(k * 32))
            vector.tensor_add(out=V["s_p0"][:, :], in0=V["s_p0"][:, :], in1=V["s_p1"][:, :])
            vector.scalar_tensor_tensor(out=V["s_ln"][:, :], in0=V["s_p0"][:, :],
                                        scalar=0.5, in1=V["s_xt"][:, :],
                                        op0=ALU.mult, op1=ALU.add) \
                .then_inc(xa.h, xa.inc(1))

    @block.gpsimd
    def _(gps):
        gmem = _getreg(gps, V, "g_member")
        gps.load(gmem, V["s_meta"][0:1, 0:1])
        for k in range(32):
            tb, tt, half = k // 16, (k // 2) % 8, k % 2
            hs = slice(half * 2048, (half + 1) * 2048)
            gps.wait_ge(xa.h, xa.t(k + 1))
            with gps.If_eq(gmem, 0):
                r = tb * 2048 + tt * 128
                gps.dma_start(out=V["x1_sh"][r:r + 128, hs], in_=V["s_ln"][:, :]) \
                    .then_inc(xs.h, xs.inc(16))
                if tb == 0:
                    gps.dma_start(out=V["x1own_dram"][tt * 128:(tt + 1) * 128, hs],
                                  in_=V["s_ln"][:, :]).then_inc(xs.h, xs.inc(16))
                else:
                    gps.dma_start(out=V["x1_sh"][r:r + 128, hs], in_=V["s_ln"][:, :]) \
                        .then_inc(xs.h, xs.inc(16))
            with gps.Else():
                r = 1024 + tb * 2048 + tt * 128
                gps.dma_start(out=V["x1_sh"][r:r + 128, hs], in_=V["s_ln"][:, :]) \
                    .then_inc(xs.h, 16)
                if tb == 0:
                    gps.dma_start(out=V["x1own_dram"][tt * 128:(tt + 1) * 128, hs],
                                  in_=V["s_ln"][:, :]).then_inc(xs.h, 16)
                else:
                    gps.dma_start(out=V["x1_sh"][r:r + 128, hs], in_=V["s_ln"][:, :]) \
                        .then_inc(xs.h, 16)
        gps.wait_ge(xs.h, xs.n)

    _fence(kb, block, V)
    _barrier(kb, block, V, 1)
    if stages <= 2:
        _dbg_out(kb, block, V, 2)
        return

    _qkv(kb, block, V)
    _barrier(kb, block, V, 2)
    if stages <= 3:
        _dbg_out(kb, block, V, 3)
        return
    _attn(kb, block, V)
    if stages <= 4:
        _dbg_out(kb, block, V, 4)
        return

    _ln3(kb, block, V)
    _barrier(kb, block, V, 3)
    _ffn(kb, block, V, "C")
    _barrier(kb, block, V, 4)
    _finish(kb, block, V)



_PROG = None
_PROG_KEY = None
LAST_RESULTS = None


def kernel(L0_attn_x, L0_attn_Wq, L0_attn_Wk, L0_attn_Wv, L0_attn_Wo,
           L0_ffn_g_codes, L0_ffn_g_absmax, L0_ffn_g_lora_a, L0_ffn_g_lora_b,
           L0_ffn_u_codes, L0_ffn_u_absmax, L0_ffn_u_lora_a, L0_ffn_u_lora_b,
           L0_ffn_d_codes, L0_ffn_d_absmax, L0_ffn_d_lora_a, L0_ffn_d_lora_b):
    global _PROG, _PROG_KEY, LAST_RESULTS
    import ml_dtypes
    bf16 = ml_dtypes.bfloat16
    rng = np.random.default_rng()
    nonce = int(rng.integers(1, 2**30))

    x = np.ascontiguousarray(np.asarray(L0_attn_x, dtype=np.float32))
    wqT = np.ascontiguousarray(np.asarray(L0_attn_Wq, np.float32).T)
    wkT = np.ascontiguousarray(np.asarray(L0_attn_Wk, np.float32).T)
    wvT = np.ascontiguousarray(np.asarray(L0_attn_Wv, np.float32).T)
    woT = np.ascontiguousarray(np.asarray(L0_attn_Wo, np.float32).T)
    gcT = np.ascontiguousarray(np.asarray(L0_ffn_g_codes, np.int32).T)
    ucT = np.ascontiguousarray(np.asarray(L0_ffn_u_codes, np.int32).T)
    dcT = np.ascontiguousarray(np.asarray(L0_ffn_d_codes, np.int32).T)
    gaT = np.ascontiguousarray(np.asarray(L0_ffn_g_absmax, np.float32).T)
    uaT = np.ascontiguousarray(np.asarray(L0_ffn_u_absmax, np.float32).T)
    daT = np.ascontiguousarray(np.asarray(L0_ffn_d_absmax, np.float32).T)
    gla = np.ascontiguousarray(np.asarray(L0_ffn_g_lora_a, np.float32))
    glb = np.ascontiguousarray(np.asarray(L0_ffn_g_lora_b, np.float32))
    ula = np.ascontiguousarray(np.asarray(L0_ffn_u_lora_a, np.float32))
    ulb = np.ascontiguousarray(np.asarray(L0_ffn_u_lora_b, np.float32))
    dla = np.ascontiguousarray(np.asarray(L0_ffn_d_lora_a, np.float32))
    dlb = np.ascontiguousarray(np.asarray(L0_ffn_d_lora_b, np.float32))

    cos, sin = _llama3_cos_sin()
    ident_b = np.eye(128, dtype=bf16)
    ident_f = np.eye(128, dtype=np.float32)

    stages = int(os.environ.get("ATH_STAGES", "9"))
    dbgf = os.environ.get("ATH_DBG", "0") == "1"
    if _PROG is None or _PROG_KEY != (stages, dbgf):
        _PROG = build_program(stages, dbgf)
        _PROG_KEY = (stages, dbgf)
    nc = _PROG

    in_maps = []
    for core in range(8):
        pair = core // 2
        member = core % 2
        batch = pair // 2
        pip = pair % 2
        rot = pip * 2048
        blk = 2 * pip + member

        perm = np.roll(np.arange(T), -rot)
        xb = np.ascontiguousarray(x[batch][perm])
        cosk4 = np.ascontiguousarray(np.tile(cos[perm], (1, 4)))
        sink4 = np.ascontiguousarray(np.tile(sin[perm], (1, 4)))
        qpos = np.arange(blk * QBLK, (blk + 1) * QBLK)
        cosq4 = np.ascontiguousarray(np.tile(cos[qpos] * ISQ, (1, 4)).astype(np.float32))
        sinq4 = np.ascontiguousarray(np.tile(sin[qpos] * ISQ, (1, 4)).astype(np.float32))
        mask01 = np.ascontiguousarray((qpos[:, None] >= perm[None, :]).astype(bf16))

        mh = slice(member * DH, (member + 1) * DH)
        kvs = slice(member * 512, (member + 1) * 512)

        in_maps.append({
            "x_b": xb,
            "g_codesT_h": np.ascontiguousarray(gcT[:, mh]),
            "u_codesT_h": np.ascontiguousarray(ucT[:, mh]),
            "d_codesT_h": np.ascontiguousarray(dcT[mh, :]),
            "g_absmaxT_h": np.ascontiguousarray(gaT[:, mh]),
            "u_absmaxT_h": np.ascontiguousarray(uaT[:, mh]),
            "d_absmaxT_h": np.ascontiguousarray(daT[member * 86:(member + 1) * 86, :]),
            "g_lora_a": gla, "u_lora_a": ula,
            "d_lora_a_h": np.ascontiguousarray(dla[mh, :]),
            "g_lora_b_h": np.ascontiguousarray(glb[:, mh]),
            "u_lora_b_h": np.ascontiguousarray(ulb[:, mh]),
            "d_lora_b": dlb,
            "wqT": wqT, "woT": woT,
            "wkT_h": np.ascontiguousarray(wkT[:, kvs]),
            "wvT_h": np.ascontiguousarray(wvT[:, kvs]),
            "cosk4": cosk4, "sink4": sink4, "cosq4": cosq4, "sinq4": sinq4,
            "mask01": mask01,
            "identb": ident_b, "identf": ident_f,
            "meta": np.array([[member, nonce, 0, 0, 0, 0, 0, 0]], np.int32),
        })

    res = run_bass_kernel_spmd(nc, in_maps, core_ids=list(range(8)))
    LAST_RESULTS = res
    out = np.empty((B, T, D), dtype=np.float32)
    for core in range(8):
        pair = core // 2
        member = core % 2
        batch = pair // 2
        pip = pair % 2
        blk = 2 * pip + member
        out[batch, blk * QBLK:(blk + 1) * QBLK, :] = res.results[core]["out_x3"]
    return out


# version 04
# speedup vs baseline: 1.4884x; 1.1455x over previous
import math
import os
import contextlib
import numpy as np
import concourse.bass as bass
import concourse.mybir as mybir
from concourse.bass_utils import run_bass_kernel_spmd

F32 = mybir.dt.float32
BF16 = mybir.dt.bfloat16
I32 = mybir.dt.int32
AF = mybir.ActivationFunctionType
ALU = mybir.AluOpType
AX = mybir.AxisListType

B, T, D = 2, 4096, 4096
DFF = 11008
DH = DFF // 2
NDH = DH // 128
NID = D // 128
R = 16
HD = 128
NH = 32
NKV = 8
KVW = NKV * HD
QBLK = 1024
EPS = 1e-5
ISQ = 1.0 / math.sqrt(HD)
CHK = 512
NTT = 4

NF4 = np.array([
    -1.0, -0.6961928009986877, -0.5250730514526367, -0.39491748809814453,
    -0.28444138169288635, -0.18477343022823334, -0.09105003625154495, 0.0,
    0.07958029955625534, 0.16093020141124725, 0.2461123019456863,
    0.33791524171829224, 0.44070982933044434, 0.5626170039176941,
    0.7229568362236023, 1.0], dtype=np.float64)


def _nf4_coeffs():
    E, O, Sq = [], [], []
    for k in range(8):
        hi, lo = NF4[8 + k], NF4[7 - k]
        u = (k + 0.5) / 7.5
        E.append((hi + lo) / 2.0)
        O.append((hi - lo) / (2.0 * u))
        Sq.append(u * u)
    cE = np.polynomial.polynomial.polyfit(np.array(Sq), np.array(E), 7)
    cO = np.polynomial.polynomial.polyfit(np.array(Sq), np.array(O), 7)
    return [float(x) for x in cE], [float(x) for x in cO]

CE, CO = _nf4_coeffs()


def _llama3_cos_sin():
    base = 500000.0
    inv_freq = 1.0 / base ** (np.arange(0, HD, 2, dtype=np.float32) / HD)
    factor, low_f, high_f, old_ctx = 32.0, 1.0, 4.0, 8192
    low_wl, high_wl = old_ctx / low_f, old_ctx / high_f
    wavelen = 2 * math.pi / inv_freq
    inv_llama = np.where(wavelen > low_wl, inv_freq / factor, inv_freq)
    smooth = (old_ctx / wavelen - low_f) / (high_f - low_f)
    smoothed = (1 - smooth) * inv_llama / factor + smooth * inv_llama
    is_med = ~(wavelen < high_wl) & ~(wavelen > low_wl)
    inv = np.where(is_med, smoothed, inv_llama).astype(np.float32)
    pos = np.arange(T, dtype=np.float32)
    freqs = pos[:, None] * inv[None, :]
    emb = np.concatenate([freqs, freqs], axis=-1)
    return np.cos(emb).astype(np.float32), np.sin(emb).astype(np.float32)


class Led:
    __slots__ = ("h", "n", "base", "fence_born")

    def __init__(self, h):
        self.h = h
        self.n = 0
        self.base = 0
        self.fence_born = -10

    def inc(self, k):
        self.n += k
        return k

    def t(self, c):
        return self.base + c


class KB:
    POOL = 92

    def __init__(self):
        self.nc = bass.Bass("TRN2", target_bir_lowering=False, debug=False)
        self.es = contextlib.ExitStack()
        self.pool = []
        self.cursor = 0
        self.nfence = 0
        self.ndedic = 0

    def new_sem(self):
        if len(self.pool) < self.POOL:
            led = Led(self.es.enter_context(self.nc.semaphore(f"s{len(self.pool)}")))
            self.pool.append(led)
        else:
            led = self.pool[self.cursor % self.POOL]
            assert self.nfence >= led.fence_born + 2, \
                f"semaphore recycled too soon ({self.nfence} vs {led.fence_born})"
        self.cursor += 1
        led.base = led.n
        led.fence_born = self.nfence
        return led

    def new_dedicated(self):
        led = Led(self.es.enter_context(self.nc.semaphore(f"d{self.ndedic}")))
        self.ndedic += 1
        return led


def _fence(kb, block, V):
    f = kb.new_sem()
    kb.nfence += 1
    for eng in ("sync", "tensor", "scalar", "vector", "gpsimd"):
        @getattr(block, eng)
        def _(e):
            e.drain()
            e.sem_inc(f.h, 1)
            e.wait_ge(f.h, f.t(5))
    f.n = f.base + 5


def _getreg(eng, V, name):
    regs = V.setdefault("_regs", {})
    if name not in regs:
        regs[name] = eng.alloc_register(name)
    return regs[name]


def _dsep(vector, V, n=2):
    for _ in range(n):
        vector.reduce_sum(out=V["s_stat"][:, 7:8], in_=V["s_lnb"][:, 0:1024],
                          axis=AX.X)


def _barrier(kb, block, V, slot):
    bar = kb.new_dedicated()

    @block.gpsimd
    def _(g):
        member = _getreg(g, V, "g_member")
        g.load(member, V["s_meta"][0:1, 0:1])
        with g.If_eq(member, 0):
            g.dma_start(out=V["flags_sh"][0:1, slot:slot + 1],
                        in_=V["s_meta"][0:1, 1:2]).then_inc(bar.h, bar.inc(16))
        with g.Else():
            g.dma_start(out=V["flags_sh"][1:2, slot:slot + 1],
                        in_=V["s_meta"][0:1, 1:2]).then_inc(bar.h, 16)
        g.wait_ge(bar.h, bar.t(16))
        nonce = _getreg(g, V, "g_nonce")
        g.load(nonce, V["s_meta"][0:1, 1:2])
        cnt = _getreg(g, V, "g_cnt")
        g.reg_mov(cnt, bar.t(16))
        val = _getreg(g, V, "g_val")
        go = _getreg(g, V, "g_go")
        g.reg_mov(go, 1)
        with g.While(go):
            with g.If_eq(member, 0):
                g.dma_start(out=V["s_flag"][0:1, slot:slot + 1],
                            in_=V["flags_sh"][1:2, slot:slot + 1]).then_inc(bar.h, 16)
            with g.Else():
                g.dma_start(out=V["s_flag"][0:1, slot:slot + 1],
                            in_=V["flags_sh"][0:1, slot:slot + 1]).then_inc(bar.h, 16)
            g.reg_add(cnt, cnt, 16)
            g.wait_ge(bar.h, cnt)
            g.load(val, V["s_flag"][0:1, slot:slot + 1])
            g.reg_alu(go, val, nonce, ALU.not_equal)

    _fence(kb, block, V)


def build_program(stages=9, dbg=False):
    kb = KB()
    nc = kb.nc
    es = kb.es

    def din(name, shape, dt=F32):
        return nc.dram_tensor(name, shape, dt, kind="ExternalInput").ap()

    def dint(name, shape, dt, shared=False):
        if shared:
            return nc.dram_tensor(name, shape, dt, addr_space="Shared").ap()
        return nc.dram_tensor(name, shape, dt).ap()

    x_in = din("x_b", [T, D])
    codes_in = {"g": din("g_codesT_h", [D, DH], I32),
                "u": din("u_codesT_h", [D, DH], I32),
                "d": din("d_codesT_h", [DH, D], I32)}
    am_in = {"g": din("g_absmaxT_h", [64, DH]),
             "u": din("u_absmaxT_h", [64, DH]),
             "d": din("d_absmaxT_h", [86, D])}
    la_in = {"g": din("g_lora_a", [D, R]),
             "u": din("u_lora_a", [D, R]),
             "d": din("d_lora_a_h", [DH, R])}
    lb_in = {"g": din("g_lora_b_h", [R, DH]),
             "u": din("u_lora_b_h", [R, DH]),
             "d": din("d_lora_b", [R, D])}
    wq_in = din("wqT", [D, D])
    wk_in = din("wkT_h", [D, 512])
    wv_in = din("wvT_h", [D, 512])
    wo_in = din("woT", [D, D])
    cosk_in = din("cosk4", [T, 512])
    sink_in = din("sink4", [T, 512])
    cosq_in = din("cosq4", [QBLK, 512])
    sinq_in = din("sinq4", [QBLK, 512])
    mask_in = din("mask01", [QBLK, T], BF16)
    identb_in = din("identb", [128, 128], BF16)
    identf_in = din("identf", [128, 128], F32)
    meta_in = din("meta", [1, 8], I32)

    out_x3 = nc.dram_tensor("out_x3", [QBLK, D], F32, kind="ExternalOutput").ap()
    if dbg:
        dbg_w = nc.dram_tensor("dbg_w", [256, DH], BF16, kind="ExternalOutput").ap()
        dbg_x1 = nc.dram_tensor("dbg_x1", [T, D], F32, kind="ExternalOutput").ap()
        dbg_pa = [nc.dram_tensor(f"dbg_pa{m}", [T, D], F32, kind="ExternalOutput").ap()
                  for m in range(2)]
        dbg_ab1 = nc.dram_tensor("dbg_ab1", [128, NDH * CHK], BF16, kind="ExternalOutput").ap()
        dbg_ab2 = nc.dram_tensor("dbg_ab2", [128, NID * CHK], BF16, kind="ExternalOutput").ap()
        dbg_P = nc.dram_tensor("dbg_P", [128, 16384], BF16, kind="ExternalOutput").ap()
        dbg_mn = nc.dram_tensor("dbg_mn", [128, 48], F32, kind="ExternalOutput").ap()
        dbg_k = nc.dram_tensor("dbg_k", [KVW, T], BF16, kind="ExternalOutput").ap()
        dbg_v = nc.dram_tensor("dbg_v", [T, KVW], BF16, kind="ExternalOutput").ap()
        dbg_q = nc.dram_tensor("dbg_q", [D, QBLK], BF16, kind="ExternalOutput").ap()
        dbg_x2 = nc.dram_tensor("dbg_x2", [QBLK, D], F32, kind="ExternalOutput").ap()

    w_cache = {"g": dint("wg_cache", [D, DH], BF16),
               "u": dint("wu_cache", [D, DH], BF16),
               "d": dint("wd_cache", [DH, D], BF16)}
    lb_cache = dint("lb_cache", [16, 2 * DH + D], BF16)
    wkv_cache = dint("wkv_cache", [D, 1024], BF16)
    wq_cache = dint("wq_cache", [D, D], BF16)
    wo_cache = dint("wo_cache", [D, D], BF16)
    q_fm = dint("q_fm", [D, QBLK], BF16)
    x1own_dram = dint("x1own", [QBLK, D], F32)
    x2_dram = dint("x2_dram", [QBLK, D], F32)

    pa_sh = [dint(f"pa_sh{m}", [T, D], F32, shared=True) for m in range(2)]
    x1_sh = dint("x1_sh", [T, D], F32, shared=True)
    k_sh = dint("k_sh", [KVW, T], BF16, shared=True)
    v_sh = dint("v_sh", [T, KVW], BF16, shared=True)
    ln3_sh = dint("ln3_sh", [4 * NID * 128, CHK], BF16, shared=True)
    pc_sh = [dint(f"pc_sh{m}", [2 * QBLK, D], F32, shared=True) for m in range(2)]
    flags_sh = dint("flags_sh", [2, 16], I32, shared=True)

    def sb(name, shape, dt):
        return es.enter_context(nc.sbuf_tensor(name, shape, dt))

    ab1 = sb("ab1", [128, NDH * CHK], BF16)
    ab2 = sb("ab2", [128, NID * CHK], BF16)
    s_codes = [sb(f"s_codes{i}", [128, 512], I32) for i in range(2)]
    s_du = sb("s_du", [128, 512], F32)
    s_ds = sb("s_ds", [128, 512], F32)
    s_dz = sb("s_dz", [128, 512], F32)
    s_pe = [sb(f"s_pe{i}", [128, 512], F32) for i in range(4)]
    s_po = [sb(f"s_po{i}", [128, 512], F32) for i in range(4)]
    s_te = sb("s_te", [128, 512], F32)
    s_to = sb("s_to", [128, 512], F32)
    s_wout = [sb(f"s_wout{i}", [128, 512], BF16) for i in range(2)]
    s_arow = [sb(f"s_arow{i}", [1, 512], F32) for i in range(2)]
    s_brow = [sb(f"s_brow{i}", [1, 512], F32) for i in range(2)]
    s_onesf = sb("s_onesf", [1, 64], F32)
    s_xt = sb("s_xt", [128, 2048], F32)
    s_p0 = sb("s_p0", [128, 2048], F32)
    s_p1 = sb("s_p1", [128, 2048], F32)
    s_ln = sb("s_ln", [128, 2048], F32)
    s_lnb = sb("s_lnb", [128, D], BF16)
    s_stat = sb("s_stat", [128, 8], F32)
    s_wg = [sb(f"s_wg{i}", [128, 512], BF16) for i in range(2)]
    s_wb = [sb(f"s_wb{i}", [128, 512], BF16) for i in range(2)]
    s_lbt = [sb(f"s_lbt{i}", [16, 512], BF16) for i in range(2)]
    s_y1 = sb("s_y1", [128, 2048], BF16)
    s_ev = [sb(f"s_ev{i}", [128, 512], F32) for i in range(2)]
    s_ptm = sb("s_ptm", [128, 2048], F32)
    s_la = sb("s_la", [128, (2 * NID + NDH) * R], BF16)
    s_t1 = {"g": sb("s_t1g", [16, 512], BF16),
            "u": sb("s_t1u", [16, 512], BF16),
            "d": sb("s_t1d", [16, 512], BF16)}
    s_rot = sb("s_rot", [128, 512], F32)
    s_ra = sb("s_ra", [128, 512], F32)
    s_kfm = sb("s_kfm", [128, T], BF16)
    s_qh = sb("s_qh", [128, CHK], BF16)
    s_ident = sb("s_ident", [128, 128], BF16)
    s_identf = sb("s_identf", [128, 128], F32)
    s_pt = sb("s_pt", [128, CHK], BF16)
    s_mask = [sb(f"s_mask{i}", [128, 2048], BF16) for i in range(2)]
    s_mneg = sb("s_mneg", [128, 40], F32)
    s_sig = sb("s_sig", [128, 8], F32)
    s_rcp = sb("s_rcp", [128, 8], F32)
    s_flag = sb("s_flag", [1, 8], I32)
    s_meta = sb("s_meta", [1, 8], I32)

    def ps(name, shape, dt):
        return es.enter_context(nc.psum_tensor(name, shape, dt))

    p_mm = [ps(f"p_mm{i}", [128, 512], F32) for i in range(4)]
    p_ma = [ps(f"p_ma{i}", [128, 512], F32) for i in range(2)]
    p_tr = ps("p_tr", [128, 512], F32)
    p_trb = ps("p_trb", [128, 1024], BF16)

    deq_tiles = []
    for mat in ("g", "u", "d"):
        nit = NID if mat != "d" else NDH
        width = DH if mat != "d" else D
        for it in range(nit):
            for oc in range((width + 511) // 512):
                deq_tiles.append((mat, it, oc, min(512, width - oc * 512)))

    V = dict(locals())
    V["dbg"] = dbg

    with es:
        with nc.Block() as block:
            _build_all(kb, block, V, stages, dbg)
    return nc



def _ln_to_fm(kb, block, V, src_fn, n_tt):
    ab2 = V["ab2"]
    lx = kb.new_sem()
    ctr = kb.new_sem()
    acc = kb.new_sem()
    tT = kb.new_sem()
    fm = kb.new_sem()

    @block.sync
    def _(sync):
        for tt in range(n_tt):
            for half in range(2):
                if half == 0 and tt > 0:
                    sync.wait_ge(ctr.h, ctr.t(5 * tt - 2))
                sync.dma_start(out=(V["s_xt"] if half == 0 else V["s_p0"])[:, :],
                               in_=src_fn(tt, half)).then_inc(lx.h, lx.inc(16))

    @block.scalar
    def _(scalar):
        for tt in range(n_tt):
            scalar.wait_ge(ctr.h, ctr.t(5 * tt + 2))
            scalar.activation(out=V["s_xt"][:, :], in_=V["s_ln"][:, :],
                              func=AF.Square, accum_out=V["s_stat"][:, 2:3])
            scalar.activation(out=V["s_p0"][:, :], in_=V["s_p1"][:, :],
                              func=AF.Square, accum_out=V["s_stat"][:, 3:4]) \
                .then_inc(acc.h, acc.inc(1))
            scalar.wait_ge(ctr.h, ctr.t(5 * tt + 3))
            scalar.activation(out=V["s_stat"][:, 5:6], in_=V["s_stat"][:, 4:5],
                              func=AF.Ln)
            scalar.activation(out=V["s_y1"][:, 0:512], in_=V["s_y1"][:, 0:512],
                              func=AF.Copy)
            scalar.activation(out=V["s_stat"][:, 6:7], in_=V["s_stat"][:, 5:6],
                              func=AF.Exp, scale=-0.5).then_inc(acc.h, acc.inc(1))

    @block.tensor
    def _(tensor):
        for tt in range(n_tt):
            tensor.wait_ge(ctr.h, ctr.t(5 * tt + 5))
            for ig in range(4):
                g = tt * 4 + ig
                if g >= 1:
                    tensor.wait_ge(fm.h, fm.t(g))
                for i8 in range(8):
                    i = ig * 8 + i8
                    ins = tensor.transpose(out=V["p_trb"][:, i8 * 128:(i8 + 1) * 128],
                                           in_=V["s_lnb"][:, i * 128:(i + 1) * 128],
                                           identity=V["s_ident"][:, :])
                ins.then_inc(tT.h, tT.inc(1))

    @block.vector
    def _(vector):
        for tt in range(n_tt):
            vector.wait_ge(lx.h, lx.t((2 * tt + 2) * 16))
            vector.reduce_sum(out=V["s_stat"][:, 0:1], in_=V["s_xt"][:, :], axis=AX.X)
            vector.reduce_sum(out=V["s_stat"][:, 1:2], in_=V["s_p0"][:, :], axis=AX.X)
            _dsep(vector, V)
            vector.tensor_add(out=V["s_stat"][:, 0:1], in0=V["s_stat"][:, 0:1],
                              in1=V["s_stat"][:, 1:2])
            _dsep(vector, V)
            vector.tensor_scalar(out=V["s_stat"][:, 1:2], in0=V["s_stat"][:, 0:1],
                                 scalar1=1.0 / D, scalar2=None, op0=ALU.mult)
            _dsep(vector, V)
            vector.tensor_scalar(out=V["s_ln"][:, :], in0=V["s_xt"][:, :],
                                 scalar1=V["s_stat"][:, 1:2], scalar2=None,
                                 op0=ALU.subtract).then_inc(ctr.h, ctr.inc(1))
            vector.tensor_scalar(out=V["s_p1"][:, :], in0=V["s_p0"][:, :],
                                 scalar1=V["s_stat"][:, 1:2], scalar2=None,
                                 op0=ALU.subtract).then_inc(ctr.h, ctr.inc(1))
            vector.wait_ge(acc.h, acc.t(2 * tt + 1))
            vector.tensor_add(out=V["s_stat"][:, 4:5], in0=V["s_stat"][:, 2:3],
                              in1=V["s_stat"][:, 3:4])
            _dsep(vector, V)
            vector.tensor_scalar(out=V["s_stat"][:, 4:5], in0=V["s_stat"][:, 4:5],
                                 scalar1=1.0 / D, scalar2=EPS,
                                 op0=ALU.mult, op1=ALU.add).then_inc(ctr.h, ctr.inc(1))
            vector.wait_ge(acc.h, acc.t(2 * tt + 2))
            if tt > 0:
                vector.wait_ge(tT.h, tT.t(tt * 4))
            vector.tensor_scalar(out=V["s_lnb"][:, 0:2048], in0=V["s_ln"][:, :],
                                 scalar1=V["s_stat"][:, 6:7], scalar2=None,
                                 op0=ALU.mult).then_inc(ctr.h, ctr.inc(1))
            vector.tensor_scalar(out=V["s_lnb"][:, 2048:4096], in0=V["s_p1"][:, :],
                                 scalar1=V["s_stat"][:, 6:7], scalar2=None,
                                 op0=ALU.mult).then_inc(ctr.h, ctr.inc(1))
            for ig in range(4):
                vector.wait_ge(tT.h, tT.t(tt * 4 + ig + 1))
                vector.tensor_copy(
                    out=ab2[:, :].rearrange("p (i t) -> p i t", t=CHK)
                        [:, ig * 8:(ig + 1) * 8, tt * 128:(tt + 1) * 128],
                    in_=V["p_trb"][:, :].rearrange("p (a b) -> p a b", b=128)) \
                    .then_inc(fm.h, fm.inc(1))

    return fm



def _ffn(kb, block, V, phase):
    ab1, ab2 = V["ab1"], V["ab2"]
    p_mm, p_tr = V["p_mm"], V["p_tr"]
    ntc = 8 if phase == "A" else 4
    psh = V["pa_sh"] if phase == "A" else V["pc_sh"]
    NOC = 11
    ow = [512] * 10 + [384]
    NDC = 8

    for tc in range(ntc):
        t0 = tc * CHK
        if phase == "A":
            fm = _ln_to_fm(kb, block, V,
                           lambda tt, half, t0=t0: V["x_in"][t0 + tt * 128:t0 + (tt + 1) * 128,
                                                             half * 2048:(half + 1) * 2048],
                           NTT)
            fm_target = 4 * NTT
        else:
            fm = kb.new_sem()

            @block.sync
            def _(sync, tc=tc):
                for g8 in range(8):
                    sync.dma_start(
                        out=ab2[:, g8 * 2048:(g8 + 1) * 2048]
                            .rearrange("p (i t) -> p i t", t=CHK),
                        in_=V["ln3_sh"][(tc * NID + g8 * 4) * 128:(tc * NID + (g8 + 1) * 4) * 128, :]
                            .rearrange("(i p) t -> p i t", p=128)) \
                        .then_inc(fm.h, fm.inc(16))
            fm_target = 8 * 16
        _fence(kb, block, V)

        lt = kb.new_sem()
        le = kb.new_sem()
        lw = [kb.new_sem() for _ in range(4)]
        lbl = [kb.new_sem(), kb.new_sem()]
        WB4 = [V["s_wg"][0], V["s_wg"][1], V["s_wb"][0], V["s_wb"][1]]
        mmi = kb.new_sem()
        grp = kb.new_sem()
        rel = kb.new_sem()
        x3s = kb.new_sem()

        @block.sync
        def _(sync):
            k = 0
            for oc in range(NOC):
                for mj, mat in enumerate(("g", "u")):
                    kb2 = 2 * oc + mj
                    if kb2 >= 2:
                        sync.wait_ge(grp.h, grp.t(kb2 - 1))
                    lbo = 0 if mat == "g" else DH
                    sync.dma_start(out=V["s_lbt"][kb2 % 2][0:16, 0:ow[oc]],
                                   in_=V["lb_cache"][0:16, lbo + oc * 512:lbo + oc * 512 + ow[oc]]) \
                        .then_inc(lbl[kb2 % 2].h, lbl[kb2 % 2].inc(16))
                    for i in range(NID):
                        if k % 2 == 0:
                            if k >= 4:
                                sync.wait_ge(mmi.h, mmi.t(k - 3))
                            sync.dma_start(out=WB4[k % 4][:, 0:ow[oc]],
                                           in_=V["w_cache"][mat][i * 128:(i + 1) * 128,
                                                                 oc * 512:oc * 512 + ow[oc]]) \
                                .then_inc(lw[k % 4].h, lw[k % 4].inc(16))
                        k += 1

        @block.tensor
        def _(tensor):
            tensor.wait_ge(fm.h, fm.t(fm_target))
            for j, mat in enumerate(("g", "u")):
                for i in range(NID):
                    ins = tensor.matmul(out=p_mm[j][0:16, :],
                                  lhsT=V["s_la"][:, (j * NID + i) * R:(j * NID + i + 1) * R],
                                  rhs=ab2[:, i * CHK:(i + 1) * CHK],
                                  start=(i == 0), stop=(i == NID - 1))
                ins.then_inc(lt.h, lt.inc(1))
            tensor.wait_ge(le.h, le.t(2))
            k = 0
            for oc in range(NOC):
                nsub = (ow[oc] + 127) // 128
                for mj, mat in enumerate(("g", "u")):
                    kb2 = 2 * oc + mj
                    if mj == 0 and oc >= 1:
                        tensor.wait_ge(x3s.h, x3s.t(oc))
                    for i in range(NID):
                        tensor.wait_ge(lw[k % 4].h, lw[k % 4].t((k // 4 + 1) * 16))
                        for sub in range(nsub):
                            ins = tensor.matmul(out=p_mm[sub][:, :],
                                          lhsT=WB4[k % 4][:, sub * 128:(sub + 1) * 128],
                                          rhs=ab2[:, i * CHK:(i + 1) * CHK],
                                          start=(i == 0), stop=False)
                        ins.then_inc(mmi.h, mmi.inc(1))
                        k += 1
                    tensor.wait_ge(lbl[kb2 % 2].h, lbl[kb2 % 2].t((kb2 // 2 + 1) * 16))
                    for sub in range(nsub):
                        ins = tensor.matmul(out=p_mm[sub][:, :],
                                      lhsT=V["s_lbt"][kb2 % 2][0:16, sub * 128:(sub + 1) * 128],
                                      rhs=V["s_t1"][mat][0:16, :],
                                      start=False, stop=True)
                    ins.then_inc(grp.h, grp.inc(1))
                    if mj == 0:
                        tensor.wait_ge(rel.h, rel.t(oc + 1))

        @block.scalar
        def _(scalar):
            k = 0
            for oc in range(NOC):
                nsub = (ow[oc] + 127) // 128
                for mj, mat in enumerate(("g", "u")):
                    for i in range(NID):
                        if k % 2 == 1:
                            if k >= 4:
                                scalar.wait_ge(mmi.h, mmi.t(k - 3))
                            scalar.dma_start(out=WB4[k % 4][:, 0:ow[oc]],
                                             in_=V["w_cache"][mat][i * 128:(i + 1) * 128,
                                                                   oc * 512:oc * 512 + ow[oc]]) \
                                .then_inc(lw[k % 4].h, lw[k % 4].inc(16))
                        k += 1
                    if mj == 0:
                        scalar.wait_ge(grp.h, grp.t(2 * oc + 1))
                        if oc >= 1:
                            scalar.wait_ge(x3s.h, x3s.t(oc))
                        for sub in range(nsub):
                            ins = scalar.activation(out=V["s_y1"][:, sub * 512:sub * 512 + 512],
                                                    in_=p_mm[sub][:, :], func=AF.Relu)
                        ins.then_inc(rel.h, rel.inc(1))

        @block.vector
        def _(vector):
            for j, mat in enumerate(("g", "u")):
                vector.wait_ge(lt.h, lt.t(j + 1))
                vector.tensor_copy(out=V["s_t1"][mat][0:16, :],
                                   in_=p_mm[j][0:16, :]).then_inc(le.h, le.inc(1))
            for oc in range(NOC):
                nsub = (ow[oc] + 127) // 128
                vector.wait_ge(grp.h, grp.t(2 * oc + 2))
                vector.wait_ge(rel.h, rel.t(oc + 1))
                for sub in range(nsub):
                    ot = oc * 4 + sub
                    ins = vector.tensor_mul(out=ab1[:, ot * CHK:(ot + 1) * CHK],
                                            in0=V["s_y1"][:, sub * 512:sub * 512 + 512],
                                            in1=p_mm[sub][:, :])
                ins.then_inc(x3s.h, x3s.inc(1))

        ltd = kb.new_sem()
        led = kb.new_sem()
        lwd = [kb.new_sem() for _ in range(4)]
        lbd = [kb.new_sem(), kb.new_sem()]
        mdi = kb.new_sem()
        mdd = kb.new_sem()
        evd = kb.new_sem()
        trd = kb.new_sem()
        ptd = kb.new_sem()
        std = kb.new_sem()

        @block.sync
        def _(sync):
            k = 0
            for dc in range(NDC):
                if dc < 2:
                    sync.wait_ge(grp.h, grp.t(2 * NOC))
                if dc >= 2:
                    sync.wait_ge(mdd.h, mdd.t(dc - 1))
                sync.dma_start(out=V["s_lbt"][dc % 2][0:16, :],
                               in_=V["lb_cache"][0:16, 2 * DH + dc * 512:2 * DH + (dc + 1) * 512]) \
                    .then_inc(lbd[dc % 2].h, lbd[dc % 2].inc(16))
                for i in range(NDH):
                    if k % 2 == 0:
                        if k >= 4:
                            sync.wait_ge(mdi.h, mdi.t(k - 3))
                        sync.dma_start(out=WB4[k % 4][:, :],
                                       in_=V["w_cache"]["d"][i * 128:(i + 1) * 128,
                                                             dc * 512:(dc + 1) * 512]) \
                            .then_inc(lwd[k % 4].h, lwd[k % 4].inc(16))
                    k += 1

        @block.scalar
        def _(scalar):
            scalar.wait_ge(grp.h, grp.t(2 * NOC))
            k = 0
            for dc in range(NDC):
                for i in range(NDH):
                    if k % 2 == 1:
                        if k >= 4:
                            scalar.wait_ge(mdi.h, mdi.t(k - 3))
                        scalar.dma_start(out=WB4[k % 4][:, :],
                                         in_=V["w_cache"]["d"][i * 128:(i + 1) * 128,
                                                               dc * 512:(dc + 1) * 512]) \
                            .then_inc(lwd[k % 4].h, lwd[k % 4].inc(16))
                    k += 1

        @block.tensor
        def _(tensor):
            tensor.wait_ge(x3s.h, x3s.t(NOC))
            for i in range(NDH):
                ins = tensor.matmul(out=p_mm[0][0:16, :],
                              lhsT=V["s_la"][:, (2 * NID + i) * R:(2 * NID + i + 1) * R],
                              rhs=ab1[:, i * CHK:(i + 1) * CHK],
                              start=(i == 0), stop=(i == NDH - 1))
            ins.then_inc(ltd.h, ltd.inc(1))
            tensor.wait_ge(led.h, led.t(1))
            k = 0
            for dc in range(NDC):
                if dc >= 1:
                    tensor.wait_ge(evd.h, evd.t(4 * dc))
                for i in range(NDH):
                    tensor.wait_ge(lwd[k % 4].h, lwd[k % 4].t((k // 4 + 1) * 16))
                    for sub in range(4):
                        ins = tensor.matmul(out=p_mm[sub][:, :],
                                      lhsT=WB4[k % 4][:, sub * 128:(sub + 1) * 128],
                                      rhs=ab1[:, i * CHK:(i + 1) * CHK],
                                      start=(i == 0), stop=False)
                    ins.then_inc(mdi.h, mdi.inc(1))
                    k += 1
                tensor.wait_ge(lbd[dc % 2].h, lbd[dc % 2].t((dc // 2 + 1) * 16))
                for sub in range(4):
                    ins = tensor.matmul(out=p_mm[sub][:, :],
                                  lhsT=V["s_lbt"][dc % 2][0:16, sub * 128:(sub + 1) * 128],
                                  rhs=V["s_t1"]["d"][0:16, :],
                                  start=False, stop=True)
                ins.then_inc(mdd.h, mdd.inc(1))
                for sub in range(4):
                    dt = 4 * dc + sub
                    tensor.wait_ge(evd.h, evd.t(dt + 1))
                    if dt >= 1:
                        tensor.wait_ge(ptd.h, ptd.t(dt))
                    for j in range(4):
                        ins = tensor.transpose(out=p_tr[:, j * 128:(j + 1) * 128],
                                         in_=V["s_ev"][sub % 2][:, j * 128:(j + 1) * 128],
                                         identity=V["s_identf"][:, :])
                    ins.then_inc(trd.h, trd.inc(1))

        @block.vector
        def _(vector):
            vector.wait_ge(ltd.h, ltd.t(1))
            vector.tensor_copy(out=V["s_t1"]["d"][0:16, :],
                               in_=p_mm[0][0:16, :]).then_inc(led.h, led.inc(1))
            for dc in range(NDC):
                vector.wait_ge(mdd.h, mdd.t(dc + 1))
                for sub in range(4):
                    dt = 4 * dc + sub
                    if dt >= 2:
                        vector.wait_ge(trd.h, trd.t(dt - 1))
                    vector.tensor_copy(out=V["s_ev"][sub % 2][:, :],
                                       in_=p_mm[sub][:, :]).then_inc(evd.h, evd.inc(1))
                    vector.wait_ge(trd.h, trd.t(dt + 1))
                    if dc >= 1 and sub == 0:
                        vector.wait_ge(std.h, std.t(dc * 64))
                    vector.tensor_copy(
                        out=V["s_ptm"][:, :].rearrange("p (j s c) -> p j s c", s=4, c=128)
                            [:, :, sub, :],
                        in_=p_tr[:, :].rearrange("p (j c) -> p j c", c=128)) \
                        .then_inc(ptd.h, ptd.inc(1))

        @block.gpsimd
        def _(gps, tc=tc, t0=t0):
            member = _getreg(gps, V, "g_member")
            gps.load(member, V["s_meta"][0:1, 0:1])
            for dc in range(NDC):
                gps.wait_ge(ptd.h, ptd.t(4 * dc + 4))
                for j in range(4):
                    row = t0 + j * 128
                    src = V["s_ptm"][:, j * 512:(j + 1) * 512]
                    with gps.If_eq(member, 0):
                        gps.dma_start(out=psh[0][row:row + 128, dc * 512:(dc + 1) * 512],
                                      in_=src).then_inc(std.h, std.inc(16))
                    with gps.Else():
                        gps.dma_start(out=psh[1][row:row + 128, dc * 512:(dc + 1) * 512],
                                      in_=src).then_inc(std.h, 16)
            gps.wait_ge(std.h, std.n)
            if V.get("dbg") and phase == "A" and tc == 0:
                gps.dma_start(out=V["dbg_ab1"], in_=ab1[:, :]).then_inc(std.h, std.inc(16))
                gps.dma_start(out=V["dbg_ab2"], in_=ab2[:, :]).then_inc(std.h, std.inc(16))
                gps.wait_ge(std.h, std.n)

        _fence(kb, block, V)



def _qkv(kb, block, V):
    ab2 = V["ab2"]
    p_mm = V["p_mm"]

    for tc in range(8):
        t0 = tc * CHK
        fm = _ln_to_fm(kb, block, V,
                       lambda tt, half, t0=t0: V["x1_sh"][t0 + tt * 128:t0 + (tt + 1) * 128,
                                                          half * 2048:(half + 1) * 2048],
                       NTT)
        _fence(kb, block, V)

        WB4 = [V["s_wg"][0], V["s_wg"][1], V["s_wb"][0], V["s_wb"][1]]
        for j in range(2):
            lw = [kb.new_sem() for _ in range(4)]
            mmi = kb.new_sem()
            kro = kb.new_sem()
            kfe = kb.new_sem()
            ktT = kb.new_sem()
            stk = kb.new_sem()
            csl = kb.new_sem()

            @block.sync
            def _(sync, j=j, t0=t0):
                if j == 0:
                    sync.dma_start(out=V["s_xt"][:, :].rearrange("p (a c) -> p a c", c=512),
                                   in_=V["cosk_in"][t0:t0 + 512, :]
                                   .rearrange("(a p) c -> p a c", p=128)) \
                        .then_inc(csl.h, csl.inc(16))
                    sync.dma_start(out=V["s_p0"][:, :].rearrange("p (a c) -> p a c", c=512),
                                   in_=V["sink_in"][t0:t0 + 512, :]
                                   .rearrange("(a p) c -> p a c", p=128)) \
                        .then_inc(csl.h, csl.inc(16))
                k = 0
                for ttp in range(2):
                    for i in range(NID):
                        if k % 2 == 0:
                            if k >= 4:
                                sync.wait_ge(mmi.h, mmi.t(k - 3))
                            sync.dma_start(out=WB4[k % 4][:, :],
                                           in_=V["wkv_cache"][i * 128:(i + 1) * 128,
                                                              j * 512:(j + 1) * 512]) \
                                .then_inc(lw[k % 4].h, lw[k % 4].inc(16))
                        k += 1

            @block.scalar
            def _(scalar, j=j):
                k = 0
                for ttp in range(2):
                    for i in range(NID):
                        if k % 2 == 1:
                            if k >= 4:
                                scalar.wait_ge(mmi.h, mmi.t(k - 3))
                            scalar.dma_start(out=WB4[k % 4][:, :],
                                             in_=V["wkv_cache"][i * 128:(i + 1) * 128,
                                                                j * 512:(j + 1) * 512]) \
                                .then_inc(lw[k % 4].h, lw[k % 4].inc(16))
                        k += 1

            @block.tensor
            def _(tensor, j=j):
                tensor.wait_ge(fm.h, fm.t(4 * NTT))
                k = 0
                for ttp in range(2):
                    if ttp == 1:
                        tensor.wait_ge(kro.h, kro.t(2))
                    for i in range(NID):
                        tensor.wait_ge(lw[k % 4].h, lw[k % 4].t((k // 4 + 1) * 16))
                        for h2 in range(2):
                            tt = ttp * 2 + h2
                            ins = tensor.matmul(out=p_mm[h2][:, :],
                                          lhsT=ab2[:, i * CHK + tt * 128:i * CHK + (tt + 1) * 128],
                                          rhs=WB4[k % 4][:, :],
                                          start=(i == 0), stop=(i == NID - 1))
                        ins.then_inc(mmi.h, mmi.inc(1))
                        k += 1
                    if j == 0:
                        for h2 in range(2):
                            tt = ttp * 2 + h2
                            tensor.wait_ge(kro.h, kro.t(tt + 1))
                            if tt >= 1:
                                tensor.wait_ge(kfe.h, kfe.t(tt))
                            for sub in range(4):
                                ins = tensor.transpose(out=V["p_trb"][:, sub * 128:(sub + 1) * 128],
                                                 in_=V["s_lnb"][:, sub * 128:(sub + 1) * 128],
                                                 identity=V["s_ident"][:, :])
                            ins.then_inc(ktT.h, ktT.inc(1))

            @block.vector
            def _(vector, j=j):
                if j == 0:
                    vector.wait_ge(csl.h, csl.t(32))
                for ttp in range(2):
                    vector.wait_ge(mmi.h, mmi.t((ttp + 1) * 32))
                    for h2 in range(2):
                        tt = ttp * 2 + h2
                        if j == 1:
                            if tt >= 1:
                                vector.wait_ge(stk.h, stk.t(tt * 16))
                            vector.tensor_copy(out=V["s_lnb"][:, 0:512],
                                               in_=p_mm[h2][:, :]) \
                                .then_inc(kro.h, kro.inc(1))
                        else:
                            pm3 = p_mm[h2][:, :].rearrange("p (a c) -> p a c", c=128)
                            rt3 = V["s_rot"][:, :].rearrange("p (a c) -> p a c", c=128)
                            vector.tensor_scalar(out=rt3[:, :, 0:64], in0=pm3[:, :, 64:128],
                                                 scalar1=-1.0, scalar2=None, op0=ALU.mult)
                            vector.tensor_copy(out=rt3[:, :, 64:128], in_=pm3[:, :, 0:64])
                            vector.tensor_mul(out=V["s_ra"][:, :], in0=p_mm[h2][:, :],
                                              in1=V["s_xt"][:, tt * 512:(tt + 1) * 512])
                            vector.tensor_mul(out=V["s_rot"][:, :], in0=V["s_rot"][:, :],
                                              in1=V["s_p0"][:, tt * 512:(tt + 1) * 512])
                            if tt >= 1:
                                vector.wait_ge(ktT.h, ktT.t(tt))
                            vector.tensor_add(out=V["s_lnb"][:, 0:512], in0=V["s_ra"][:, :],
                                              in1=V["s_rot"][:, :]).then_inc(kro.h, kro.inc(1))
                            vector.wait_ge(ktT.h, ktT.t(tt + 1))
                            vector.tensor_copy(
                                out=V["s_kfm"][:, 0:2048].rearrange("p (s t) -> p s t", t=512)
                                    [:, :, tt * 128:(tt + 1) * 128],
                                in_=V["p_trb"][:, 0:512].rearrange("p (s c) -> p s c", c=128)) \
                                .then_inc(kfe.h, kfe.inc(1))

            @block.gpsimd
            def _(gps, j=j, t0=t0):
                gmem = _getreg(gps, V, "g_member")
                gps.load(gmem, V["s_meta"][0:1, 0:1])
                if j == 0:
                    gps.wait_ge(kfe.h, kfe.t(4))
                    for sub in range(4):
                        with gps.If_eq(gmem, 0):
                            gps.dma_start(out=V["k_sh"][sub * 128:(sub + 1) * 128, t0:t0 + 512],
                                          in_=V["s_kfm"][:, sub * 512:(sub + 1) * 512]) \
                                .then_inc(stk.h, stk.inc(16))
                        with gps.Else():
                            gps.dma_start(out=V["k_sh"][512 + sub * 128:512 + (sub + 1) * 128,
                                                        t0:t0 + 512],
                                          in_=V["s_kfm"][:, sub * 512:(sub + 1) * 512]) \
                                .then_inc(stk.h, 16)
                else:
                    for tt in range(NTT):
                        gps.wait_ge(kro.h, kro.t(tt + 1))
                        r = t0 + tt * 128
                        with gps.If_eq(gmem, 0):
                            gps.dma_start(out=V["v_sh"][r:r + 128, 0:512],
                                          in_=V["s_lnb"][:, 0:512]).then_inc(stk.h, stk.inc(16))
                        with gps.Else():
                            gps.dma_start(out=V["v_sh"][r:r + 128, 512:1024],
                                          in_=V["s_lnb"][:, 0:512]).then_inc(stk.h, 16)
                gps.wait_ge(stk.h, stk.n)

            _fence(kb, block, V)

    for qc2 in range(2):
        q0 = qc2 * CHK
        fmq = _ln_to_fm(kb, block, V,
                        lambda tt, half, q0=q0: V["x1own_dram"][q0 + tt * 128:q0 + (tt + 1) * 128,
                                                                half * 2048:(half + 1) * 2048],
                        NTT)
        _fence(kb, block, V)

        lwq = [kb.new_sem() for _ in range(4)]
        WB4 = [V["s_wg"][0], V["s_wg"][1], V["s_wb"][0], V["s_wb"][1]]
        mmq = kb.new_sem()
        rpq = kb.new_sem()
        stq = kb.new_sem()
        csl = kb.new_sem()
        qtT = kb.new_sem()
        qfe = kb.new_sem()

        @block.sync
        def _(sync, q0=q0):
            sync.dma_start(out=V["s_xt"][:, :].rearrange("p (a c) -> p a c", c=512),
                           in_=V["cosq_in"][q0:q0 + 512, :]
                           .rearrange("(a p) c -> p a c", p=128)).then_inc(csl.h, csl.inc(16))
            sync.dma_start(out=V["s_p0"][:, :].rearrange("p (a c) -> p a c", c=512),
                           in_=V["sinq_in"][q0:q0 + 512, :]
                           .rearrange("(a p) c -> p a c", p=128)).then_inc(csl.h, csl.inc(16))
            k = 0
            for qc in range(8):
                for ttp in range(2):
                    for i in range(NID):
                        if k % 2 == 0:
                            if k >= 4:
                                sync.wait_ge(mmq.h, mmq.t(k - 3))
                            sync.dma_start(out=WB4[k % 4][:, :],
                                           in_=V["wq_cache"][i * 128:(i + 1) * 128,
                                                             qc * 512:(qc + 1) * 512]) \
                                .then_inc(lwq[k % 4].h, lwq[k % 4].inc(16))
                        k += 1

        @block.scalar
        def _(scalar):
            k = 0
            for qc in range(8):
                for ttp in range(2):
                    for i in range(NID):
                        if k % 2 == 1:
                            if k >= 4:
                                scalar.wait_ge(mmq.h, mmq.t(k - 3))
                            scalar.dma_start(out=WB4[k % 4][:, :],
                                             in_=V["wq_cache"][i * 128:(i + 1) * 128,
                                                               qc * 512:(qc + 1) * 512]) \
                                .then_inc(lwq[k % 4].h, lwq[k % 4].inc(16))
                        k += 1

        @block.tensor
        def _(tensor):
            tensor.wait_ge(fmq.h, fmq.t(4 * NTT))
            k = 0
            for qc in range(8):
                for ttp in range(2):
                    g2 = qc * 2 + ttp
                    if g2 >= 1:
                        tensor.wait_ge(rpq.h, rpq.t(g2 * 2))
                    for i in range(NID):
                        tensor.wait_ge(lwq[k % 4].h, lwq[k % 4].t((k // 4 + 1) * 16))
                        for h2 in range(2):
                            tt = ttp * 2 + h2
                            ins = tensor.matmul(out=p_mm[h2][:, :],
                                          lhsT=ab2[:, i * CHK + tt * 128:i * CHK + (tt + 1) * 128],
                                          rhs=WB4[k % 4][:, :],
                                          start=(i == 0), stop=(i == NID - 1))
                        ins.then_inc(mmq.h, mmq.inc(1))
                        k += 1
                    for h2 in range(2):
                        tt = ttp * 2 + h2
                        gtt = qc * 4 + tt
                        tensor.wait_ge(rpq.h, rpq.t(gtt + 1))
                        if gtt >= 1:
                            tensor.wait_ge(qfe.h, qfe.t(gtt))
                        for sub in range(4):
                            ins = tensor.transpose(out=V["p_trb"][:, sub * 128:(sub + 1) * 128],
                                             in_=V["s_lnb"][:, sub * 128:(sub + 1) * 128],
                                             identity=V["s_ident"][:, :])
                        ins.then_inc(qtT.h, qtT.inc(1))


        @block.vector
        def _(vector):
            vector.wait_ge(csl.h, csl.t(32))
            idx = 0
            for qc in range(8):
                for ttp in range(2):
                    vector.wait_ge(mmq.h, mmq.t((qc * 2 + ttp + 1) * 32))
                    for h2 in range(2):
                        tt = ttp * 2 + h2
                        pm3 = p_mm[h2][:, :].rearrange("p (a c) -> p a c", c=128)
                        rt3 = V["s_rot"][:, :].rearrange("p (a c) -> p a c", c=128)
                        vector.tensor_scalar(out=rt3[:, :, 0:64], in0=pm3[:, :, 64:128],
                                             scalar1=-1.0, scalar2=None, op0=ALU.mult)
                        vector.tensor_copy(out=rt3[:, :, 64:128], in_=pm3[:, :, 0:64])
                        vector.tensor_mul(out=V["s_ra"][:, :], in0=p_mm[h2][:, :],
                                          in1=V["s_xt"][:, tt * 512:(tt + 1) * 512])
                        vector.tensor_mul(out=V["s_rot"][:, :], in0=V["s_rot"][:, :],
                                          in1=V["s_p0"][:, tt * 512:(tt + 1) * 512])
                        if idx >= 1:
                            vector.wait_ge(qtT.h, qtT.t(idx))
                        vector.tensor_add(out=V["s_lnb"][:, 0:512], in0=V["s_ra"][:, :],
                                          in1=V["s_rot"][:, :]).then_inc(rpq.h, rpq.inc(1))
                        vector.wait_ge(qtT.h, qtT.t(idx + 1))
                        if idx >= 4 and idx % 4 == 0:
                            vector.wait_ge(stq.h, stq.t(16 * idx))
                        vector.tensor_copy(
                            out=V["s_kfm"][:, 0:2048].rearrange("p (s t) -> p s t", t=512)
                                [:, :, tt * 128:(tt + 1) * 128],
                            in_=V["p_trb"][:, 0:512].rearrange("p (s c) -> p s c", c=128)) \
                            .then_inc(qfe.h, qfe.inc(1))
                        idx += 1

        @block.gpsimd
        def _(gps, q0=q0):
            for qc in range(8):
                gps.wait_ge(qfe.h, qfe.t((qc + 1) * 4))
                for sub in range(4):
                    gps.dma_start(out=V["q_fm"][qc * 512 + sub * 128:qc * 512 + (sub + 1) * 128,
                                                q0:q0 + 512],
                                  in_=V["s_kfm"][:, sub * 512:(sub + 1) * 512]) \
                        .then_inc(stq.h, stq.inc(16))
            gps.wait_ge(stq.h, stq.n)

        _fence(kb, block, V)



def _attn(kb, block, V):
    ab1, ab2 = V["ab1"], V["ab2"]
    p_mm, p_trb = V["p_mm"], V["p_trb"]

    grp_state = {}
    for qh in range(2):
        q0 = qh * CHK
        for h in range(NH):
            kvh = h // 4
            if h % 4 == 0:
                grp_state["klf"] = kb.new_sem()
                grp_state["vlk"] = kb.new_sem()
            klf = grp_state["klf"]
            vlk = grp_state["vlk"]
            ql = kb.new_sem()
            mm1 = kb.new_sem()
            rxp = kb.new_sem()
            rxf = kb.new_sem()
            mm2 = kb.new_sem()
            exl = kb.new_sem()
            mkl = [kb.new_sem(), kb.new_sem()]
            msk = kb.new_sem()
            sgm = kb.new_sem()
            ptT = kb.new_sem()
            pte = kb.new_sem()
            avm = kb.new_sem()

            @block.sync
            def _(sync, h=h, kvh=kvh, q0=q0):
                if h % 4 == 0:
                    sync.dma_start(out=V["s_kfm"][:, :],
                                   in_=V["k_sh"][kvh * 128:(kvh + 1) * 128, :]) \
                        .then_inc(klf.h, klf.inc(16))
                    for ks in range(32):
                        sync.dma_start(out=ab1[:, 16384 + ks * 128:16384 + (ks + 1) * 128],
                                       in_=V["v_sh"][ks * 128:(ks + 1) * 128,
                                                     kvh * 128:(kvh + 1) * 128]) \
                            .then_inc(vlk.h, vlk.inc(16))
                sync.dma_start(out=V["s_qh"][:, :],
                               in_=V["q_fm"][h * 128:(h + 1) * 128, q0:q0 + 512]) \
                    .then_inc(ql.h, ql.inc(16))
                for k in range(8):
                    qt, half = k // 2, k % 2
                    if k >= 2:
                        sync.wait_ge(msk.h, msk.t(k - 1))
                    sync.dma_start(out=V["s_mask"][k % 2][:, :],
                                   in_=V["mask_in"][q0 + qt * 128:q0 + (qt + 1) * 128,
                                                    half * 2048:(half + 1) * 2048]) \
                        .then_inc(mkl[k % 2].h, mkl[k % 2].inc(16))

            @block.tensor
            def _(tensor, h=h):
                tensor.wait_ge(ql.h, ql.t(16))
                tensor.wait_ge(klf.h, klf.t(16))
                for qt in range(4):
                    for uh in range(8):
                        idx = qt * 8 + uh
                        if idx >= 2:
                            tensor.wait_ge(rxp.h, rxp.t(idx - 1))
                        tensor.matmul(out=p_mm[uh % 2][:, :],
                                      lhsT=V["s_qh"][:, qt * 128:(qt + 1) * 128],
                                      rhs=V["s_kfm"][:, uh * 512:(uh + 1) * 512],
                                      start=True, stop=True) \
                            .then_inc(mm1.h, mm1.inc(1))
                tensor.wait_ge(rxf.h, rxf.t(4))
                for qt in range(4):
                    for uh in range(8):
                        idx = qt * 8 + uh
                        if idx >= 2:
                            tensor.wait_ge(exl.h, exl.t(idx - 1))
                        tensor.matmul(out=p_mm[uh % 2][:, :],
                                      lhsT=V["s_qh"][:, qt * 128:(qt + 1) * 128],
                                      rhs=V["s_kfm"][:, uh * 512:(uh + 1) * 512],
                                      start=True, stop=True) \
                            .then_inc(mm2.h, mm2.inc(1))
                tensor.wait_ge(sgm.h, sgm.t(5))
                if h % 4 == 0:
                    tensor.wait_ge(vlk.h, vlk.t(512))
                for ks in range(32):
                    if ks >= 1:
                        tensor.wait_ge(pte.h, pte.t(ks))
                    for qt in range(4):
                        ins = tensor.transpose(out=p_trb[:, qt * 128:(qt + 1) * 128],
                                         in_=ab1[:, qt * 4096 + ks * 128:qt * 4096 + ks * 128 + 128],
                                         identity=V["s_ident"][:, :])
                    ins.then_inc(ptT.h, ptT.inc(1))
                    tensor.wait_ge(pte.h, pte.t(ks + 1))
                    tensor.matmul(out=p_mm[2][:, :],
                                  lhsT=ab1[:, 16384 + ks * 128:16384 + (ks + 1) * 128],
                                  rhs=V["s_pt"][:, :],
                                  start=(ks == 0), stop=(ks == 31)) \
                        .then_inc(avm.h, avm.inc(1))

            @block.scalar
            def _(scalar, h=h):
                for qt in range(4):
                    for uh in range(8):
                        idx = qt * 8 + uh
                        scalar.wait_ge(mm2.h, mm2.t(idx + 1))
                        scalar.activation(
                            out=ab1[:, qt * 4096 + uh * 512:qt * 4096 + (uh + 1) * 512],
                            in_=p_mm[uh % 2][:, :], func=AF.Exp,
                            bias=V["s_mneg"][:, qt:qt + 1]).then_inc(exl.h, exl.inc(1))

            @block.vector
            def _(vector, h=h):
                for qt in range(4):
                    for uh in range(8):
                        c = 8 + qt * 8 + uh
                        vector.wait_ge(mm1.h, mm1.t(qt * 8 + uh + 1))
                        vector.reduce_max(out=V["s_mneg"][:, c:c + 1],
                                          in_=p_mm[uh % 2][:, :], axis=AX.X) \
                            .then_inc(rxp.h, rxp.inc(1))
                _dsep(vector, V)
                for qt in range(4):
                    vector.reduce_max(out=V["s_mneg"][:, qt:qt + 1],
                                      in_=V["s_mneg"][:, 8 + qt * 8:16 + qt * 8],
                                      axis=AX.X, negate=True) \
                        .then_inc(rxf.h, rxf.inc(1))
                for qt in range(4):
                    for half in range(2):
                        k = qt * 2 + half
                        vector.wait_ge(mkl[k % 2].h, mkl[k % 2].t((k // 2 + 1) * 16))
                        for u4 in range(4):
                            uh = half * 4 + u4
                            vector.wait_ge(exl.h, exl.t(qt * 8 + uh + 1))
                            ins = vector.tensor_mul(
                                out=ab1[:, qt * 4096 + uh * 512:qt * 4096 + (uh + 1) * 512],
                                in0=ab1[:, qt * 4096 + uh * 512:qt * 4096 + (uh + 1) * 512],
                                in1=V["s_mask"][k % 2][:, u4 * 512:(u4 + 1) * 512])
                        ins.then_inc(msk.h, msk.inc(1))
                    vector.reduce_sum(out=V["s_sig"][:, qt:qt + 1],
                                      in_=ab1[:, qt * 4096:(qt + 1) * 4096], axis=AX.X) \
                        .then_inc(sgm.h, sgm.inc(1))
                _dsep(vector, V)
                vector.reciprocal(out=V["s_rcp"][:, 0:4], in_=V["s_sig"][:, 0:4])
                _dsep(vector, V)
                for qt in range(4):
                    ins = vector.tensor_scalar(out=ab1[:, qt * 4096:(qt + 1) * 4096],
                                         in0=ab1[:, qt * 4096:(qt + 1) * 4096],
                                         scalar1=V["s_rcp"][:, qt:qt + 1], scalar2=None,
                                         op0=ALU.mult)
                ins.then_inc(sgm.h, sgm.inc(1))
                for ks in range(32):
                    vector.wait_ge(ptT.h, ptT.t(ks + 1))
                    if ks >= 1:
                        vector.wait_ge(avm.h, avm.t(ks))
                    vector.tensor_copy(out=V["s_pt"][:, :], in_=p_trb[:, 0:512]) \
                        .then_inc(pte.h, pte.inc(1))
                vector.wait_ge(avm.h, avm.t(32))
                vector.tensor_copy(out=ab2[:, h * 512:(h + 1) * 512], in_=p_mm[2][:, :])

            if V.get("dbg") and qh == 0 and h == 0:
                dmp = kb.new_sem()

                @block.gpsimd
                def _(gps):
                    gps.wait_ge(sgm.h, sgm.t(5))
                    for c4 in range(4):
                        gps.dma_start(out=V["dbg_P"][:, c4 * 4096:(c4 + 1) * 4096],
                                      in_=ab1[:, c4 * 4096:(c4 + 1) * 4096]) \
                            .then_inc(dmp.h, dmp.inc(16))
                    gps.dma_start(out=V["dbg_mn"][:, 0:40], in_=V["s_mneg"][:, :]) \
                        .then_inc(dmp.h, dmp.inc(16))
                    gps.dma_start(out=V["dbg_mn"][:, 40:44], in_=V["s_sig"][:, 0:4]) \
                        .then_inc(dmp.h, dmp.inc(16))
                    gps.dma_start(out=V["dbg_mn"][:, 44:48], in_=V["s_rcp"][:, 0:4]) \
                        .then_inc(dmp.h, dmp.inc(16))
                    gps.wait_ge(dmp.h, dmp.n)

            _fence(kb, block, V)

        lwo = [kb.new_sem() for _ in range(4)]
        WB4 = [V["s_wg"][0], V["s_wg"][1], V["s_wb"][0], V["s_wb"][1]]
        wti = kb.new_sem()
        x1l = [kb.new_sem(), kb.new_sem()]
        ewo = kb.new_sem()
        swo = kb.new_sem()

        @block.sync
        def _(sync, q0=q0):
            k = 0
            for dc in range(8):
                if dc >= 2:
                    sync.wait_ge(ewo.h, ewo.t((dc - 1) * 4))
                xbuf = V["s_xt"] if dc % 2 == 0 else V["s_p0"]
                for qt in range(4):
                    sync.dma_start(out=xbuf[:, qt * 512:(qt + 1) * 512],
                                   in_=V["x1own_dram"][q0 + qt * 128:q0 + (qt + 1) * 128,
                                                       dc * 512:(dc + 1) * 512]) \
                        .then_inc(x1l[dc % 2].h, x1l[dc % 2].inc(16))
                for ot in range(NID):
                    if k % 2 == 0:
                        if k >= 4:
                            sync.wait_ge(wti.h, wti.t(k - 3))
                        sync.dma_start(out=WB4[k % 4][:, :],
                                       in_=V["wo_cache"][ot * 128:(ot + 1) * 128,
                                                         dc * 512:(dc + 1) * 512]) \
                            .then_inc(lwo[k % 4].h, lwo[k % 4].inc(16))
                    k += 1

        @block.scalar
        def _(scalar, q0=q0):
            k = 0
            for dc in range(8):
                for ot in range(NID):
                    if k % 2 == 1:
                        if k >= 4:
                            scalar.wait_ge(wti.h, wti.t(k - 3))
                        scalar.dma_start(out=WB4[k % 4][:, :],
                                         in_=V["wo_cache"][ot * 128:(ot + 1) * 128,
                                                           dc * 512:(dc + 1) * 512]) \
                            .then_inc(lwo[k % 4].h, lwo[k % 4].inc(16))
                    k += 1

        @block.tensor
        def _(tensor):
            k = 0
            for dc in range(8):
                if dc >= 1:
                    tensor.wait_ge(ewo.h, ewo.t(dc * 4))
                for ot in range(NID):
                    tensor.wait_ge(lwo[k % 4].h, lwo[k % 4].t((k // 4 + 1) * 16))
                    for qt in range(4):
                        ins = tensor.matmul(out=p_mm[qt][:, :],
                                      lhsT=ab2[:, ot * 512 + qt * 128:ot * 512 + (qt + 1) * 128],
                                      rhs=WB4[k % 4][:, :],
                                      start=(ot == 0), stop=(ot == NID - 1))
                    ins.then_inc(wti.h, wti.inc(1))
                    k += 1

        @block.vector
        def _(vector):
            for dc in range(8):
                vector.wait_ge(wti.h, wti.t((dc + 1) * 32))
                vector.wait_ge(x1l[dc % 2].h, x1l[dc % 2].t((dc // 2 + 1) * 64))
                xbuf = V["s_xt"] if dc % 2 == 0 else V["s_p0"]
                for qt in range(4):
                    idx = dc * 4 + qt
                    if idx >= 2:
                        vector.wait_ge(swo.h, swo.t((idx - 1) * 16))
                    vector.scalar_tensor_tensor(out=V["s_ev"][qt % 2][:, :],
                                                in0=p_mm[qt][:, :], scalar=1.0,
                                                in1=xbuf[:, qt * 512:(qt + 1) * 512],
                                                op0=ALU.mult, op1=ALU.add) \
                        .then_inc(ewo.h, ewo.inc(1))

        @block.gpsimd
        def _(gps, q0=q0):
            for dc in range(8):
                for qt in range(4):
                    gps.wait_ge(ewo.h, ewo.t(dc * 4 + qt + 1))
                    gps.dma_start(out=V["x2_dram"][q0 + qt * 128:q0 + (qt + 1) * 128,
                                                   dc * 512:(dc + 1) * 512],
                                  in_=V["s_ev"][qt % 2][:, :]).then_inc(swo.h, swo.inc(16))
            gps.wait_ge(swo.h, swo.n)

        _fence(kb, block, V)


def _ln3(kb, block, V):
    for qc2 in range(2):
        q0 = qc2 * CHK
        fm = _ln_to_fm(kb, block, V,
                       lambda tt, half, q0=q0: V["x2_dram"][q0 + tt * 128:q0 + (tt + 1) * 128,
                                                            half * 2048:(half + 1) * 2048],
                       NTT)
        st3 = kb.new_sem()

        @block.gpsimd
        def _(gps, qc2=qc2):
            gmem = _getreg(gps, V, "g_member")
            gps.load(gmem, V["s_meta"][0:1, 0:1])
            gps.wait_ge(fm.h, fm.t(4 * NTT))
            for g8 in range(8):
                r0 = g8 * 4 * 128
                src = V["ab2"][:, g8 * 2048:(g8 + 1) * 2048] \
                    .rearrange("p (i t) -> p i t", t=CHK)
                with gps.If_eq(gmem, 0):
                    gps.dma_start(
                        out=V["ln3_sh"][qc2 * NID * 128 + r0:qc2 * NID * 128 + r0 + 512, :]
                        .rearrange("(i p) t -> p i t", p=128),
                        in_=src).then_inc(st3.h, st3.inc(16))
                with gps.Else():
                    gps.dma_start(
                        out=V["ln3_sh"][(2 + qc2) * NID * 128 + r0:(2 + qc2) * NID * 128 + r0 + 512, :]
                        .rearrange("(i p) t -> p i t", p=128),
                        in_=src).then_inc(st3.h, 16)
            gps.wait_ge(st3.h, st3.n)

        _fence(kb, block, V)


def _finish(kb, block, V):
    fl = kb.new_sem()
    fv = kb.new_sem()
    fs = kb.new_sem()

    @block.sync
    def _(sync):
        smem = _getreg(sync, V, "s_member")
        sync.load(smem, V["s_meta"][0:1, 0:1])
        for k in range(16):
            tt, half = k // 2, k % 2
            if k > 0:
                sync.wait_ge(fv.h, fv.t(k))
            hs = slice(half * 2048, (half + 1) * 2048)
            sync.dma_start(out=V["s_xt"][:, :],
                           in_=V["x2_dram"][tt * 128:(tt + 1) * 128, hs]) \
                .then_inc(fl.h, fl.inc(16))
            with sync.If_eq(smem, 0):
                r = tt * 128
                sync.dma_start(out=V["s_p0"][:, :],
                               in_=V["pc_sh"][0][r:r + 128, hs]).then_inc(fl.h, fl.inc(16))
                sync.dma_start(out=V["s_p1"][:, :],
                               in_=V["pc_sh"][1][r:r + 128, hs]).then_inc(fl.h, fl.inc(16))
            with sync.Else():
                r = 1024 + tt * 128
                sync.dma_start(out=V["s_p0"][:, :],
                               in_=V["pc_sh"][0][r:r + 128, hs]).then_inc(fl.h, 16)
                sync.dma_start(out=V["s_p1"][:, :],
                               in_=V["pc_sh"][1][r:r + 128, hs]).then_inc(fl.h, 16)

    @block.vector
    def _(vector):
        for k in range(16):
            vector.wait_ge(fl.h, fl.t((k + 1) * 48))
            if k >= 1:
                vector.wait_ge(fs.h, fs.t(k * 16))
            vector.tensor_add(out=V["s_p0"][:, :], in0=V["s_p0"][:, :], in1=V["s_p1"][:, :])
            vector.scalar_tensor_tensor(out=V["s_ln"][:, :], in0=V["s_p0"][:, :],
                                        scalar=0.5, in1=V["s_xt"][:, :],
                                        op0=ALU.mult, op1=ALU.add) \
                .then_inc(fv.h, fv.inc(1))

    @block.gpsimd
    def _(gps):
        for k in range(16):
            tt, half = k // 2, k % 2
            gps.wait_ge(fv.h, fv.t(k + 1))
            gps.dma_start(out=V["out_x3"][tt * 128:(tt + 1) * 128,
                                          half * 2048:(half + 1) * 2048],
                          in_=V["s_ln"][:, :]).then_inc(fs.h, fs.inc(16))
        gps.wait_ge(fs.h, fs.n)


def _dbg_out(kb, block, V, upto):
    ze = kb.new_sem()
    st = kb.new_sem()

    @block.vector
    def _(vector):
        vector.memset(V["s_ln"][:, :], 0.0).then_inc(ze.h, ze.inc(1))

    @block.gpsimd
    def _(gps):
        gps.wait_ge(ze.h, ze.t(1))
        if V.get("dbg"):
            if upto >= 1:
                gps.dma_start(out=V["dbg_w"][:, :],
                              in_=V["w_cache"]["g"][0:256, :]).then_inc(st.h, st.inc(16))
            if upto >= 2:
                for q in range(4):
                    gps.dma_start(out=V["dbg_x1"][q * 1024:(q + 1) * 1024, :],
                                  in_=V["x1_sh"][q * 1024:(q + 1) * 1024, :]) \
                        .then_inc(st.h, st.inc(16))
                    for m in range(2):
                        gps.dma_start(out=V["dbg_pa"][m][q * 1024:(q + 1) * 1024, :],
                                      in_=V["pa_sh"][m][q * 1024:(q + 1) * 1024, :]) \
                            .then_inc(st.h, st.inc(16))
            if upto >= 3:
                gps.dma_start(out=V["dbg_k"][:, :], in_=V["k_sh"][:, :]).then_inc(st.h, st.inc(16))
                gps.dma_start(out=V["dbg_v"][:, :], in_=V["v_sh"][:, :]).then_inc(st.h, st.inc(16))
                gps.dma_start(out=V["dbg_q"][:, :], in_=V["q_fm"][:, :]) \
                    .then_inc(st.h, st.inc(16))
            if upto >= 4:
                gps.dma_start(out=V["dbg_x2"][:, :], in_=V["x2_dram"][:, :]).then_inc(st.h, st.inc(16))
        for k in range(16):
            tt, half = k // 2, k % 2
            gps.dma_start(out=V["out_x3"][tt * 128:(tt + 1) * 128,
                                          half * 2048:(half + 1) * 2048],
                          in_=V["s_ln"][:, :]).then_inc(st.h, st.inc(16))
        gps.wait_ge(st.h, st.n)



def _build_all(kb, block, V, stages=9, dbg=False):
    p_ma = V["p_ma"]

    m0 = kb.new_sem()

    @block.sync
    def _(sync):
        sync.dma_start(out=V["s_meta"][:], in_=V["meta_in"]).then_inc(m0.h, m0.inc(16))
        sync.dma_start(out=V["s_ident"][:, :], in_=V["identb_in"]).then_inc(m0.h, m0.inc(16))
        sync.dma_start(out=V["s_identf"][:, :], in_=V["identf_in"]).then_inc(m0.h, m0.inc(16))

    @block.vector
    def _(vector):
        vector.memset(V["s_onesf"][:], 1.0)
        vector.memset(V["s_lnb"][:, :], 0.0)
        vector.wait_ge(m0.h, m0.t(48))

    _fence(kb, block, V)

    lca = kb.new_sem()
    lcv = kb.new_sem()
    la_specs = [("g", NID, 0), ("u", NID, NID * R), ("d", NDH, 2 * NID * R)]

    @block.sync
    def _(sync):
        for k, (mat, nit, dst) in enumerate(la_specs):
            if k >= 1:
                sync.wait_ge(lcv.h, lcv.t(k))
            for i in range(nit):
                sync.dma_start(out=V["s_xt"][:, i * R:(i + 1) * R],
                               in_=V["la_in"][mat][i * 128:(i + 1) * 128, :]) \
                    .then_inc(lca.h, lca.inc(16))

    @block.vector
    def _(vector):
        tot = 0
        for k, (mat, nit, dst) in enumerate(la_specs):
            tot += nit
            vector.wait_ge(lca.h, lca.t(tot * 16))
            vector.tensor_copy(out=V["s_la"][:, dst:dst + nit * R],
                               in_=V["s_xt"][:, 0:nit * R]).then_inc(lcv.h, lcv.inc(1))

    lb_specs = []
    for mat, width, lbo in (("g", DH, 0), ("u", DH, DH), ("d", D, 2 * DH)):
        for c0 in range(0, width, 2048):
            lb_specs.append((mat, c0, lbo, min(2048, width - c0)))
    lbL = kb.new_sem()
    lbC = kb.new_sem()
    lbS = kb.new_sem()

    @block.sync
    def _(sync):
        sync.wait_ge(lcv.h, lcv.t(3))
        for k, (mat, c0, lbo, w) in enumerate(lb_specs):
            if k >= 1:
                sync.wait_ge(lbC.h, lbC.t(k))
            sync.dma_start(out=V["s_xt"][0:16, 0:w],
                           in_=V["lb_in"][mat][0:16, c0:c0 + w]) \
                .then_inc(lbL.h, lbL.inc(16))

    @block.vector
    def _(vector):
        for k, (mat, c0, lbo, w) in enumerate(lb_specs):
            vector.wait_ge(lbL.h, lbL.t((k + 1) * 16))
            if k >= 1:
                vector.wait_ge(lbS.h, lbS.t(k * 16))
            vector.tensor_copy(out=V["s_lnb"][0:16, 0:w],
                               in_=V["s_xt"][0:16, 0:w]).then_inc(lbC.h, lbC.inc(1))

    @block.scalar
    def _(scalar):
        for k, (mat, c0, lbo, w) in enumerate(lb_specs):
            scalar.wait_ge(lbC.h, lbC.t(k + 1))
            scalar.dma_start(out=V["lb_cache"][0:16, lbo + c0:lbo + c0 + w],
                             in_=V["s_lnb"][0:16, 0:w]).then_inc(lbS.h, lbS.inc(16))
        scalar.wait_ge(lbS.h, lbS.n)

    _fence(kb, block, V)

    wc_specs = []
    for it in range(NID):
        wc_specs.append((V["wk_in"], it * 128, 0, V["wkv_cache"], it * 128, 0))
        wc_specs.append((V["wv_in"], it * 128, 0, V["wkv_cache"], it * 128, 512))
    for it in range(NID):
        for oc in range(8):
            wc_specs.append((V["wq_in"], it * 128, oc * 512, V["wq_cache"], it * 128, oc * 512))
            wc_specs.append((V["wo_in"], it * 128, oc * 512, V["wo_cache"], it * 128, oc * 512))
    wca = [kb.new_sem() for _ in range(4)]
    wcv = kb.new_sem()
    wcs = kb.new_sem()
    CIN4 = [V["s_xt"][:, 0:512], V["s_xt"][:, 512:1024],
            V["s_p0"][:, 0:512], V["s_p0"][:, 512:1024]]
    COUT4 = [V["s_wb"][0], V["s_wb"][1], V["s_wg"][0], V["s_wg"][1]]

    @block.sync
    def _(sync):
        for k, (src, r0, c0, dst, dr, dc) in enumerate(wc_specs):
            if k % 2 == 0:
                if k >= 4:
                    sync.wait_ge(wcv.h, wcv.t(k - 3))
                sync.dma_start(out=CIN4[k % 4],
                               in_=src[r0:r0 + 128, c0:c0 + 512]) \
                    .then_inc(wca[k % 4].h, wca[k % 4].inc(16))

    @block.scalar
    def _(scalar):
        for k, (src, r0, c0, dst, dr, dc) in enumerate(wc_specs):
            if k % 2 == 1:
                if k >= 4:
                    scalar.wait_ge(wcv.h, wcv.t(k - 3))
                scalar.dma_start(out=CIN4[k % 4],
                                 in_=src[r0:r0 + 128, c0:c0 + 512]) \
                    .then_inc(wca[k % 4].h, wca[k % 4].inc(16))

    @block.vector
    def _(vector):
        for k in range(len(wc_specs)):
            vector.wait_ge(wca[k % 4].h, wca[k % 4].t((k // 4 + 1) * 16))
            if k >= 4:
                vector.wait_ge(wcs.h, wcs.t((k - 3) * 16))
            vector.tensor_copy(out=COUT4[k % 4][:, :],
                               in_=CIN4[k % 4]).then_inc(wcv.h, wcv.inc(1))

    @block.gpsimd
    def _(gps):
        for k, (src, r0, c0, dst, dr, dc) in enumerate(wc_specs):
            gps.wait_ge(wcv.h, wcv.t(k + 1))
            gps.dma_start(out=dst[dr:dr + 128, dc:dc + 512],
                          in_=COUT4[k % 4][:, :]).then_inc(wcs.h, wcs.inc(16))
        gps.wait_ge(wcs.h, wcs.n)

    _fence(kb, block, V)

    DQ0 = {"du": V["s_du"][:, :], "ds": V["s_ds"][:, :], "dz": V["s_dz"][:, :],
           "pe": [V["s_pe"][j][:, :] for j in range(4)],
           "po": [V["s_po"][j][:, :] for j in range(4)],
           "te": V["s_te"][:, :], "to": V["s_to"][:, :]}
    DQ1 = {"du": V["s_xt"][:, 0:512], "ds": V["s_xt"][:, 512:1024],
           "dz": V["s_xt"][:, 1024:1536],
           "pe": [V["s_xt"][:, 1536:2048], V["s_p0"][:, 0:512],
                  V["s_p0"][:, 512:1024], V["s_p0"][:, 1024:1536]],
           "po": [V["s_p0"][:, 1536:2048], V["s_p1"][:, 0:512],
                  V["s_p1"][:, 512:1024], V["s_p1"][:, 1024:1536]],
           "te": V["s_p1"][:, 1536:2048], "to": V["s_ln"][:, 0:512]}
    DQ = [DQ0, DQ1]
    deq = V["deq_tiles"]
    cdl = [kb.new_sem(), kb.new_sem()]
    arl = [kb.new_sem(), kb.new_sem()]
    mrow = kb.new_sem()
    dqa = kb.new_sem()
    dqv = kb.new_sem()
    wst = kb.new_sem()

    @block.sync
    def _(sync):
        for k, (mat, it, oc, w) in enumerate(deq):
            if k >= 2:
                sync.wait_ge(dqv.h, dqv.t(k - 1))
            sync.dma_start(out=V["s_codes"][k % 2][:, 0:w],
                           in_=V["codes_in"][mat][it * 128:(it + 1) * 128,
                                                  oc * 512:oc * 512 + w]) \
                .then_inc(cdl[k % 2].h, cdl[k % 2].inc(16))
            sync.dma_start(out=V["s_arow"][k % 2][0:1, 0:w],
                           in_=V["am_in"][mat][2 * it:2 * it + 1, oc * 512:oc * 512 + w]) \
                .then_inc(arl[k % 2].h, arl[k % 2].inc(16))
            sync.dma_start(out=V["s_brow"][k % 2][0:1, 0:w],
                           in_=V["am_in"][mat][2 * it + 1:2 * it + 2, oc * 512:oc * 512 + w]) \
                .then_inc(arl[k % 2].h, arl[k % 2].inc(16))

    @block.tensor
    def _(tensor):
        for k, (mat, it, oc, w) in enumerate(deq):
            tensor.wait_ge(arl[k % 2].h, arl[k % 2].t((k // 2 + 1) * 32))
            if k >= 2:
                tensor.wait_ge(dqv.h, dqv.t(k - 1))
            tensor.matmul(out=p_ma[k % 2][0:64, 0:w], lhsT=V["s_onesf"][0:1, 0:64],
                          rhs=V["s_arow"][k % 2][0:1, 0:w], start=True, stop=True)
            tensor.matmul(out=p_ma[k % 2][64:128, 0:w], lhsT=V["s_onesf"][0:1, 0:64],
                          rhs=V["s_brow"][k % 2][0:1, 0:w], start=True, stop=True,
                          tile_position=(0, 64)).then_inc(mrow.h, mrow.inc(1))

    @block.scalar
    def _(scalar):
        for k, (mat, it, oc, w) in enumerate(deq):
            q = DQ[k % 2]
            scalar.wait_ge(cdl[k % 2].h, cdl[k % 2].t((k // 2 + 1) * 16))
            if k >= 2:
                scalar.wait_ge(dqv.h, dqv.t(k - 1))
            cw = slice(0, w)
            scalar.activation(out=q["du"][:, cw], in_=V["s_codes"][k % 2][:, cw],
                              func=AF.Copy, scale=1.0 / 7.5, bias=-1.0)
            scalar.activation(out=q["ds"][:, cw], in_=q["du"][:, cw], func=AF.Square)
            scalar.activation(out=q["dz"][:, cw], in_=q["ds"][:, cw], func=AF.Square)
            for j in range(4):
                ins = scalar.activation(out=q["pe"][j][:, cw], in_=q["ds"][:, cw],
                                        func=AF.Copy, scale=CE[2 * j + 1], bias=CE[2 * j])
            ins.then_inc(dqa.h, dqa.inc(1))

    @block.vector
    def _(vector):
        for k, (mat, it, oc, w) in enumerate(deq):
            q = DQ[k % 2]
            vector.wait_ge(dqa.h, dqa.t(k + 1))
            vector.wait_ge(mrow.h, mrow.t(k + 1))
            if k >= 2:
                vector.wait_ge(wst.h, wst.t((k - 1) * 16))
            cw = slice(0, w)
            for j in range(4):
                vector.tensor_scalar(out=q["po"][j][:, cw], in0=q["ds"][:, cw],
                                     scalar1=CO[2 * j + 1], scalar2=CO[2 * j],
                                     op0=ALU.mult, op1=ALU.add)
            for tgt, parts in ((q["te"], q["pe"]), (q["to"], q["po"])):
                vector.scalar_tensor_tensor(out=tgt[:, cw], in0=parts[3][:, cw],
                                            scalar=0.0, in1=q["dz"][:, cw],
                                            op0=ALU.add, op1=ALU.mult)
                vector.tensor_add(out=tgt[:, cw], in0=tgt[:, cw], in1=parts[2][:, cw])
                vector.scalar_tensor_tensor(out=tgt[:, cw], in0=tgt[:, cw],
                                            scalar=0.0, in1=q["dz"][:, cw],
                                            op0=ALU.add, op1=ALU.mult)
                vector.tensor_add(out=tgt[:, cw], in0=tgt[:, cw], in1=parts[1][:, cw])
                vector.scalar_tensor_tensor(out=tgt[:, cw], in0=tgt[:, cw],
                                            scalar=0.0, in1=q["dz"][:, cw],
                                            op0=ALU.add, op1=ALU.mult)
                vector.tensor_add(out=tgt[:, cw], in0=tgt[:, cw], in1=parts[0][:, cw])
            vector.tensor_mul(out=q["to"][:, cw], in0=q["to"][:, cw],
                              in1=q["du"][:, cw])
            vector.tensor_add(out=q["te"][:, cw], in0=q["te"][:, cw],
                              in1=q["to"][:, cw])
            vector.tensor_mul(out=V["s_wout"][k % 2][:, cw], in0=q["te"][:, cw],
                              in1=p_ma[k % 2][:, cw]).then_inc(dqv.h, dqv.inc(1))

    @block.gpsimd
    def _(gps):
        for k, (mat, it, oc, w) in enumerate(deq):
            gps.wait_ge(dqv.h, dqv.t(k + 1))
            gps.dma_start(out=V["w_cache"][mat][it * 128:(it + 1) * 128,
                                                oc * 512:oc * 512 + w],
                          in_=V["s_wout"][k % 2][:, 0:w]).then_inc(wst.h, wst.inc(16))
        gps.wait_ge(wst.h, wst.n)

    _fence(kb, block, V)
    if stages <= 1:
        _dbg_out(kb, block, V, 1)
        return

    _ffn(kb, block, V, "A")
    _barrier(kb, block, V, 0)

    xa = kb.new_sem()
    xl = kb.new_sem()
    xs = kb.new_sem()

    @block.sync
    def _(sync):
        smem = _getreg(sync, V, "s_member")
        sync.load(smem, V["s_meta"][0:1, 0:1])
        for k in range(32):
            tb, tt, half = k // 16, (k // 2) % 8, k % 2
            if k > 0:
                sync.wait_ge(xa.h, xa.t(k))
            hs = slice(half * 2048, (half + 1) * 2048)
            with sync.If_eq(smem, 0):
                r = tb * 2048 + tt * 128
                sync.dma_start(out=V["s_xt"][:, :], in_=V["x_in"][r:r + 128, hs]) \
                    .then_inc(xl.h, xl.inc(16))
                sync.dma_start(out=V["s_p0"][:, :], in_=V["pa_sh"][0][r:r + 128, hs]) \
                    .then_inc(xl.h, xl.inc(16))
                sync.dma_start(out=V["s_p1"][:, :], in_=V["pa_sh"][1][r:r + 128, hs]) \
                    .then_inc(xl.h, xl.inc(16))
            with sync.Else():
                r = 1024 + tb * 2048 + tt * 128
                sync.dma_start(out=V["s_xt"][:, :], in_=V["x_in"][r:r + 128, hs]) \
                    .then_inc(xl.h, 16)
                sync.dma_start(out=V["s_p0"][:, :], in_=V["pa_sh"][0][r:r + 128, hs]) \
                    .then_inc(xl.h, 16)
                sync.dma_start(out=V["s_p1"][:, :], in_=V["pa_sh"][1][r:r + 128, hs]) \
                    .then_inc(xl.h, 16)

    @block.vector
    def _(vector):
        for k in range(32):
            vector.wait_ge(xl.h, xl.t((k + 1) * 48))
            if k >= 1:
                vector.wait_ge(xs.h, xs.t(k * 32))
            vector.tensor_add(out=V["s_p0"][:, :], in0=V["s_p0"][:, :], in1=V["s_p1"][:, :])
            vector.scalar_tensor_tensor(out=V["s_ln"][:, :], in0=V["s_p0"][:, :],
                                        scalar=0.5, in1=V["s_xt"][:, :],
                                        op0=ALU.mult, op1=ALU.add) \
                .then_inc(xa.h, xa.inc(1))

    @block.gpsimd
    def _(gps):
        gmem = _getreg(gps, V, "g_member")
        gps.load(gmem, V["s_meta"][0:1, 0:1])
        for k in range(32):
            tb, tt, half = k // 16, (k // 2) % 8, k % 2
            hs = slice(half * 2048, (half + 1) * 2048)
            gps.wait_ge(xa.h, xa.t(k + 1))
            with gps.If_eq(gmem, 0):
                r = tb * 2048 + tt * 128
                gps.dma_start(out=V["x1_sh"][r:r + 128, hs], in_=V["s_ln"][:, :]) \
                    .then_inc(xs.h, xs.inc(16))
                if tb == 0:
                    gps.dma_start(out=V["x1own_dram"][tt * 128:(tt + 1) * 128, hs],
                                  in_=V["s_ln"][:, :]).then_inc(xs.h, xs.inc(16))
                else:
                    gps.dma_start(out=V["x1_sh"][r:r + 128, hs], in_=V["s_ln"][:, :]) \
                        .then_inc(xs.h, xs.inc(16))
            with gps.Else():
                r = 1024 + tb * 2048 + tt * 128
                gps.dma_start(out=V["x1_sh"][r:r + 128, hs], in_=V["s_ln"][:, :]) \
                    .then_inc(xs.h, 16)
                if tb == 0:
                    gps.dma_start(out=V["x1own_dram"][tt * 128:(tt + 1) * 128, hs],
                                  in_=V["s_ln"][:, :]).then_inc(xs.h, 16)
                else:
                    gps.dma_start(out=V["x1_sh"][r:r + 128, hs], in_=V["s_ln"][:, :]) \
                        .then_inc(xs.h, 16)
        gps.wait_ge(xs.h, xs.n)

    _fence(kb, block, V)
    _barrier(kb, block, V, 1)
    if stages <= 2:
        _dbg_out(kb, block, V, 2)
        return

    _qkv(kb, block, V)
    _barrier(kb, block, V, 2)
    if stages <= 3:
        _dbg_out(kb, block, V, 3)
        return
    _attn(kb, block, V)
    if stages <= 4:
        _dbg_out(kb, block, V, 4)
        return

    _ln3(kb, block, V)
    _barrier(kb, block, V, 3)
    _ffn(kb, block, V, "C")
    _barrier(kb, block, V, 4)
    _finish(kb, block, V)



_PROG = None
_PROG_KEY = None
LAST_RESULTS = None


def kernel(L0_attn_x, L0_attn_Wq, L0_attn_Wk, L0_attn_Wv, L0_attn_Wo,
           L0_ffn_g_codes, L0_ffn_g_absmax, L0_ffn_g_lora_a, L0_ffn_g_lora_b,
           L0_ffn_u_codes, L0_ffn_u_absmax, L0_ffn_u_lora_a, L0_ffn_u_lora_b,
           L0_ffn_d_codes, L0_ffn_d_absmax, L0_ffn_d_lora_a, L0_ffn_d_lora_b):
    global _PROG, _PROG_KEY, LAST_RESULTS
    import ml_dtypes
    bf16 = ml_dtypes.bfloat16
    rng = np.random.default_rng()
    nonce = int(rng.integers(1, 2**30))

    x = np.ascontiguousarray(np.asarray(L0_attn_x, dtype=np.float32))
    wqT = np.ascontiguousarray(np.asarray(L0_attn_Wq, np.float32).T)
    wkT = np.ascontiguousarray(np.asarray(L0_attn_Wk, np.float32).T)
    wvT = np.ascontiguousarray(np.asarray(L0_attn_Wv, np.float32).T)
    woT = np.ascontiguousarray(np.asarray(L0_attn_Wo, np.float32).T)
    gcT = np.ascontiguousarray(np.asarray(L0_ffn_g_codes, np.int32).T)
    ucT = np.ascontiguousarray(np.asarray(L0_ffn_u_codes, np.int32).T)
    dcT = np.ascontiguousarray(np.asarray(L0_ffn_d_codes, np.int32).T)
    gaT = np.ascontiguousarray(np.asarray(L0_ffn_g_absmax, np.float32).T)
    uaT = np.ascontiguousarray(np.asarray(L0_ffn_u_absmax, np.float32).T)
    daT = np.ascontiguousarray(np.asarray(L0_ffn_d_absmax, np.float32).T)
    gla = np.ascontiguousarray(np.asarray(L0_ffn_g_lora_a, np.float32))
    glb = np.ascontiguousarray(np.asarray(L0_ffn_g_lora_b, np.float32))
    ula = np.ascontiguousarray(np.asarray(L0_ffn_u_lora_a, np.float32))
    ulb = np.ascontiguousarray(np.asarray(L0_ffn_u_lora_b, np.float32))
    dla = np.ascontiguousarray(np.asarray(L0_ffn_d_lora_a, np.float32))
    dlb = np.ascontiguousarray(np.asarray(L0_ffn_d_lora_b, np.float32))

    cos, sin = _llama3_cos_sin()
    ident_b = np.eye(128, dtype=bf16)
    ident_f = np.eye(128, dtype=np.float32)

    stages = int(os.environ.get("ATH_STAGES", "9"))
    dbgf = os.environ.get("ATH_DBG", "0") == "1"
    if _PROG is None or _PROG_KEY != (stages, dbgf):
        _PROG = build_program(stages, dbgf)
        _PROG_KEY = (stages, dbgf)
    nc = _PROG

    in_maps = []
    for core in range(8):
        pair = core // 2
        member = core % 2
        batch = pair // 2
        pip = pair % 2
        rot = pip * 2048
        blk = 2 * pip + member

        perm = np.roll(np.arange(T), -rot)
        xb = np.ascontiguousarray(x[batch][perm])
        cosk4 = np.ascontiguousarray(np.tile(cos[perm], (1, 4)))
        sink4 = np.ascontiguousarray(np.tile(sin[perm], (1, 4)))
        qpos = np.arange(blk * QBLK, (blk + 1) * QBLK)
        cosq4 = np.ascontiguousarray(np.tile(cos[qpos] * ISQ, (1, 4)).astype(np.float32))
        sinq4 = np.ascontiguousarray(np.tile(sin[qpos] * ISQ, (1, 4)).astype(np.float32))
        mask01 = np.ascontiguousarray((qpos[:, None] >= perm[None, :]).astype(bf16))

        mh = slice(member * DH, (member + 1) * DH)
        kvs = slice(member * 512, (member + 1) * 512)

        in_maps.append({
            "x_b": xb,
            "g_codesT_h": np.ascontiguousarray(gcT[:, mh]),
            "u_codesT_h": np.ascontiguousarray(ucT[:, mh]),
            "d_codesT_h": np.ascontiguousarray(dcT[mh, :]),
            "g_absmaxT_h": np.ascontiguousarray(gaT[:, mh]),
            "u_absmaxT_h": np.ascontiguousarray(uaT[:, mh]),
            "d_absmaxT_h": np.ascontiguousarray(daT[member * 86:(member + 1) * 86, :]),
            "g_lora_a": gla, "u_lora_a": ula,
            "d_lora_a_h": np.ascontiguousarray(dla[mh, :]),
            "g_lora_b_h": np.ascontiguousarray(glb[:, mh]),
            "u_lora_b_h": np.ascontiguousarray(ulb[:, mh]),
            "d_lora_b": dlb,
            "wqT": wqT, "woT": woT,
            "wkT_h": np.ascontiguousarray(wkT[:, kvs]),
            "wvT_h": np.ascontiguousarray(wvT[:, kvs]),
            "cosk4": cosk4, "sink4": sink4, "cosq4": cosq4, "sinq4": sinq4,
            "mask01": mask01,
            "identb": ident_b, "identf": ident_f,
            "meta": np.array([[member, nonce, 0, 0, 0, 0, 0, 0]], np.int32),
        })

    res = run_bass_kernel_spmd(nc, in_maps, core_ids=list(range(8)))
    LAST_RESULTS = res
    out = np.empty((B, T, D), dtype=np.float32)
    for core in range(8):
        pair = core // 2
        member = core % 2
        batch = pair // 2
        pip = pair % 2
        blk = 2 * pip + member
        out[batch, blk * QBLK:(blk + 1) * QBLK, :] = res.results[core]["out_x3"]
    return out


# version 05 (final)
# speedup vs baseline: 1.5017x; 1.0089x over previous
import math
import os
import contextlib
import numpy as np
import concourse.bass as bass
import concourse.mybir as mybir
from concourse.bass_utils import run_bass_kernel_spmd

F32 = mybir.dt.float32
BF16 = mybir.dt.bfloat16
I32 = mybir.dt.int32
AF = mybir.ActivationFunctionType
ALU = mybir.AluOpType
AX = mybir.AxisListType

B, T, D = 2, 4096, 4096
DFF = 11008
DH = DFF // 2
NDH = DH // 128
NID = D // 128
R = 16
HD = 128
NH = 32
NKV = 8
KVW = NKV * HD
QBLK = 1024
EPS = 1e-5
ISQ = 1.0 / math.sqrt(HD)
CHK = 512
NTT = 4

NF4 = np.array([
    -1.0, -0.6961928009986877, -0.5250730514526367, -0.39491748809814453,
    -0.28444138169288635, -0.18477343022823334, -0.09105003625154495, 0.0,
    0.07958029955625534, 0.16093020141124725, 0.2461123019456863,
    0.33791524171829224, 0.44070982933044434, 0.5626170039176941,
    0.7229568362236023, 1.0], dtype=np.float64)


def _nf4_coeffs():
    E, O, Sq = [], [], []
    for k in range(8):
        hi, lo = NF4[8 + k], NF4[7 - k]
        u = (k + 0.5) / 7.5
        E.append((hi + lo) / 2.0)
        O.append((hi - lo) / (2.0 * u))
        Sq.append(u * u)
    cE = np.polynomial.polynomial.polyfit(np.array(Sq), np.array(E), 7)
    cO = np.polynomial.polynomial.polyfit(np.array(Sq), np.array(O), 7)
    return [float(x) for x in cE], [float(x) for x in cO]

CE, CO = _nf4_coeffs()


def _llama3_cos_sin():
    base = 500000.0
    inv_freq = 1.0 / base ** (np.arange(0, HD, 2, dtype=np.float32) / HD)
    factor, low_f, high_f, old_ctx = 32.0, 1.0, 4.0, 8192
    low_wl, high_wl = old_ctx / low_f, old_ctx / high_f
    wavelen = 2 * math.pi / inv_freq
    inv_llama = np.where(wavelen > low_wl, inv_freq / factor, inv_freq)
    smooth = (old_ctx / wavelen - low_f) / (high_f - low_f)
    smoothed = (1 - smooth) * inv_llama / factor + smooth * inv_llama
    is_med = ~(wavelen < high_wl) & ~(wavelen > low_wl)
    inv = np.where(is_med, smoothed, inv_llama).astype(np.float32)
    pos = np.arange(T, dtype=np.float32)
    freqs = pos[:, None] * inv[None, :]
    emb = np.concatenate([freqs, freqs], axis=-1)
    return np.cos(emb).astype(np.float32), np.sin(emb).astype(np.float32)


class Led:
    __slots__ = ("h", "n", "base", "fence_born")

    def __init__(self, h):
        self.h = h
        self.n = 0
        self.base = 0
        self.fence_born = -10

    def inc(self, k):
        self.n += k
        return k

    def t(self, c):
        return self.base + c


class KB:
    POOL = 92

    def __init__(self):
        self.nc = bass.Bass("TRN2", target_bir_lowering=False, debug=False)
        self.es = contextlib.ExitStack()
        self.pool = []
        self.cursor = 0
        self.nfence = 0
        self.ndedic = 0

    def new_sem(self):
        if len(self.pool) < self.POOL:
            led = Led(self.es.enter_context(self.nc.semaphore(f"s{len(self.pool)}")))
            self.pool.append(led)
        else:
            led = self.pool[self.cursor % self.POOL]
            assert self.nfence >= led.fence_born + 2, \
                f"semaphore recycled too soon ({self.nfence} vs {led.fence_born})"
        self.cursor += 1
        led.base = led.n
        led.fence_born = self.nfence
        return led

    def new_dedicated(self):
        led = Led(self.es.enter_context(self.nc.semaphore(f"d{self.ndedic}")))
        self.ndedic += 1
        return led


def _fence(kb, block, V):
    f = kb.new_sem()
    kb.nfence += 1
    for eng in ("sync", "tensor", "scalar", "vector", "gpsimd"):
        @getattr(block, eng)
        def _(e):
            e.drain()
            e.sem_inc(f.h, 1)
            e.wait_ge(f.h, f.t(5))
    f.n = f.base + 5


def _getreg(eng, V, name):
    regs = V.setdefault("_regs", {})
    if name not in regs:
        regs[name] = eng.alloc_register(name)
    return regs[name]


def _dsep(vector, V, n=2):
    for _ in range(n):
        vector.reduce_sum(out=V["s_stat"][:, 7:8], in_=V["s_lnb"][:, 0:1024],
                          axis=AX.X)


def _barrier(kb, block, V, slot):
    bar = kb.new_dedicated()

    @block.gpsimd
    def _(g):
        member = _getreg(g, V, "g_member")
        g.load(member, V["s_meta"][0:1, 0:1])
        with g.If_eq(member, 0):
            g.dma_start(out=V["flags_sh"][0:1, slot:slot + 1],
                        in_=V["s_meta"][0:1, 1:2]).then_inc(bar.h, bar.inc(16))
        with g.Else():
            g.dma_start(out=V["flags_sh"][1:2, slot:slot + 1],
                        in_=V["s_meta"][0:1, 1:2]).then_inc(bar.h, 16)
        g.wait_ge(bar.h, bar.t(16))
        nonce = _getreg(g, V, "g_nonce")
        g.load(nonce, V["s_meta"][0:1, 1:2])
        cnt = _getreg(g, V, "g_cnt")
        g.reg_mov(cnt, bar.t(16))
        val = _getreg(g, V, "g_val")
        go = _getreg(g, V, "g_go")
        g.reg_mov(go, 1)
        with g.While(go):
            with g.If_eq(member, 0):
                g.dma_start(out=V["s_flag"][0:1, slot:slot + 1],
                            in_=V["flags_sh"][1:2, slot:slot + 1]).then_inc(bar.h, 16)
            with g.Else():
                g.dma_start(out=V["s_flag"][0:1, slot:slot + 1],
                            in_=V["flags_sh"][0:1, slot:slot + 1]).then_inc(bar.h, 16)
            g.reg_add(cnt, cnt, 16)
            g.wait_ge(bar.h, cnt)
            g.load(val, V["s_flag"][0:1, slot:slot + 1])
            g.reg_alu(go, val, nonce, ALU.not_equal)

    _fence(kb, block, V)


def build_program(stages=9, dbg=False):
    kb = KB()
    nc = kb.nc
    es = kb.es

    def din(name, shape, dt=F32):
        return nc.dram_tensor(name, shape, dt, kind="ExternalInput").ap()

    def dint(name, shape, dt, shared=False):
        if shared:
            return nc.dram_tensor(name, shape, dt, addr_space="Shared").ap()
        return nc.dram_tensor(name, shape, dt).ap()

    x_in = din("x_b", [T, D])
    codes_in = {"g": din("g_codesT_h", [D, DH], I32),
                "u": din("u_codesT_h", [D, DH], I32),
                "d": din("d_codesT_h", [DH, D], I32)}
    am_in = {"g": din("g_absmaxT_h", [64, DH]),
             "u": din("u_absmaxT_h", [64, DH]),
             "d": din("d_absmaxT_h", [86, D])}
    la_in = {"g": din("g_lora_a", [D, R]),
             "u": din("u_lora_a", [D, R]),
             "d": din("d_lora_a_h", [DH, R])}
    lb_in = {"g": din("g_lora_b_h", [R, DH]),
             "u": din("u_lora_b_h", [R, DH]),
             "d": din("d_lora_b", [R, D])}
    wq_in = din("wqT", [D, D])
    wk_in = din("wkT_h", [D, 512])
    wv_in = din("wvT_h", [D, 512])
    wo_in = din("woT", [D, D])
    cosk_in = din("cosk4", [T, 512])
    sink_in = din("sink4", [T, 512])
    cosq_in = din("cosq4", [QBLK, 512])
    sinq_in = din("sinq4", [QBLK, 512])
    mask_in = din("mask01", [QBLK, T], BF16)
    identb_in = din("identb", [128, 128], BF16)
    identf_in = din("identf", [128, 128], F32)
    meta_in = din("meta", [1, 8], I32)

    out_x3 = nc.dram_tensor("out_x3", [QBLK, D], F32, kind="ExternalOutput").ap()
    if dbg:
        dbg_w = nc.dram_tensor("dbg_w", [256, DH], BF16, kind="ExternalOutput").ap()
        dbg_x1 = nc.dram_tensor("dbg_x1", [T, D], F32, kind="ExternalOutput").ap()
        dbg_pa = [nc.dram_tensor(f"dbg_pa{m}", [T, D], F32, kind="ExternalOutput").ap()
                  for m in range(2)]
        dbg_ab1 = nc.dram_tensor("dbg_ab1", [128, NDH * CHK], BF16, kind="ExternalOutput").ap()
        dbg_ab2 = nc.dram_tensor("dbg_ab2", [128, NID * CHK], BF16, kind="ExternalOutput").ap()
        dbg_P = nc.dram_tensor("dbg_P", [128, 16384], BF16, kind="ExternalOutput").ap()
        dbg_mn = nc.dram_tensor("dbg_mn", [128, 48], F32, kind="ExternalOutput").ap()
        dbg_k = nc.dram_tensor("dbg_k", [KVW, T], BF16, kind="ExternalOutput").ap()
        dbg_v = nc.dram_tensor("dbg_v", [T, KVW], BF16, kind="ExternalOutput").ap()
        dbg_q = nc.dram_tensor("dbg_q", [D, QBLK], BF16, kind="ExternalOutput").ap()
        dbg_x2 = nc.dram_tensor("dbg_x2", [QBLK, D], F32, kind="ExternalOutput").ap()

    w_cache = {"g": dint("wg_cache", [D, DH], BF16),
               "u": dint("wu_cache", [D, DH], BF16),
               "d": dint("wd_cache", [DH, D], BF16)}
    lb_cache = dint("lb_cache", [16, 2 * DH + D], BF16)
    wkv_cache = dint("wkv_cache", [D, 1024], BF16)
    wq_cache = dint("wq_cache", [D, D], BF16)
    wo_cache = dint("wo_cache", [D, D], BF16)
    q_fm = dint("q_fm", [D, QBLK], BF16)
    x1own_dram = dint("x1own", [QBLK, D], F32)
    x2_dram = dint("x2_dram", [QBLK, D], F32)

    pa_sh = [dint(f"pa_sh{m}", [T, D], F32, shared=True) for m in range(2)]
    x1_sh = dint("x1_sh", [T, D], F32, shared=True)
    k_sh = dint("k_sh", [KVW, T], BF16, shared=True)
    v_sh = dint("v_sh", [T, KVW], BF16, shared=True)
    ln3_sh = dint("ln3_sh", [4 * NID * 128, CHK], BF16, shared=True)
    pc_sh = [dint(f"pc_sh{m}", [2 * QBLK, D], F32, shared=True) for m in range(2)]
    flags_sh = dint("flags_sh", [2, 16], I32, shared=True)

    def sb(name, shape, dt):
        return es.enter_context(nc.sbuf_tensor(name, shape, dt))

    ab1 = sb("ab1", [128, NDH * CHK], BF16)
    ab2 = sb("ab2", [128, NID * CHK], BF16)
    s_codes = [sb(f"s_codes{i}", [128, 512], I32) for i in range(2)]
    s_du = sb("s_du", [128, 512], F32)
    s_ds = sb("s_ds", [128, 512], F32)
    s_dz = sb("s_dz", [128, 512], F32)
    s_pe = [sb(f"s_pe{i}", [128, 512], F32) for i in range(4)]
    s_po = [sb(f"s_po{i}", [128, 512], F32) for i in range(4)]
    s_te = sb("s_te", [128, 512], F32)
    s_to = sb("s_to", [128, 512], F32)
    s_wout = [sb(f"s_wout{i}", [128, 512], BF16) for i in range(2)]
    s_arow = [sb(f"s_arow{i}", [1, 512], F32) for i in range(2)]
    s_brow = [sb(f"s_brow{i}", [1, 512], F32) for i in range(2)]
    s_onesf = sb("s_onesf", [1, 64], F32)
    s_xt = sb("s_xt", [128, 2048], F32)
    s_p0 = sb("s_p0", [128, 2048], F32)
    s_p1 = sb("s_p1", [128, 2048], F32)
    s_ln = sb("s_ln", [128, 2048], F32)
    s_lnb = sb("s_lnb", [128, D], BF16)
    s_stat = sb("s_stat", [128, 8], F32)
    s_wg = [sb(f"s_wg{i}", [128, 512], BF16) for i in range(2)]
    s_wb = [sb(f"s_wb{i}", [128, 512], BF16) for i in range(2)]
    s_lbt = [sb(f"s_lbt{i}", [16, 512], BF16) for i in range(2)]
    s_y1 = sb("s_y1", [128, 2048], BF16)
    s_ev = [sb(f"s_ev{i}", [128, 512], F32) for i in range(2)]
    s_ptm = sb("s_ptm", [128, 2048], F32)
    s_la = sb("s_la", [128, (2 * NID + NDH) * R], BF16)
    s_t1 = {"g": sb("s_t1g", [16, 512], BF16),
            "u": sb("s_t1u", [16, 512], BF16),
            "d": sb("s_t1d", [16, 512], BF16)}
    s_rot = sb("s_rot", [128, 512], F32)
    s_ra = sb("s_ra", [128, 512], F32)
    s_kfm = sb("s_kfm", [128, T], BF16)
    s_qh = sb("s_qh", [128, CHK], BF16)
    s_ident = sb("s_ident", [128, 128], BF16)
    s_identf = sb("s_identf", [128, 128], F32)
    s_pt = sb("s_pt", [128, CHK], BF16)
    s_mask = [sb(f"s_mask{i}", [128, 2048], BF16) for i in range(2)]
    s_mneg = sb("s_mneg", [128, 40], F32)
    s_sig = sb("s_sig", [128, 8], F32)
    s_rcp = sb("s_rcp", [128, 8], F32)
    s_flag = sb("s_flag", [1, 8], I32)
    s_meta = sb("s_meta", [1, 8], I32)

    def ps(name, shape, dt):
        return es.enter_context(nc.psum_tensor(name, shape, dt))

    p_mm = [ps(f"p_mm{i}", [128, 512], F32) for i in range(4)]
    p_ma = [ps(f"p_ma{i}", [128, 512], F32) for i in range(2)]
    p_tr = ps("p_tr", [128, 512], F32)
    p_trb = ps("p_trb", [128, 1024], BF16)

    deq_tiles = []
    for mat in ("g", "u", "d"):
        nit = NID if mat != "d" else NDH
        width = DH if mat != "d" else D
        for it in range(nit):
            for oc in range((width + 511) // 512):
                deq_tiles.append((mat, it, oc, min(512, width - oc * 512)))

    V = dict(locals())
    V["dbg"] = dbg

    with es:
        with nc.Block() as block:
            _build_all(kb, block, V, stages, dbg)
    return nc



def _ln_to_fm(kb, block, V, src_fn, n_tt):
    ab2 = V["ab2"]
    lx = kb.new_sem()
    ctr = kb.new_sem()
    acc = kb.new_sem()
    tT = kb.new_sem()
    fm = kb.new_sem()

    @block.sync
    def _(sync):
        for tt in range(n_tt):
            for half in range(2):
                if half == 0 and tt > 0:
                    sync.wait_ge(ctr.h, ctr.t(5 * tt - 2))
                sync.dma_start(out=(V["s_xt"] if half == 0 else V["s_p0"])[:, :],
                               in_=src_fn(tt, half)).then_inc(lx.h, lx.inc(16))

    @block.scalar
    def _(scalar):
        for tt in range(n_tt):
            scalar.wait_ge(ctr.h, ctr.t(5 * tt + 2))
            scalar.activation(out=V["s_xt"][:, :], in_=V["s_ln"][:, :],
                              func=AF.Square, accum_out=V["s_stat"][:, 2:3])
            scalar.activation(out=V["s_p0"][:, :], in_=V["s_p1"][:, :],
                              func=AF.Square, accum_out=V["s_stat"][:, 3:4]) \
                .then_inc(acc.h, acc.inc(1))
            scalar.wait_ge(ctr.h, ctr.t(5 * tt + 3))
            scalar.activation(out=V["s_stat"][:, 5:6], in_=V["s_stat"][:, 4:5],
                              func=AF.Ln)
            scalar.activation(out=V["s_y1"][:, 0:512], in_=V["s_y1"][:, 0:512],
                              func=AF.Copy)
            scalar.activation(out=V["s_stat"][:, 6:7], in_=V["s_stat"][:, 5:6],
                              func=AF.Exp, scale=-0.5).then_inc(acc.h, acc.inc(1))

    @block.tensor
    def _(tensor):
        for tt in range(n_tt):
            tensor.wait_ge(ctr.h, ctr.t(5 * tt + 5))
            for ig in range(4):
                g = tt * 4 + ig
                if g >= 1:
                    tensor.wait_ge(fm.h, fm.t(g))
                for i8 in range(8):
                    i = ig * 8 + i8
                    ins = tensor.transpose(out=V["p_trb"][:, i8 * 128:(i8 + 1) * 128],
                                           in_=V["s_lnb"][:, i * 128:(i + 1) * 128],
                                           identity=V["s_ident"][:, :])
                ins.then_inc(tT.h, tT.inc(1))

    @block.vector
    def _(vector):
        for tt in range(n_tt):
            vector.wait_ge(lx.h, lx.t((2 * tt + 2) * 16))
            vector.reduce_sum(out=V["s_stat"][:, 0:1], in_=V["s_xt"][:, :], axis=AX.X)
            vector.reduce_sum(out=V["s_stat"][:, 1:2], in_=V["s_p0"][:, :], axis=AX.X)
            _dsep(vector, V)
            vector.tensor_add(out=V["s_stat"][:, 0:1], in0=V["s_stat"][:, 0:1],
                              in1=V["s_stat"][:, 1:2])
            _dsep(vector, V)
            vector.tensor_scalar(out=V["s_stat"][:, 1:2], in0=V["s_stat"][:, 0:1],
                                 scalar1=1.0 / D, scalar2=None, op0=ALU.mult)
            _dsep(vector, V)
            vector.tensor_scalar(out=V["s_ln"][:, :], in0=V["s_xt"][:, :],
                                 scalar1=V["s_stat"][:, 1:2], scalar2=None,
                                 op0=ALU.subtract).then_inc(ctr.h, ctr.inc(1))
            vector.tensor_scalar(out=V["s_p1"][:, :], in0=V["s_p0"][:, :],
                                 scalar1=V["s_stat"][:, 1:2], scalar2=None,
                                 op0=ALU.subtract).then_inc(ctr.h, ctr.inc(1))
            vector.wait_ge(acc.h, acc.t(2 * tt + 1))
            vector.tensor_add(out=V["s_stat"][:, 4:5], in0=V["s_stat"][:, 2:3],
                              in1=V["s_stat"][:, 3:4])
            _dsep(vector, V)
            vector.tensor_scalar(out=V["s_stat"][:, 4:5], in0=V["s_stat"][:, 4:5],
                                 scalar1=1.0 / D, scalar2=EPS,
                                 op0=ALU.mult, op1=ALU.add).then_inc(ctr.h, ctr.inc(1))
            vector.wait_ge(acc.h, acc.t(2 * tt + 2))
            if tt > 0:
                vector.wait_ge(tT.h, tT.t(tt * 4))
            vector.tensor_scalar(out=V["s_lnb"][:, 0:2048], in0=V["s_ln"][:, :],
                                 scalar1=V["s_stat"][:, 6:7], scalar2=None,
                                 op0=ALU.mult).then_inc(ctr.h, ctr.inc(1))
            vector.tensor_scalar(out=V["s_lnb"][:, 2048:4096], in0=V["s_p1"][:, :],
                                 scalar1=V["s_stat"][:, 6:7], scalar2=None,
                                 op0=ALU.mult).then_inc(ctr.h, ctr.inc(1))
            for ig in range(4):
                vector.wait_ge(tT.h, tT.t(tt * 4 + ig + 1))
                vector.tensor_copy(
                    out=ab2[:, :].rearrange("p (i t) -> p i t", t=CHK)
                        [:, ig * 8:(ig + 1) * 8, tt * 128:(tt + 1) * 128],
                    in_=V["p_trb"][:, :].rearrange("p (a b) -> p a b", b=128)) \
                    .then_inc(fm.h, fm.inc(1))

    return fm



def _ffn(kb, block, V, phase):
    ab1, ab2 = V["ab1"], V["ab2"]
    p_mm, p_tr = V["p_mm"], V["p_tr"]
    ntc = 8 if phase == "A" else 4
    psh = V["pa_sh"] if phase == "A" else V["pc_sh"]
    NOC = 11
    ow = [512] * 10 + [384]
    NDC = 8

    for tc in range(ntc):
        t0 = tc * CHK
        if phase == "A":
            fm = _ln_to_fm(kb, block, V,
                           lambda tt, half, t0=t0: V["x_in"][t0 + tt * 128:t0 + (tt + 1) * 128,
                                                             half * 2048:(half + 1) * 2048],
                           NTT)
            fm_target = 4 * NTT
        else:
            fm = kb.new_sem()

            @block.sync
            def _(sync, tc=tc):
                for g8 in range(8):
                    sync.dma_start(
                        out=ab2[:, g8 * 2048:(g8 + 1) * 2048]
                            .rearrange("p (i t) -> p i t", t=CHK),
                        in_=V["ln3_sh"][(tc * NID + g8 * 4) * 128:(tc * NID + (g8 + 1) * 4) * 128, :]
                            .rearrange("(i p) t -> p i t", p=128)) \
                        .then_inc(fm.h, fm.inc(16))
            fm_target = 8 * 16
        _fence(kb, block, V)

        lt = kb.new_sem()
        le = kb.new_sem()
        lw = [kb.new_sem() for _ in range(4)]
        lbl = [kb.new_sem(), kb.new_sem()]
        WB4 = [V["s_wg"][0], V["s_wg"][1], V["s_wb"][0], V["s_wb"][1]]
        mmi = kb.new_sem()
        grp = kb.new_sem()
        rel = kb.new_sem()
        x3s = kb.new_sem()

        @block.sync
        def _(sync):
            k = 0
            for oc in range(NOC):
                for mj, mat in enumerate(("g", "u")):
                    kb2 = 2 * oc + mj
                    if kb2 >= 2:
                        sync.wait_ge(grp.h, grp.t(kb2 - 1))
                    lbo = 0 if mat == "g" else DH
                    sync.dma_start(out=V["s_lbt"][kb2 % 2][0:16, 0:ow[oc]],
                                   in_=V["lb_cache"][0:16, lbo + oc * 512:lbo + oc * 512 + ow[oc]]) \
                        .then_inc(lbl[kb2 % 2].h, lbl[kb2 % 2].inc(16))
                    for i in range(NID):
                        if k % 2 == 0:
                            if k >= 4:
                                sync.wait_ge(mmi.h, mmi.t(k - 3))
                            sync.dma_start(out=WB4[k % 4][:, 0:ow[oc]],
                                           in_=V["w_cache"][mat][i * 128:(i + 1) * 128,
                                                                 oc * 512:oc * 512 + ow[oc]]) \
                                .then_inc(lw[k % 4].h, lw[k % 4].inc(16))
                        k += 1

        @block.tensor
        def _(tensor):
            tensor.wait_ge(fm.h, fm.t(fm_target))
            for j, mat in enumerate(("g", "u")):
                for i in range(NID):
                    ins = tensor.matmul(out=p_mm[j][0:16, :],
                                  lhsT=V["s_la"][:, (j * NID + i) * R:(j * NID + i + 1) * R],
                                  rhs=ab2[:, i * CHK:(i + 1) * CHK],
                                  start=(i == 0), stop=(i == NID - 1))
                ins.then_inc(lt.h, lt.inc(1))
            tensor.wait_ge(le.h, le.t(2))
            k = 0
            for oc in range(NOC):
                nsub = (ow[oc] + 127) // 128
                for mj, mat in enumerate(("g", "u")):
                    kb2 = 2 * oc + mj
                    if mj == 0 and oc >= 1:
                        tensor.wait_ge(x3s.h, x3s.t(oc))
                    for i in range(NID):
                        tensor.wait_ge(lw[k % 4].h, lw[k % 4].t((k // 4 + 1) * 16))
                        for sub in range(nsub):
                            ins = tensor.matmul(out=p_mm[sub][:, :],
                                          lhsT=WB4[k % 4][:, sub * 128:(sub + 1) * 128],
                                          rhs=ab2[:, i * CHK:(i + 1) * CHK],
                                          start=(i == 0), stop=False)
                        ins.then_inc(mmi.h, mmi.inc(1))
                        k += 1
                    tensor.wait_ge(lbl[kb2 % 2].h, lbl[kb2 % 2].t((kb2 // 2 + 1) * 16))
                    for sub in range(nsub):
                        ins = tensor.matmul(out=p_mm[sub][:, :],
                                      lhsT=V["s_lbt"][kb2 % 2][0:16, sub * 128:(sub + 1) * 128],
                                      rhs=V["s_t1"][mat][0:16, :],
                                      start=False, stop=True)
                    ins.then_inc(grp.h, grp.inc(1))
                    if mj == 0:
                        tensor.wait_ge(rel.h, rel.t(oc + 1))

        @block.scalar
        def _(scalar):
            k = 0
            for oc in range(NOC):
                nsub = (ow[oc] + 127) // 128
                for mj, mat in enumerate(("g", "u")):
                    for i in range(NID):
                        if k % 2 == 1:
                            if k >= 4:
                                scalar.wait_ge(mmi.h, mmi.t(k - 3))
                            scalar.dma_start(out=WB4[k % 4][:, 0:ow[oc]],
                                             in_=V["w_cache"][mat][i * 128:(i + 1) * 128,
                                                                   oc * 512:oc * 512 + ow[oc]]) \
                                .then_inc(lw[k % 4].h, lw[k % 4].inc(16))
                        k += 1
                    if mj == 0:
                        scalar.wait_ge(grp.h, grp.t(2 * oc + 1))
                        if oc >= 1:
                            scalar.wait_ge(x3s.h, x3s.t(oc))
                        for sub in range(nsub):
                            ins = scalar.activation(out=V["s_y1"][:, sub * 512:sub * 512 + 512],
                                                    in_=p_mm[sub][:, :], func=AF.Relu)
                        ins.then_inc(rel.h, rel.inc(1))

        @block.vector
        def _(vector):
            for j, mat in enumerate(("g", "u")):
                vector.wait_ge(lt.h, lt.t(j + 1))
                vector.tensor_copy(out=V["s_t1"][mat][0:16, :],
                                   in_=p_mm[j][0:16, :]).then_inc(le.h, le.inc(1))
            for oc in range(NOC):
                nsub = (ow[oc] + 127) // 128
                vector.wait_ge(grp.h, grp.t(2 * oc + 2))
                vector.wait_ge(rel.h, rel.t(oc + 1))
                for sub in range(nsub):
                    ot = oc * 4 + sub
                    ins = vector.tensor_mul(out=ab1[:, ot * CHK:(ot + 1) * CHK],
                                            in0=V["s_y1"][:, sub * 512:sub * 512 + 512],
                                            in1=p_mm[sub][:, :])
                ins.then_inc(x3s.h, x3s.inc(1))

        ltd = kb.new_sem()
        led = kb.new_sem()
        lwd = [kb.new_sem() for _ in range(4)]
        lbd = [kb.new_sem(), kb.new_sem()]
        mdi = kb.new_sem()
        mdd = kb.new_sem()
        evd = kb.new_sem()
        trd = kb.new_sem()
        ptd = kb.new_sem()
        std = kb.new_sem()

        @block.sync
        def _(sync):
            k = 0
            for dc in range(NDC):
                if dc < 2:
                    sync.wait_ge(grp.h, grp.t(2 * NOC))
                if dc >= 2:
                    sync.wait_ge(mdd.h, mdd.t(dc - 1))
                sync.dma_start(out=V["s_lbt"][dc % 2][0:16, :],
                               in_=V["lb_cache"][0:16, 2 * DH + dc * 512:2 * DH + (dc + 1) * 512]) \
                    .then_inc(lbd[dc % 2].h, lbd[dc % 2].inc(16))
                for i in range(NDH):
                    if k % 2 == 0:
                        if k >= 4:
                            sync.wait_ge(mdi.h, mdi.t(k - 3))
                        sync.dma_start(out=WB4[k % 4][:, :],
                                       in_=V["w_cache"]["d"][i * 128:(i + 1) * 128,
                                                             dc * 512:(dc + 1) * 512]) \
                            .then_inc(lwd[k % 4].h, lwd[k % 4].inc(16))
                    k += 1

        @block.scalar
        def _(scalar):
            scalar.wait_ge(grp.h, grp.t(2 * NOC))
            k = 0
            for dc in range(NDC):
                for i in range(NDH):
                    if k % 2 == 1:
                        if k >= 4:
                            scalar.wait_ge(mdi.h, mdi.t(k - 3))
                        scalar.dma_start(out=WB4[k % 4][:, :],
                                         in_=V["w_cache"]["d"][i * 128:(i + 1) * 128,
                                                               dc * 512:(dc + 1) * 512]) \
                            .then_inc(lwd[k % 4].h, lwd[k % 4].inc(16))
                    k += 1

        @block.tensor
        def _(tensor):
            tensor.wait_ge(x3s.h, x3s.t(NOC))
            for i in range(NDH):
                ins = tensor.matmul(out=p_mm[0][0:16, :],
                              lhsT=V["s_la"][:, (2 * NID + i) * R:(2 * NID + i + 1) * R],
                              rhs=ab1[:, i * CHK:(i + 1) * CHK],
                              start=(i == 0), stop=(i == NDH - 1))
            ins.then_inc(ltd.h, ltd.inc(1))
            tensor.wait_ge(led.h, led.t(1))
            k = 0
            for dc in range(NDC):
                if dc >= 1:
                    tensor.wait_ge(evd.h, evd.t(4 * dc))
                for i in range(NDH):
                    tensor.wait_ge(lwd[k % 4].h, lwd[k % 4].t((k // 4 + 1) * 16))
                    for sub in range(4):
                        ins = tensor.matmul(out=p_mm[sub][:, :],
                                      lhsT=WB4[k % 4][:, sub * 128:(sub + 1) * 128],
                                      rhs=ab1[:, i * CHK:(i + 1) * CHK],
                                      start=(i == 0), stop=False)
                    ins.then_inc(mdi.h, mdi.inc(1))
                    k += 1
                tensor.wait_ge(lbd[dc % 2].h, lbd[dc % 2].t((dc // 2 + 1) * 16))
                for sub in range(4):
                    ins = tensor.matmul(out=p_mm[sub][:, :],
                                  lhsT=V["s_lbt"][dc % 2][0:16, sub * 128:(sub + 1) * 128],
                                  rhs=V["s_t1"]["d"][0:16, :],
                                  start=False, stop=True)
                ins.then_inc(mdd.h, mdd.inc(1))
                for sub in range(4):
                    dt = 4 * dc + sub
                    tensor.wait_ge(evd.h, evd.t(dt + 1))
                    if dt >= 1:
                        tensor.wait_ge(ptd.h, ptd.t(dt))
                    for j in range(4):
                        ins = tensor.transpose(out=p_tr[:, j * 128:(j + 1) * 128],
                                         in_=V["s_ev"][sub % 2][:, j * 128:(j + 1) * 128],
                                         identity=V["s_identf"][:, :])
                    ins.then_inc(trd.h, trd.inc(1))

        @block.vector
        def _(vector):
            vector.wait_ge(ltd.h, ltd.t(1))
            vector.tensor_copy(out=V["s_t1"]["d"][0:16, :],
                               in_=p_mm[0][0:16, :]).then_inc(led.h, led.inc(1))
            for dc in range(NDC):
                vector.wait_ge(mdd.h, mdd.t(dc + 1))
                for sub in range(4):
                    dt = 4 * dc + sub
                    if dt >= 2:
                        vector.wait_ge(trd.h, trd.t(dt - 1))
                    vector.tensor_copy(out=V["s_ev"][sub % 2][:, :],
                                       in_=p_mm[sub][:, :]).then_inc(evd.h, evd.inc(1))
                    vector.wait_ge(trd.h, trd.t(dt + 1))
                    if dc >= 1 and sub == 0:
                        vector.wait_ge(std.h, std.t(dc * 64))
                    vector.tensor_copy(
                        out=V["s_ptm"][:, :].rearrange("p (j s c) -> p j s c", s=4, c=128)
                            [:, :, sub, :],
                        in_=p_tr[:, :].rearrange("p (j c) -> p j c", c=128)) \
                        .then_inc(ptd.h, ptd.inc(1))

        @block.gpsimd
        def _(gps, tc=tc, t0=t0):
            member = _getreg(gps, V, "g_member")
            gps.load(member, V["s_meta"][0:1, 0:1])
            for dc in range(NDC):
                gps.wait_ge(ptd.h, ptd.t(4 * dc + 4))
                for j in range(4):
                    row = t0 + j * 128
                    src = V["s_ptm"][:, j * 512:(j + 1) * 512]
                    with gps.If_eq(member, 0):
                        gps.dma_start(out=psh[0][row:row + 128, dc * 512:(dc + 1) * 512],
                                      in_=src).then_inc(std.h, std.inc(16))
                    with gps.Else():
                        gps.dma_start(out=psh[1][row:row + 128, dc * 512:(dc + 1) * 512],
                                      in_=src).then_inc(std.h, 16)
            gps.wait_ge(std.h, std.n)
            if V.get("dbg") and phase == "A" and tc == 0:
                gps.dma_start(out=V["dbg_ab1"], in_=ab1[:, :]).then_inc(std.h, std.inc(16))
                gps.dma_start(out=V["dbg_ab2"], in_=ab2[:, :]).then_inc(std.h, std.inc(16))
                gps.wait_ge(std.h, std.n)

        _fence(kb, block, V)



def _qkv(kb, block, V):
    ab2 = V["ab2"]
    p_mm = V["p_mm"]

    for tc in range(8):
        t0 = tc * CHK
        fm = _ln_to_fm(kb, block, V,
                       lambda tt, half, t0=t0: V["x1_sh"][t0 + tt * 128:t0 + (tt + 1) * 128,
                                                          half * 2048:(half + 1) * 2048],
                       NTT)
        _fence(kb, block, V)

        WB4 = [V["s_wg"][0], V["s_wg"][1], V["s_wb"][0], V["s_wb"][1]]
        for j in range(2):
            lw = [kb.new_sem() for _ in range(4)]
            mmi = kb.new_sem()
            kro = kb.new_sem()
            kfe = kb.new_sem()
            ktT = kb.new_sem()
            stk = kb.new_sem()
            csl = kb.new_sem()

            @block.sync
            def _(sync, j=j, t0=t0):
                if j == 0:
                    sync.dma_start(out=V["s_xt"][:, :].rearrange("p (a c) -> p a c", c=512),
                                   in_=V["cosk_in"][t0:t0 + 512, :]
                                   .rearrange("(a p) c -> p a c", p=128)) \
                        .then_inc(csl.h, csl.inc(16))
                    sync.dma_start(out=V["s_p0"][:, :].rearrange("p (a c) -> p a c", c=512),
                                   in_=V["sink_in"][t0:t0 + 512, :]
                                   .rearrange("(a p) c -> p a c", p=128)) \
                        .then_inc(csl.h, csl.inc(16))
                k = 0
                for ttp in range(2):
                    for i in range(NID):
                        if k % 2 == 0:
                            if k >= 4:
                                sync.wait_ge(mmi.h, mmi.t(k - 3))
                            sync.dma_start(out=WB4[k % 4][:, :],
                                           in_=V["wkv_cache"][i * 128:(i + 1) * 128,
                                                              j * 512:(j + 1) * 512]) \
                                .then_inc(lw[k % 4].h, lw[k % 4].inc(16))
                        k += 1

            @block.scalar
            def _(scalar, j=j):
                k = 0
                for ttp in range(2):
                    for i in range(NID):
                        if k % 2 == 1:
                            if k >= 4:
                                scalar.wait_ge(mmi.h, mmi.t(k - 3))
                            scalar.dma_start(out=WB4[k % 4][:, :],
                                             in_=V["wkv_cache"][i * 128:(i + 1) * 128,
                                                                j * 512:(j + 1) * 512]) \
                                .then_inc(lw[k % 4].h, lw[k % 4].inc(16))
                        k += 1

            @block.tensor
            def _(tensor, j=j):
                tensor.wait_ge(fm.h, fm.t(4 * NTT))
                k = 0
                for ttp in range(2):
                    if ttp == 1:
                        tensor.wait_ge(kro.h, kro.t(2))
                    for i in range(NID):
                        tensor.wait_ge(lw[k % 4].h, lw[k % 4].t((k // 4 + 1) * 16))
                        for h2 in range(2):
                            tt = ttp * 2 + h2
                            ins = tensor.matmul(out=p_mm[h2][:, :],
                                          lhsT=ab2[:, i * CHK + tt * 128:i * CHK + (tt + 1) * 128],
                                          rhs=WB4[k % 4][:, :],
                                          start=(i == 0), stop=(i == NID - 1))
                        ins.then_inc(mmi.h, mmi.inc(1))
                        k += 1
                    if j == 0:
                        for h2 in range(2):
                            tt = ttp * 2 + h2
                            tensor.wait_ge(kro.h, kro.t(tt + 1))
                            if tt >= 1:
                                tensor.wait_ge(kfe.h, kfe.t(tt))
                            for sub in range(4):
                                ins = tensor.transpose(out=V["p_trb"][:, sub * 128:(sub + 1) * 128],
                                                 in_=V["s_lnb"][:, sub * 128:(sub + 1) * 128],
                                                 identity=V["s_ident"][:, :])
                            ins.then_inc(ktT.h, ktT.inc(1))

            @block.vector
            def _(vector, j=j):
                if j == 0:
                    vector.wait_ge(csl.h, csl.t(32))
                for ttp in range(2):
                    vector.wait_ge(mmi.h, mmi.t((ttp + 1) * 32))
                    for h2 in range(2):
                        tt = ttp * 2 + h2
                        if j == 1:
                            if tt >= 1:
                                vector.wait_ge(stk.h, stk.t(tt * 16))
                            vector.tensor_copy(out=V["s_lnb"][:, 0:512],
                                               in_=p_mm[h2][:, :]) \
                                .then_inc(kro.h, kro.inc(1))
                        else:
                            pm3 = p_mm[h2][:, :].rearrange("p (a c) -> p a c", c=128)
                            rt3 = V["s_rot"][:, :].rearrange("p (a c) -> p a c", c=128)
                            vector.tensor_scalar(out=rt3[:, :, 0:64], in0=pm3[:, :, 64:128],
                                                 scalar1=-1.0, scalar2=None, op0=ALU.mult)
                            vector.tensor_copy(out=rt3[:, :, 64:128], in_=pm3[:, :, 0:64])
                            vector.tensor_mul(out=V["s_ra"][:, :], in0=p_mm[h2][:, :],
                                              in1=V["s_xt"][:, tt * 512:(tt + 1) * 512])
                            vector.tensor_mul(out=V["s_rot"][:, :], in0=V["s_rot"][:, :],
                                              in1=V["s_p0"][:, tt * 512:(tt + 1) * 512])
                            if tt >= 1:
                                vector.wait_ge(ktT.h, ktT.t(tt))
                            vector.tensor_add(out=V["s_lnb"][:, 0:512], in0=V["s_ra"][:, :],
                                              in1=V["s_rot"][:, :]).then_inc(kro.h, kro.inc(1))
                            vector.wait_ge(ktT.h, ktT.t(tt + 1))
                            vector.tensor_copy(
                                out=V["s_kfm"][:, 0:2048].rearrange("p (s t) -> p s t", t=512)
                                    [:, :, tt * 128:(tt + 1) * 128],
                                in_=V["p_trb"][:, 0:512].rearrange("p (s c) -> p s c", c=128)) \
                                .then_inc(kfe.h, kfe.inc(1))

            @block.gpsimd
            def _(gps, j=j, t0=t0):
                gmem = _getreg(gps, V, "g_member")
                gps.load(gmem, V["s_meta"][0:1, 0:1])
                if j == 0:
                    gps.wait_ge(kfe.h, kfe.t(4))
                    for sub in range(4):
                        with gps.If_eq(gmem, 0):
                            gps.dma_start(out=V["k_sh"][sub * 128:(sub + 1) * 128, t0:t0 + 512],
                                          in_=V["s_kfm"][:, sub * 512:(sub + 1) * 512]) \
                                .then_inc(stk.h, stk.inc(16))
                        with gps.Else():
                            gps.dma_start(out=V["k_sh"][512 + sub * 128:512 + (sub + 1) * 128,
                                                        t0:t0 + 512],
                                          in_=V["s_kfm"][:, sub * 512:(sub + 1) * 512]) \
                                .then_inc(stk.h, 16)
                else:
                    for tt in range(NTT):
                        gps.wait_ge(kro.h, kro.t(tt + 1))
                        r = t0 + tt * 128
                        with gps.If_eq(gmem, 0):
                            gps.dma_start(out=V["v_sh"][r:r + 128, 0:512],
                                          in_=V["s_lnb"][:, 0:512]).then_inc(stk.h, stk.inc(16))
                        with gps.Else():
                            gps.dma_start(out=V["v_sh"][r:r + 128, 512:1024],
                                          in_=V["s_lnb"][:, 0:512]).then_inc(stk.h, 16)
                gps.wait_ge(stk.h, stk.n)

            _fence(kb, block, V)

    for qc2 in range(2):
        q0 = qc2 * CHK
        fmq = _ln_to_fm(kb, block, V,
                        lambda tt, half, q0=q0: V["x1own_dram"][q0 + tt * 128:q0 + (tt + 1) * 128,
                                                                half * 2048:(half + 1) * 2048],
                        NTT)
        _fence(kb, block, V)

        lwq = [kb.new_sem() for _ in range(4)]
        WB4 = [V["s_wg"][0], V["s_wg"][1], V["s_wb"][0], V["s_wb"][1]]
        mmq = kb.new_sem()
        rpq = kb.new_sem()
        stq = kb.new_sem()
        csl = kb.new_sem()
        qtT = kb.new_sem()
        qfe = kb.new_sem()

        @block.sync
        def _(sync, q0=q0):
            sync.dma_start(out=V["s_xt"][:, :].rearrange("p (a c) -> p a c", c=512),
                           in_=V["cosq_in"][q0:q0 + 512, :]
                           .rearrange("(a p) c -> p a c", p=128)).then_inc(csl.h, csl.inc(16))
            sync.dma_start(out=V["s_p0"][:, :].rearrange("p (a c) -> p a c", c=512),
                           in_=V["sinq_in"][q0:q0 + 512, :]
                           .rearrange("(a p) c -> p a c", p=128)).then_inc(csl.h, csl.inc(16))
            k = 0
            for qc in range(8):
                for ttp in range(2):
                    for i in range(NID):
                        if k % 2 == 0:
                            if k >= 4:
                                sync.wait_ge(mmq.h, mmq.t(k - 3))
                            sync.dma_start(out=WB4[k % 4][:, :],
                                           in_=V["wq_cache"][i * 128:(i + 1) * 128,
                                                             qc * 512:(qc + 1) * 512]) \
                                .then_inc(lwq[k % 4].h, lwq[k % 4].inc(16))
                        k += 1

        @block.scalar
        def _(scalar):
            k = 0
            for qc in range(8):
                for ttp in range(2):
                    for i in range(NID):
                        if k % 2 == 1:
                            if k >= 4:
                                scalar.wait_ge(mmq.h, mmq.t(k - 3))
                            scalar.dma_start(out=WB4[k % 4][:, :],
                                             in_=V["wq_cache"][i * 128:(i + 1) * 128,
                                                               qc * 512:(qc + 1) * 512]) \
                                .then_inc(lwq[k % 4].h, lwq[k % 4].inc(16))
                        k += 1

        @block.tensor
        def _(tensor):
            tensor.wait_ge(fmq.h, fmq.t(4 * NTT))
            k = 0
            for qc in range(8):
                for ttp in range(2):
                    g2 = qc * 2 + ttp
                    if g2 >= 1:
                        tensor.wait_ge(rpq.h, rpq.t(g2 * 2))
                    for i in range(NID):
                        tensor.wait_ge(lwq[k % 4].h, lwq[k % 4].t((k // 4 + 1) * 16))
                        for h2 in range(2):
                            tt = ttp * 2 + h2
                            ins = tensor.matmul(out=p_mm[h2][:, :],
                                          lhsT=ab2[:, i * CHK + tt * 128:i * CHK + (tt + 1) * 128],
                                          rhs=WB4[k % 4][:, :],
                                          start=(i == 0), stop=(i == NID - 1))
                        ins.then_inc(mmq.h, mmq.inc(1))
                        k += 1
                    for h2 in range(2):
                        tt = ttp * 2 + h2
                        gtt = qc * 4 + tt
                        tensor.wait_ge(rpq.h, rpq.t(gtt + 1))
                        if gtt >= 1:
                            tensor.wait_ge(qfe.h, qfe.t(gtt))
                        for sub in range(4):
                            ins = tensor.transpose(out=V["p_trb"][:, sub * 128:(sub + 1) * 128],
                                             in_=V["s_lnb"][:, sub * 128:(sub + 1) * 128],
                                             identity=V["s_ident"][:, :])
                        ins.then_inc(qtT.h, qtT.inc(1))


        @block.vector
        def _(vector):
            vector.wait_ge(csl.h, csl.t(32))
            idx = 0
            for qc in range(8):
                for ttp in range(2):
                    vector.wait_ge(mmq.h, mmq.t((qc * 2 + ttp + 1) * 32))
                    for h2 in range(2):
                        tt = ttp * 2 + h2
                        pm3 = p_mm[h2][:, :].rearrange("p (a c) -> p a c", c=128)
                        rt3 = V["s_rot"][:, :].rearrange("p (a c) -> p a c", c=128)
                        vector.tensor_scalar(out=rt3[:, :, 0:64], in0=pm3[:, :, 64:128],
                                             scalar1=-1.0, scalar2=None, op0=ALU.mult)
                        vector.tensor_copy(out=rt3[:, :, 64:128], in_=pm3[:, :, 0:64])
                        vector.tensor_mul(out=V["s_ra"][:, :], in0=p_mm[h2][:, :],
                                          in1=V["s_xt"][:, tt * 512:(tt + 1) * 512])
                        vector.tensor_mul(out=V["s_rot"][:, :], in0=V["s_rot"][:, :],
                                          in1=V["s_p0"][:, tt * 512:(tt + 1) * 512])
                        if idx >= 1:
                            vector.wait_ge(qtT.h, qtT.t(idx))
                        vector.tensor_add(out=V["s_lnb"][:, 0:512], in0=V["s_ra"][:, :],
                                          in1=V["s_rot"][:, :]).then_inc(rpq.h, rpq.inc(1))
                        vector.wait_ge(qtT.h, qtT.t(idx + 1))
                        if idx >= 4 and idx % 4 == 0:
                            vector.wait_ge(stq.h, stq.t(16 * idx))
                        vector.tensor_copy(
                            out=V["s_kfm"][:, 0:2048].rearrange("p (s t) -> p s t", t=512)
                                [:, :, tt * 128:(tt + 1) * 128],
                            in_=V["p_trb"][:, 0:512].rearrange("p (s c) -> p s c", c=128)) \
                            .then_inc(qfe.h, qfe.inc(1))
                        idx += 1

        @block.gpsimd
        def _(gps, q0=q0):
            for qc in range(8):
                gps.wait_ge(qfe.h, qfe.t((qc + 1) * 4))
                for sub in range(4):
                    gps.dma_start(out=V["q_fm"][qc * 512 + sub * 128:qc * 512 + (sub + 1) * 128,
                                                q0:q0 + 512],
                                  in_=V["s_kfm"][:, sub * 512:(sub + 1) * 512]) \
                        .then_inc(stq.h, stq.inc(16))
            gps.wait_ge(stq.h, stq.n)

        _fence(kb, block, V)



def _attn(kb, block, V):
    ab1, ab2 = V["ab1"], V["ab2"]
    p_mm, p_trb = V["p_mm"], V["p_trb"]

    grp_state = {}
    for qh in range(2):
        q0 = qh * CHK
        for h in range(NH):
            kvh = h // 4
            if h % 4 == 0:
                grp_state["klf"] = kb.new_sem()
                grp_state["vlk"] = kb.new_sem()
            klf = grp_state["klf"]
            vlk = grp_state["vlk"]
            ql = kb.new_sem()
            mm1 = kb.new_sem()
            rxp = kb.new_sem()
            rxf = kb.new_sem()
            mm2 = kb.new_sem()
            exl = kb.new_sem()
            mkl = [kb.new_sem(), kb.new_sem()]
            msk = kb.new_sem()
            sgm = kb.new_sem()
            ptT = kb.new_sem()
            pte = kb.new_sem()
            avm = kb.new_sem()

            @block.sync
            def _(sync, h=h, kvh=kvh, q0=q0):
                if h % 4 == 0:
                    sync.dma_start(out=V["s_kfm"][:, :],
                                   in_=V["k_sh"][kvh * 128:(kvh + 1) * 128, :]) \
                        .then_inc(klf.h, klf.inc(16))
                    for ks in range(32):
                        sync.dma_start(out=ab1[:, 16384 + ks * 128:16384 + (ks + 1) * 128],
                                       in_=V["v_sh"][ks * 128:(ks + 1) * 128,
                                                     kvh * 128:(kvh + 1) * 128]) \
                            .then_inc(vlk.h, vlk.inc(16))
                sync.dma_start(out=V["s_qh"][:, :],
                               in_=V["q_fm"][h * 128:(h + 1) * 128, q0:q0 + 512]) \
                    .then_inc(ql.h, ql.inc(16))
                for k in range(8):
                    qt, half = k // 2, k % 2
                    if k >= 2:
                        sync.wait_ge(msk.h, msk.t(k - 1))
                    sync.dma_start(out=V["s_mask"][k % 2][:, :],
                                   in_=V["mask_in"][q0 + qt * 128:q0 + (qt + 1) * 128,
                                                    half * 2048:(half + 1) * 2048]) \
                        .then_inc(mkl[k % 2].h, mkl[k % 2].inc(16))

            @block.tensor
            def _(tensor, h=h):
                tensor.wait_ge(ql.h, ql.t(16))
                tensor.wait_ge(klf.h, klf.t(16))
                for qt in range(4):
                    for uh in range(8):
                        idx = qt * 8 + uh
                        if idx >= 2:
                            tensor.wait_ge(rxp.h, rxp.t(idx - 1))
                        tensor.matmul(out=p_mm[uh % 2][:, :],
                                      lhsT=V["s_qh"][:, qt * 128:(qt + 1) * 128],
                                      rhs=V["s_kfm"][:, uh * 512:(uh + 1) * 512],
                                      start=True, stop=True) \
                            .then_inc(mm1.h, mm1.inc(1))
                tensor.wait_ge(rxf.h, rxf.t(4))
                for qt in range(4):
                    for uh in range(8):
                        idx = qt * 8 + uh
                        if idx >= 2:
                            tensor.wait_ge(exl.h, exl.t(idx - 1))
                        tensor.matmul(out=p_mm[uh % 2][:, :],
                                      lhsT=V["s_qh"][:, qt * 128:(qt + 1) * 128],
                                      rhs=V["s_kfm"][:, uh * 512:(uh + 1) * 512],
                                      start=True, stop=True) \
                            .then_inc(mm2.h, mm2.inc(1))
                tensor.wait_ge(sgm.h, sgm.t(5))
                if h % 4 == 0:
                    tensor.wait_ge(vlk.h, vlk.t(512))
                for ks in range(32):
                    if ks >= 1:
                        tensor.wait_ge(pte.h, pte.t(ks))
                    for qt in range(4):
                        ins = tensor.transpose(out=p_trb[:, qt * 128:(qt + 1) * 128],
                                         in_=ab1[:, qt * 4096 + ks * 128:qt * 4096 + ks * 128 + 128],
                                         identity=V["s_ident"][:, :])
                    ins.then_inc(ptT.h, ptT.inc(1))
                    tensor.wait_ge(pte.h, pte.t(ks + 1))
                    tensor.matmul(out=p_mm[2][:, :],
                                  lhsT=ab1[:, 16384 + ks * 128:16384 + (ks + 1) * 128],
                                  rhs=V["s_pt"][:, :],
                                  start=(ks == 0), stop=(ks == 31)) \
                        .then_inc(avm.h, avm.inc(1))

            @block.scalar
            def _(scalar, h=h):
                for qt in range(4):
                    for uh in range(8):
                        idx = qt * 8 + uh
                        scalar.wait_ge(mm2.h, mm2.t(idx + 1))
                        scalar.activation(
                            out=ab1[:, qt * 4096 + uh * 512:qt * 4096 + (uh + 1) * 512],
                            in_=p_mm[uh % 2][:, :], func=AF.Exp,
                            bias=V["s_mneg"][:, qt:qt + 1]).then_inc(exl.h, exl.inc(1))

            @block.vector
            def _(vector, h=h):
                for qt in range(4):
                    for uh in range(8):
                        c = 8 + qt * 8 + uh
                        vector.wait_ge(mm1.h, mm1.t(qt * 8 + uh + 1))
                        vector.reduce_max(out=V["s_mneg"][:, c:c + 1],
                                          in_=p_mm[uh % 2][:, :], axis=AX.X) \
                            .then_inc(rxp.h, rxp.inc(1))
                _dsep(vector, V)
                for qt in range(4):
                    vector.reduce_max(out=V["s_mneg"][:, qt:qt + 1],
                                      in_=V["s_mneg"][:, 8 + qt * 8:16 + qt * 8],
                                      axis=AX.X, negate=True) \
                        .then_inc(rxf.h, rxf.inc(1))
                for qt in range(4):
                    for half in range(2):
                        k = qt * 2 + half
                        vector.wait_ge(mkl[k % 2].h, mkl[k % 2].t((k // 2 + 1) * 16))
                        for u4 in range(4):
                            uh = half * 4 + u4
                            vector.wait_ge(exl.h, exl.t(qt * 8 + uh + 1))
                            ins = vector.tensor_mul(
                                out=ab1[:, qt * 4096 + uh * 512:qt * 4096 + (uh + 1) * 512],
                                in0=ab1[:, qt * 4096 + uh * 512:qt * 4096 + (uh + 1) * 512],
                                in1=V["s_mask"][k % 2][:, u4 * 512:(u4 + 1) * 512])
                        ins.then_inc(msk.h, msk.inc(1))
                    vector.reduce_sum(out=V["s_sig"][:, qt:qt + 1],
                                      in_=ab1[:, qt * 4096:(qt + 1) * 4096], axis=AX.X) \
                        .then_inc(sgm.h, sgm.inc(1))
                _dsep(vector, V)
                vector.reciprocal(out=V["s_rcp"][:, 0:4], in_=V["s_sig"][:, 0:4])
                _dsep(vector, V)
                for qt in range(4):
                    ins = vector.tensor_scalar(out=ab1[:, qt * 4096:(qt + 1) * 4096],
                                         in0=ab1[:, qt * 4096:(qt + 1) * 4096],
                                         scalar1=V["s_rcp"][:, qt:qt + 1], scalar2=None,
                                         op0=ALU.mult)
                ins.then_inc(sgm.h, sgm.inc(1))
                for ks in range(32):
                    vector.wait_ge(ptT.h, ptT.t(ks + 1))
                    if ks >= 1:
                        vector.wait_ge(avm.h, avm.t(ks))
                    vector.tensor_copy(out=V["s_pt"][:, :], in_=p_trb[:, 0:512]) \
                        .then_inc(pte.h, pte.inc(1))
                vector.wait_ge(avm.h, avm.t(32))
                vector.tensor_copy(out=ab2[:, h * 512:(h + 1) * 512], in_=p_mm[2][:, :])

            if V.get("dbg") and qh == 0 and h == 0:
                dmp = kb.new_sem()

                @block.gpsimd
                def _(gps):
                    gps.wait_ge(sgm.h, sgm.t(5))
                    for c4 in range(4):
                        gps.dma_start(out=V["dbg_P"][:, c4 * 4096:(c4 + 1) * 4096],
                                      in_=ab1[:, c4 * 4096:(c4 + 1) * 4096]) \
                            .then_inc(dmp.h, dmp.inc(16))
                    gps.dma_start(out=V["dbg_mn"][:, 0:40], in_=V["s_mneg"][:, :]) \
                        .then_inc(dmp.h, dmp.inc(16))
                    gps.dma_start(out=V["dbg_mn"][:, 40:44], in_=V["s_sig"][:, 0:4]) \
                        .then_inc(dmp.h, dmp.inc(16))
                    gps.dma_start(out=V["dbg_mn"][:, 44:48], in_=V["s_rcp"][:, 0:4]) \
                        .then_inc(dmp.h, dmp.inc(16))
                    gps.wait_ge(dmp.h, dmp.n)

            _fence(kb, block, V)

        lwo = [kb.new_sem() for _ in range(4)]
        WB4 = [V["s_wg"][0], V["s_wg"][1], V["s_wb"][0], V["s_wb"][1]]
        wti = kb.new_sem()
        x1l = [kb.new_sem(), kb.new_sem()]
        ewo = kb.new_sem()
        swo = kb.new_sem()

        @block.sync
        def _(sync, q0=q0):
            k = 0
            for dc in range(8):
                if dc >= 2:
                    sync.wait_ge(ewo.h, ewo.t((dc - 1) * 4))
                xbuf = V["s_xt"] if dc % 2 == 0 else V["s_p0"]
                for qt in range(4):
                    sync.dma_start(out=xbuf[:, qt * 512:(qt + 1) * 512],
                                   in_=V["x1own_dram"][q0 + qt * 128:q0 + (qt + 1) * 128,
                                                       dc * 512:(dc + 1) * 512]) \
                        .then_inc(x1l[dc % 2].h, x1l[dc % 2].inc(16))
                for ot in range(NID):
                    if k % 2 == 0:
                        if k >= 4:
                            sync.wait_ge(wti.h, wti.t(k - 3))
                        sync.dma_start(out=WB4[k % 4][:, :],
                                       in_=V["wo_cache"][ot * 128:(ot + 1) * 128,
                                                         dc * 512:(dc + 1) * 512]) \
                            .then_inc(lwo[k % 4].h, lwo[k % 4].inc(16))
                    k += 1

        @block.scalar
        def _(scalar, q0=q0):
            k = 0
            for dc in range(8):
                for ot in range(NID):
                    if k % 2 == 1:
                        if k >= 4:
                            scalar.wait_ge(wti.h, wti.t(k - 3))
                        scalar.dma_start(out=WB4[k % 4][:, :],
                                         in_=V["wo_cache"][ot * 128:(ot + 1) * 128,
                                                           dc * 512:(dc + 1) * 512]) \
                            .then_inc(lwo[k % 4].h, lwo[k % 4].inc(16))
                    k += 1

        @block.tensor
        def _(tensor):
            k = 0
            for dc in range(8):
                if dc >= 1:
                    tensor.wait_ge(ewo.h, ewo.t(dc * 4))
                for ot in range(NID):
                    tensor.wait_ge(lwo[k % 4].h, lwo[k % 4].t((k // 4 + 1) * 16))
                    for qt in range(4):
                        ins = tensor.matmul(out=p_mm[qt][:, :],
                                      lhsT=ab2[:, ot * 512 + qt * 128:ot * 512 + (qt + 1) * 128],
                                      rhs=WB4[k % 4][:, :],
                                      start=(ot == 0), stop=(ot == NID - 1))
                    ins.then_inc(wti.h, wti.inc(1))
                    k += 1

        @block.vector
        def _(vector):
            for dc in range(8):
                vector.wait_ge(wti.h, wti.t((dc + 1) * 32))
                vector.wait_ge(x1l[dc % 2].h, x1l[dc % 2].t((dc // 2 + 1) * 64))
                xbuf = V["s_xt"] if dc % 2 == 0 else V["s_p0"]
                for qt in range(4):
                    idx = dc * 4 + qt
                    if idx >= 2:
                        vector.wait_ge(swo.h, swo.t((idx - 1) * 16))
                    vector.scalar_tensor_tensor(out=V["s_ev"][qt % 2][:, :],
                                                in0=p_mm[qt][:, :], scalar=1.0,
                                                in1=xbuf[:, qt * 512:(qt + 1) * 512],
                                                op0=ALU.mult, op1=ALU.add) \
                        .then_inc(ewo.h, ewo.inc(1))

        @block.gpsimd
        def _(gps, q0=q0):
            for dc in range(8):
                for qt in range(4):
                    gps.wait_ge(ewo.h, ewo.t(dc * 4 + qt + 1))
                    gps.dma_start(out=V["x2_dram"][q0 + qt * 128:q0 + (qt + 1) * 128,
                                                   dc * 512:(dc + 1) * 512],
                                  in_=V["s_ev"][qt % 2][:, :]).then_inc(swo.h, swo.inc(16))
            gps.wait_ge(swo.h, swo.n)

        _fence(kb, block, V)


def _ln3(kb, block, V):
    for qc2 in range(2):
        q0 = qc2 * CHK
        fm = _ln_to_fm(kb, block, V,
                       lambda tt, half, q0=q0: V["x2_dram"][q0 + tt * 128:q0 + (tt + 1) * 128,
                                                            half * 2048:(half + 1) * 2048],
                       NTT)
        st3 = kb.new_sem()

        @block.gpsimd
        def _(gps, qc2=qc2):
            gmem = _getreg(gps, V, "g_member")
            gps.load(gmem, V["s_meta"][0:1, 0:1])
            gps.wait_ge(fm.h, fm.t(4 * NTT))
            for g8 in range(8):
                r0 = g8 * 4 * 128
                src = V["ab2"][:, g8 * 2048:(g8 + 1) * 2048] \
                    .rearrange("p (i t) -> p i t", t=CHK)
                with gps.If_eq(gmem, 0):
                    gps.dma_start(
                        out=V["ln3_sh"][qc2 * NID * 128 + r0:qc2 * NID * 128 + r0 + 512, :]
                        .rearrange("(i p) t -> p i t", p=128),
                        in_=src).then_inc(st3.h, st3.inc(16))
                with gps.Else():
                    gps.dma_start(
                        out=V["ln3_sh"][(2 + qc2) * NID * 128 + r0:(2 + qc2) * NID * 128 + r0 + 512, :]
                        .rearrange("(i p) t -> p i t", p=128),
                        in_=src).then_inc(st3.h, 16)
            gps.wait_ge(st3.h, st3.n)

        _fence(kb, block, V)


def _finish(kb, block, V):
    fl = kb.new_sem()
    fv = kb.new_sem()
    fs = kb.new_sem()

    @block.sync
    def _(sync):
        smem = _getreg(sync, V, "s_member")
        sync.load(smem, V["s_meta"][0:1, 0:1])
        for k in range(16):
            tt, half = k // 2, k % 2
            if k > 0:
                sync.wait_ge(fv.h, fv.t(k))
            hs = slice(half * 2048, (half + 1) * 2048)
            sync.dma_start(out=V["s_xt"][:, :],
                           in_=V["x2_dram"][tt * 128:(tt + 1) * 128, hs]) \
                .then_inc(fl.h, fl.inc(16))
            with sync.If_eq(smem, 0):
                r = tt * 128
                sync.dma_start(out=V["s_p0"][:, :],
                               in_=V["pc_sh"][0][r:r + 128, hs]).then_inc(fl.h, fl.inc(16))
                sync.dma_start(out=V["s_p1"][:, :],
                               in_=V["pc_sh"][1][r:r + 128, hs]).then_inc(fl.h, fl.inc(16))
            with sync.Else():
                r = 1024 + tt * 128
                sync.dma_start(out=V["s_p0"][:, :],
                               in_=V["pc_sh"][0][r:r + 128, hs]).then_inc(fl.h, 16)
                sync.dma_start(out=V["s_p1"][:, :],
                               in_=V["pc_sh"][1][r:r + 128, hs]).then_inc(fl.h, 16)

    @block.vector
    def _(vector):
        for k in range(16):
            vector.wait_ge(fl.h, fl.t((k + 1) * 48))
            if k >= 1:
                vector.wait_ge(fs.h, fs.t(k * 16))
            vector.tensor_add(out=V["s_p0"][:, :], in0=V["s_p0"][:, :], in1=V["s_p1"][:, :])
            vector.scalar_tensor_tensor(out=V["s_ln"][:, :], in0=V["s_p0"][:, :],
                                        scalar=0.5, in1=V["s_xt"][:, :],
                                        op0=ALU.mult, op1=ALU.add) \
                .then_inc(fv.h, fv.inc(1))

    @block.gpsimd
    def _(gps):
        for k in range(16):
            tt, half = k // 2, k % 2
            gps.wait_ge(fv.h, fv.t(k + 1))
            gps.dma_start(out=V["out_x3"][tt * 128:(tt + 1) * 128,
                                          half * 2048:(half + 1) * 2048],
                          in_=V["s_ln"][:, :]).then_inc(fs.h, fs.inc(16))
        gps.wait_ge(fs.h, fs.n)


def _dbg_out(kb, block, V, upto):
    ze = kb.new_sem()
    st = kb.new_sem()

    @block.vector
    def _(vector):
        vector.memset(V["s_ln"][:, :], 0.0).then_inc(ze.h, ze.inc(1))

    @block.gpsimd
    def _(gps):
        gps.wait_ge(ze.h, ze.t(1))
        if V.get("dbg"):
            if upto >= 1:
                gps.dma_start(out=V["dbg_w"][:, :],
                              in_=V["w_cache"]["g"][0:256, :]).then_inc(st.h, st.inc(16))
            if upto >= 2:
                for q in range(4):
                    gps.dma_start(out=V["dbg_x1"][q * 1024:(q + 1) * 1024, :],
                                  in_=V["x1_sh"][q * 1024:(q + 1) * 1024, :]) \
                        .then_inc(st.h, st.inc(16))
                    for m in range(2):
                        gps.dma_start(out=V["dbg_pa"][m][q * 1024:(q + 1) * 1024, :],
                                      in_=V["pa_sh"][m][q * 1024:(q + 1) * 1024, :]) \
                            .then_inc(st.h, st.inc(16))
            if upto >= 3:
                gps.dma_start(out=V["dbg_k"][:, :], in_=V["k_sh"][:, :]).then_inc(st.h, st.inc(16))
                gps.dma_start(out=V["dbg_v"][:, :], in_=V["v_sh"][:, :]).then_inc(st.h, st.inc(16))
                gps.dma_start(out=V["dbg_q"][:, :], in_=V["q_fm"][:, :]) \
                    .then_inc(st.h, st.inc(16))
            if upto >= 4:
                gps.dma_start(out=V["dbg_x2"][:, :], in_=V["x2_dram"][:, :]).then_inc(st.h, st.inc(16))
        for k in range(16):
            tt, half = k // 2, k % 2
            gps.dma_start(out=V["out_x3"][tt * 128:(tt + 1) * 128,
                                          half * 2048:(half + 1) * 2048],
                          in_=V["s_ln"][:, :]).then_inc(st.h, st.inc(16))
        gps.wait_ge(st.h, st.n)



def _build_all(kb, block, V, stages=9, dbg=False):
    p_ma = V["p_ma"]

    m0 = kb.new_sem()

    @block.sync
    def _(sync):
        sync.dma_start(out=V["s_meta"][:], in_=V["meta_in"]).then_inc(m0.h, m0.inc(16))
        sync.dma_start(out=V["s_ident"][:, :], in_=V["identb_in"]).then_inc(m0.h, m0.inc(16))
        sync.dma_start(out=V["s_identf"][:, :], in_=V["identf_in"]).then_inc(m0.h, m0.inc(16))

    @block.vector
    def _(vector):
        vector.memset(V["s_onesf"][:], 1.0)
        vector.memset(V["s_lnb"][:, :], 0.0)
        vector.wait_ge(m0.h, m0.t(48))

    _fence(kb, block, V)

    lca = kb.new_sem()
    lcv = kb.new_sem()
    la_specs = [("g", NID, 0), ("u", NID, NID * R), ("d", NDH, 2 * NID * R)]

    @block.sync
    def _(sync):
        for k, (mat, nit, dst) in enumerate(la_specs):
            if k >= 1:
                sync.wait_ge(lcv.h, lcv.t(k))
            for i in range(nit):
                sync.dma_start(out=V["s_xt"][:, i * R:(i + 1) * R],
                               in_=V["la_in"][mat][i * 128:(i + 1) * 128, :]) \
                    .then_inc(lca.h, lca.inc(16))

    @block.vector
    def _(vector):
        tot = 0
        for k, (mat, nit, dst) in enumerate(la_specs):
            tot += nit
            vector.wait_ge(lca.h, lca.t(tot * 16))
            vector.tensor_copy(out=V["s_la"][:, dst:dst + nit * R],
                               in_=V["s_xt"][:, 0:nit * R]).then_inc(lcv.h, lcv.inc(1))

    lb_specs = []
    for mat, width, lbo in (("g", DH, 0), ("u", DH, DH), ("d", D, 2 * DH)):
        for c0 in range(0, width, 2048):
            lb_specs.append((mat, c0, lbo, min(2048, width - c0)))
    lbL = kb.new_sem()
    lbC = kb.new_sem()
    lbS = kb.new_sem()

    @block.sync
    def _(sync):
        sync.wait_ge(lcv.h, lcv.t(3))
        for k, (mat, c0, lbo, w) in enumerate(lb_specs):
            if k >= 1:
                sync.wait_ge(lbC.h, lbC.t(k))
            sync.dma_start(out=V["s_xt"][0:16, 0:w],
                           in_=V["lb_in"][mat][0:16, c0:c0 + w]) \
                .then_inc(lbL.h, lbL.inc(16))

    @block.vector
    def _(vector):
        for k, (mat, c0, lbo, w) in enumerate(lb_specs):
            vector.wait_ge(lbL.h, lbL.t((k + 1) * 16))
            if k >= 1:
                vector.wait_ge(lbS.h, lbS.t(k * 16))
            vector.tensor_copy(out=V["s_lnb"][0:16, 0:w],
                               in_=V["s_xt"][0:16, 0:w]).then_inc(lbC.h, lbC.inc(1))

    @block.scalar
    def _(scalar):
        for k, (mat, c0, lbo, w) in enumerate(lb_specs):
            scalar.wait_ge(lbC.h, lbC.t(k + 1))
            scalar.dma_start(out=V["lb_cache"][0:16, lbo + c0:lbo + c0 + w],
                             in_=V["s_lnb"][0:16, 0:w]).then_inc(lbS.h, lbS.inc(16))
        scalar.wait_ge(lbS.h, lbS.n)

    _fence(kb, block, V)

    wc_specs = []
    for it in range(NID):
        wc_specs.append((V["wk_in"], it * 128, 0, V["wkv_cache"], it * 128, 0))
        wc_specs.append((V["wv_in"], it * 128, 0, V["wkv_cache"], it * 128, 512))
    for it in range(NID):
        for oc in range(8):
            wc_specs.append((V["wq_in"], it * 128, oc * 512, V["wq_cache"], it * 128, oc * 512))
            wc_specs.append((V["wo_in"], it * 128, oc * 512, V["wo_cache"], it * 128, oc * 512))
    wca = [kb.new_sem() for _ in range(4)]
    wcv = kb.new_sem()
    wcs = [kb.new_sem() for _ in range(4)]
    CIN4 = [V["s_xt"][:, 0:512], V["s_xt"][:, 512:1024],
            V["s_p0"][:, 0:512], V["s_p0"][:, 512:1024]]
    COUT4 = [V["s_wb"][0], V["s_wb"][1], V["s_wg"][0], V["s_wg"][1]]

    @block.sync
    def _(sync):
        for k, (src, r0, c0, dst, dr, dc) in enumerate(wc_specs):
            if k % 2 == 0:
                if k >= 4:
                    sync.wait_ge(wcv.h, wcv.t(k - 3))
                sync.dma_start(out=CIN4[k % 4],
                               in_=src[r0:r0 + 128, c0:c0 + 512]) \
                    .then_inc(wca[k % 4].h, wca[k % 4].inc(16))

    @block.scalar
    def _(scalar):
        for k, (src, r0, c0, dst, dr, dc) in enumerate(wc_specs):
            if k % 2 == 1:
                if k >= 4:
                    scalar.wait_ge(wcv.h, wcv.t(k - 3))
                scalar.dma_start(out=CIN4[k % 4],
                                 in_=src[r0:r0 + 128, c0:c0 + 512]) \
                    .then_inc(wca[k % 4].h, wca[k % 4].inc(16))

    @block.vector
    def _(vector):
        for k in range(len(wc_specs)):
            vector.wait_ge(wca[k % 4].h, wca[k % 4].t((k // 4 + 1) * 16))
            if k >= 4:
                vector.wait_ge(wcs[k % 4].h, wcs[k % 4].t((k // 4) * 16))
            vector.tensor_copy(out=COUT4[k % 4][:, :],
                               in_=CIN4[k % 4]).then_inc(wcv.h, wcv.inc(1))

    @block.gpsimd
    def _(gps):
        for k, (src, r0, c0, dst, dr, dc) in enumerate(wc_specs):
            gps.wait_ge(wcv.h, wcv.t(k + 1))
            gps.dma_start(out=dst[dr:dr + 128, dc:dc + 512],
                          in_=COUT4[k % 4][:, :]).then_inc(wcs[k % 4].h, wcs[k % 4].inc(16))
        for m in range(4):
            gps.wait_ge(wcs[m].h, wcs[m].n)

    _fence(kb, block, V)

    DQ0 = {"du": V["s_du"][:, :], "ds": V["s_ds"][:, :], "dz": V["s_dz"][:, :],
           "pe": [V["s_pe"][j][:, :] for j in range(4)],
           "po": [V["s_po"][j][:, :] for j in range(4)],
           "te": V["s_te"][:, :], "to": V["s_to"][:, :]}
    DQ1 = {"du": V["s_xt"][:, 0:512], "ds": V["s_xt"][:, 512:1024],
           "dz": V["s_xt"][:, 1024:1536],
           "pe": [V["s_xt"][:, 1536:2048], V["s_p0"][:, 0:512],
                  V["s_p0"][:, 512:1024], V["s_p0"][:, 1024:1536]],
           "po": [V["s_p0"][:, 1536:2048], V["s_p1"][:, 0:512],
                  V["s_p1"][:, 512:1024], V["s_p1"][:, 1024:1536]],
           "te": V["s_p1"][:, 1536:2048], "to": V["s_ln"][:, 0:512]}
    DQ = [DQ0, DQ1]
    deq = V["deq_tiles"]
    cdl = [kb.new_sem(), kb.new_sem()]
    arl = [kb.new_sem(), kb.new_sem()]
    mrow = kb.new_sem()
    dqa = kb.new_sem()
    dqv = kb.new_sem()
    wst = kb.new_sem()

    @block.sync
    def _(sync):
        for k, (mat, it, oc, w) in enumerate(deq):
            if k >= 2:
                sync.wait_ge(dqv.h, dqv.t(k - 1))
            sync.dma_start(out=V["s_codes"][k % 2][:, 0:w],
                           in_=V["codes_in"][mat][it * 128:(it + 1) * 128,
                                                  oc * 512:oc * 512 + w]) \
                .then_inc(cdl[k % 2].h, cdl[k % 2].inc(16))
            sync.dma_start(out=V["s_arow"][k % 2][0:1, 0:w],
                           in_=V["am_in"][mat][2 * it:2 * it + 1, oc * 512:oc * 512 + w]) \
                .then_inc(arl[k % 2].h, arl[k % 2].inc(16))
            sync.dma_start(out=V["s_brow"][k % 2][0:1, 0:w],
                           in_=V["am_in"][mat][2 * it + 1:2 * it + 2, oc * 512:oc * 512 + w]) \
                .then_inc(arl[k % 2].h, arl[k % 2].inc(16))

    @block.tensor
    def _(tensor):
        for k, (mat, it, oc, w) in enumerate(deq):
            tensor.wait_ge(arl[k % 2].h, arl[k % 2].t((k // 2 + 1) * 32))
            if k >= 2:
                tensor.wait_ge(dqv.h, dqv.t(k - 1))
            tensor.matmul(out=p_ma[k % 2][0:64, 0:w], lhsT=V["s_onesf"][0:1, 0:64],
                          rhs=V["s_arow"][k % 2][0:1, 0:w], start=True, stop=True)
            tensor.matmul(out=p_ma[k % 2][64:128, 0:w], lhsT=V["s_onesf"][0:1, 0:64],
                          rhs=V["s_brow"][k % 2][0:1, 0:w], start=True, stop=True,
                          tile_position=(0, 64)).then_inc(mrow.h, mrow.inc(1))

    @block.scalar
    def _(scalar):
        for k, (mat, it, oc, w) in enumerate(deq):
            q = DQ[k % 2]
            scalar.wait_ge(cdl[k % 2].h, cdl[k % 2].t((k // 2 + 1) * 16))
            if k >= 2:
                scalar.wait_ge(dqv.h, dqv.t(k - 1))
            cw = slice(0, w)
            scalar.activation(out=q["du"][:, cw], in_=V["s_codes"][k % 2][:, cw],
                              func=AF.Copy, scale=1.0 / 7.5, bias=-1.0)
            scalar.activation(out=q["ds"][:, cw], in_=q["du"][:, cw], func=AF.Square)
            scalar.activation(out=q["dz"][:, cw], in_=q["ds"][:, cw], func=AF.Square)
            for j in range(4):
                ins = scalar.activation(out=q["pe"][j][:, cw], in_=q["ds"][:, cw],
                                        func=AF.Copy, scale=CE[2 * j + 1], bias=CE[2 * j])
            ins.then_inc(dqa.h, dqa.inc(1))

    @block.vector
    def _(vector):
        for k, (mat, it, oc, w) in enumerate(deq):
            q = DQ[k % 2]
            vector.wait_ge(dqa.h, dqa.t(k + 1))
            vector.wait_ge(mrow.h, mrow.t(k + 1))
            if k >= 2:
                vector.wait_ge(wst.h, wst.t((k - 1) * 16))
            cw = slice(0, w)
            for j in range(4):
                vector.tensor_scalar(out=q["po"][j][:, cw], in0=q["ds"][:, cw],
                                     scalar1=CO[2 * j + 1], scalar2=CO[2 * j],
                                     op0=ALU.mult, op1=ALU.add)
            for tgt, parts in ((q["te"], q["pe"]), (q["to"], q["po"])):
                vector.scalar_tensor_tensor(out=tgt[:, cw], in0=parts[3][:, cw],
                                            scalar=0.0, in1=q["dz"][:, cw],
                                            op0=ALU.add, op1=ALU.mult)
                vector.tensor_add(out=tgt[:, cw], in0=tgt[:, cw], in1=parts[2][:, cw])
                vector.scalar_tensor_tensor(out=tgt[:, cw], in0=tgt[:, cw],
                                            scalar=0.0, in1=q["dz"][:, cw],
                                            op0=ALU.add, op1=ALU.mult)
                vector.tensor_add(out=tgt[:, cw], in0=tgt[:, cw], in1=parts[1][:, cw])
                vector.scalar_tensor_tensor(out=tgt[:, cw], in0=tgt[:, cw],
                                            scalar=0.0, in1=q["dz"][:, cw],
                                            op0=ALU.add, op1=ALU.mult)
                vector.tensor_add(out=tgt[:, cw], in0=tgt[:, cw], in1=parts[0][:, cw])
            vector.tensor_mul(out=q["to"][:, cw], in0=q["to"][:, cw],
                              in1=q["du"][:, cw])
            vector.tensor_add(out=q["te"][:, cw], in0=q["te"][:, cw],
                              in1=q["to"][:, cw])
            vector.tensor_mul(out=V["s_wout"][k % 2][:, cw], in0=q["te"][:, cw],
                              in1=p_ma[k % 2][:, cw]).then_inc(dqv.h, dqv.inc(1))

    @block.gpsimd
    def _(gps):
        for k, (mat, it, oc, w) in enumerate(deq):
            gps.wait_ge(dqv.h, dqv.t(k + 1))
            gps.dma_start(out=V["w_cache"][mat][it * 128:(it + 1) * 128,
                                                oc * 512:oc * 512 + w],
                          in_=V["s_wout"][k % 2][:, 0:w]).then_inc(wst.h, wst.inc(16))
        gps.wait_ge(wst.h, wst.n)

    _fence(kb, block, V)
    if stages <= 1:
        _dbg_out(kb, block, V, 1)
        return

    _ffn(kb, block, V, "A")
    _barrier(kb, block, V, 0)

    xa = kb.new_sem()
    xl = kb.new_sem()
    xs = kb.new_sem()

    @block.sync
    def _(sync):
        smem = _getreg(sync, V, "s_member")
        sync.load(smem, V["s_meta"][0:1, 0:1])
        for k in range(32):
            tb, tt, half = k // 16, (k // 2) % 8, k % 2
            if k > 0:
                sync.wait_ge(xa.h, xa.t(k))
            hs = slice(half * 2048, (half + 1) * 2048)
            with sync.If_eq(smem, 0):
                r = tb * 2048 + tt * 128
                sync.dma_start(out=V["s_xt"][:, :], in_=V["x_in"][r:r + 128, hs]) \
                    .then_inc(xl.h, xl.inc(16))
                sync.dma_start(out=V["s_p0"][:, :], in_=V["pa_sh"][0][r:r + 128, hs]) \
                    .then_inc(xl.h, xl.inc(16))
                sync.dma_start(out=V["s_p1"][:, :], in_=V["pa_sh"][1][r:r + 128, hs]) \
                    .then_inc(xl.h, xl.inc(16))
            with sync.Else():
                r = 1024 + tb * 2048 + tt * 128
                sync.dma_start(out=V["s_xt"][:, :], in_=V["x_in"][r:r + 128, hs]) \
                    .then_inc(xl.h, 16)
                sync.dma_start(out=V["s_p0"][:, :], in_=V["pa_sh"][0][r:r + 128, hs]) \
                    .then_inc(xl.h, 16)
                sync.dma_start(out=V["s_p1"][:, :], in_=V["pa_sh"][1][r:r + 128, hs]) \
                    .then_inc(xl.h, 16)

    @block.vector
    def _(vector):
        for k in range(32):
            vector.wait_ge(xl.h, xl.t((k + 1) * 48))
            if k >= 1:
                vector.wait_ge(xs.h, xs.t(k * 32))
            vector.tensor_add(out=V["s_p0"][:, :], in0=V["s_p0"][:, :], in1=V["s_p1"][:, :])
            vector.scalar_tensor_tensor(out=V["s_ln"][:, :], in0=V["s_p0"][:, :],
                                        scalar=0.5, in1=V["s_xt"][:, :],
                                        op0=ALU.mult, op1=ALU.add) \
                .then_inc(xa.h, xa.inc(1))

    @block.gpsimd
    def _(gps):
        gmem = _getreg(gps, V, "g_member")
        gps.load(gmem, V["s_meta"][0:1, 0:1])
        for k in range(32):
            tb, tt, half = k // 16, (k // 2) % 8, k % 2
            hs = slice(half * 2048, (half + 1) * 2048)
            gps.wait_ge(xa.h, xa.t(k + 1))
            with gps.If_eq(gmem, 0):
                r = tb * 2048 + tt * 128
                gps.dma_start(out=V["x1_sh"][r:r + 128, hs], in_=V["s_ln"][:, :]) \
                    .then_inc(xs.h, xs.inc(16))
                if tb == 0:
                    gps.dma_start(out=V["x1own_dram"][tt * 128:(tt + 1) * 128, hs],
                                  in_=V["s_ln"][:, :]).then_inc(xs.h, xs.inc(16))
                else:
                    gps.dma_start(out=V["x1_sh"][r:r + 128, hs], in_=V["s_ln"][:, :]) \
                        .then_inc(xs.h, xs.inc(16))
            with gps.Else():
                r = 1024 + tb * 2048 + tt * 128
                gps.dma_start(out=V["x1_sh"][r:r + 128, hs], in_=V["s_ln"][:, :]) \
                    .then_inc(xs.h, 16)
                if tb == 0:
                    gps.dma_start(out=V["x1own_dram"][tt * 128:(tt + 1) * 128, hs],
                                  in_=V["s_ln"][:, :]).then_inc(xs.h, 16)
                else:
                    gps.dma_start(out=V["x1_sh"][r:r + 128, hs], in_=V["s_ln"][:, :]) \
                        .then_inc(xs.h, 16)
        gps.wait_ge(xs.h, xs.n)

    _fence(kb, block, V)
    _barrier(kb, block, V, 1)
    if stages <= 2:
        _dbg_out(kb, block, V, 2)
        return

    _qkv(kb, block, V)
    _barrier(kb, block, V, 2)
    if stages <= 3:
        _dbg_out(kb, block, V, 3)
        return
    _attn(kb, block, V)
    if stages <= 4:
        _dbg_out(kb, block, V, 4)
        return

    _ln3(kb, block, V)
    _barrier(kb, block, V, 3)
    _ffn(kb, block, V, "C")
    _barrier(kb, block, V, 4)
    _finish(kb, block, V)



_PROG = None
_PROG_KEY = None
LAST_RESULTS = None


def kernel(L0_attn_x, L0_attn_Wq, L0_attn_Wk, L0_attn_Wv, L0_attn_Wo,
           L0_ffn_g_codes, L0_ffn_g_absmax, L0_ffn_g_lora_a, L0_ffn_g_lora_b,
           L0_ffn_u_codes, L0_ffn_u_absmax, L0_ffn_u_lora_a, L0_ffn_u_lora_b,
           L0_ffn_d_codes, L0_ffn_d_absmax, L0_ffn_d_lora_a, L0_ffn_d_lora_b):
    global _PROG, _PROG_KEY, LAST_RESULTS
    import ml_dtypes
    bf16 = ml_dtypes.bfloat16
    rng = np.random.default_rng()
    nonce = int(rng.integers(1, 2**30))

    x = np.ascontiguousarray(np.asarray(L0_attn_x, dtype=np.float32))
    wqT = np.ascontiguousarray(np.asarray(L0_attn_Wq, np.float32).T)
    wkT = np.ascontiguousarray(np.asarray(L0_attn_Wk, np.float32).T)
    wvT = np.ascontiguousarray(np.asarray(L0_attn_Wv, np.float32).T)
    woT = np.ascontiguousarray(np.asarray(L0_attn_Wo, np.float32).T)
    gcT = np.ascontiguousarray(np.asarray(L0_ffn_g_codes, np.int32).T)
    ucT = np.ascontiguousarray(np.asarray(L0_ffn_u_codes, np.int32).T)
    dcT = np.ascontiguousarray(np.asarray(L0_ffn_d_codes, np.int32).T)
    gaT = np.ascontiguousarray(np.asarray(L0_ffn_g_absmax, np.float32).T)
    uaT = np.ascontiguousarray(np.asarray(L0_ffn_u_absmax, np.float32).T)
    daT = np.ascontiguousarray(np.asarray(L0_ffn_d_absmax, np.float32).T)
    gla = np.ascontiguousarray(np.asarray(L0_ffn_g_lora_a, np.float32))
    glb = np.ascontiguousarray(np.asarray(L0_ffn_g_lora_b, np.float32))
    ula = np.ascontiguousarray(np.asarray(L0_ffn_u_lora_a, np.float32))
    ulb = np.ascontiguousarray(np.asarray(L0_ffn_u_lora_b, np.float32))
    dla = np.ascontiguousarray(np.asarray(L0_ffn_d_lora_a, np.float32))
    dlb = np.ascontiguousarray(np.asarray(L0_ffn_d_lora_b, np.float32))

    cos, sin = _llama3_cos_sin()
    ident_b = np.eye(128, dtype=bf16)
    ident_f = np.eye(128, dtype=np.float32)

    stages = int(os.environ.get("ATH_STAGES", "9"))
    dbgf = os.environ.get("ATH_DBG", "0") == "1"
    if _PROG is None or _PROG_KEY != (stages, dbgf):
        _PROG = build_program(stages, dbgf)
        _PROG_KEY = (stages, dbgf)
    nc = _PROG

    in_maps = []
    for core in range(8):
        pair = core // 2
        member = core % 2
        batch = pair // 2
        pip = pair % 2
        rot = pip * 2048
        blk = 2 * pip + member

        perm = np.roll(np.arange(T), -rot)
        xb = np.ascontiguousarray(x[batch][perm])
        cosk4 = np.ascontiguousarray(np.tile(cos[perm], (1, 4)))
        sink4 = np.ascontiguousarray(np.tile(sin[perm], (1, 4)))
        qpos = np.arange(blk * QBLK, (blk + 1) * QBLK)
        cosq4 = np.ascontiguousarray(np.tile(cos[qpos] * ISQ, (1, 4)).astype(np.float32))
        sinq4 = np.ascontiguousarray(np.tile(sin[qpos] * ISQ, (1, 4)).astype(np.float32))
        mask01 = np.ascontiguousarray((qpos[:, None] >= perm[None, :]).astype(bf16))

        mh = slice(member * DH, (member + 1) * DH)
        kvs = slice(member * 512, (member + 1) * 512)

        in_maps.append({
            "x_b": xb,
            "g_codesT_h": np.ascontiguousarray(gcT[:, mh]),
            "u_codesT_h": np.ascontiguousarray(ucT[:, mh]),
            "d_codesT_h": np.ascontiguousarray(dcT[mh, :]),
            "g_absmaxT_h": np.ascontiguousarray(gaT[:, mh]),
            "u_absmaxT_h": np.ascontiguousarray(uaT[:, mh]),
            "d_absmaxT_h": np.ascontiguousarray(daT[member * 86:(member + 1) * 86, :]),
            "g_lora_a": gla, "u_lora_a": ula,
            "d_lora_a_h": np.ascontiguousarray(dla[mh, :]),
            "g_lora_b_h": np.ascontiguousarray(glb[:, mh]),
            "u_lora_b_h": np.ascontiguousarray(ulb[:, mh]),
            "d_lora_b": dlb,
            "wqT": wqT, "woT": woT,
            "wkT_h": np.ascontiguousarray(wkT[:, kvs]),
            "wvT_h": np.ascontiguousarray(wvT[:, kvs]),
            "cosk4": cosk4, "sink4": sink4, "cosq4": cosq4, "sinq4": sinq4,
            "mask01": mask01,
            "identb": ident_b, "identf": ident_f,
            "meta": np.array([[member, nonce, 0, 0, 0, 0, 0, 0]], np.int32),
        })

    res = run_bass_kernel_spmd(nc, in_maps, core_ids=list(range(8)))
    LAST_RESULTS = res
    out = np.empty((B, T, D), dtype=np.float32)
    for core in range(8):
        pair = core // 2
        member = core % 2
        batch = pair // 2
        pip = pair % 2
        blk = 2 * pip + member
        out[batch, blk * QBLK:(blk + 1) * QBLK, :] = res.results[core]["out_x3"]
    return out
